# version 13
# baseline (speedup 1.0000x reference)
import math
import itertools
import contextlib
from collections import defaultdict

import numpy as np
import concourse.bass as bass
import concourse.mybir as mybir
from concourse.bass_utils import run_bass_kernel_spmd

F32 = mybir.dt.float32
BF16 = mybir.dt.bfloat16
U8 = mybir.dt.uint8
AF = mybir.ActivationFunctionType
ALU = mybir.AluOpType
AX = mybir.AxisListType

MAGIC = float(np.float32(12582912.0))


ENGS = ("tensor", "vector", "scalar", "gpsimd", "sync")


class Op:
    __slots__ = ("eng", "emit", "deps", "inc", "count", "is_dma", "dma_buf",
                 "skey", "sem_group")

    def __init__(self, eng, emit, deps, is_dma):
        self.eng = eng
        self.emit = emit
        self.deps = deps
        self.is_dma = is_dma
        self.inc = is_dma
        self.count = None
        self.sem_group = None


class Buf:

    __slots__ = ("h", "writer", "readers", "name", "win", "after_bufs")

    def __init__(self, h, name, win=None):
        self.h = h
        self.name = name
        self.writer = None
        self.readers = []
        self.win = win
        self.after_bufs = None

    def __getitem__(self, sl):
        if self.win is None:
            return self.h[sl]
        if len(self.win) == 4:
            ro, rows, off, n = self.win
        else:
            rows, off, n = self.win
            ro = 0
        rs, cs = sl
        start = (cs.start or 0) + off
        stop = (cs.stop if cs.stop is not None else n) + off
        assert stop - off <= n
        r0 = 0 if rs == slice(None) else (rs.start or 0)
        r1 = rows if rs == slice(None) else rs.stop
        return self.h[ro + r0:ro + r1, start:stop]

    @property
    def ap_all(self):
        if self.win is None:
            return self.h[:, :]
        if len(self.win) == 4:
            ro, rows, off, n = self.win
            return self.h[ro:ro + rows, off:off + n]
        rows, off, n = self.win
        return self.h[0:rows, off:off + n]


class KB:

    def __init__(self, name="mk"):
        self.nc = bass.Bass("TRN2", target_bir_lowering=False)
        self.ctx = contextlib.ExitStack()
        self.ops = []
        self.bufs = []
        self._uid = 0

    def uid(self):
        self._uid += 1
        return self._uid

    def dram_in(self, name, shape, dtype=F32):
        if not hasattr(self, "input_names"):
            self.input_names = []
        self.input_names.append(name)
        return self.nc.dram_tensor(name, shape, dtype, kind="ExternalInput").ap()

    def dram_out(self, name, shape, dtype=F32):
        return self.nc.dram_tensor(name, shape, dtype, kind="ExternalOutput").ap()

    def dram_internal(self, name, shape, dtype=BF16):
        return self.nc.dram_tensor(name, shape, dtype).ap()

    def sbuf(self, name, shape, dtype=F32):
        h = self.ctx.enter_context(self.nc.sbuf_tensor(name, shape, dtype))
        b = Buf(h, name)
        self.bufs.append(b)
        return b

    def psum(self, name, shape, dtype=F32):
        per = {F32: 512, BF16: 1024}[dtype]
        assert shape[1] <= per
        h = self.ctx.enter_context(self.nc.psum_tensor(name, [128, per], dtype))
        b = Buf(h, name)
        self.bufs.append(b)
        return b

    def op(self, eng, emit, reads=(), writes=(), is_dma=False):
        deps = []
        for b in reads:
            if b.writer is not None:
                deps.append(b.writer)
        o = Op(eng, emit, deps, is_dma)
        for b in reads:
            b.readers.append(o)
        for b in writes:
            if b.after_bufs:
                for ob in b.after_bufs:
                    deps.extend(ob.readers)
                    if ob.writer is not None:
                        deps.append(ob.writer)
                b.after_bufs = None
            if b.writer is not None:
                deps.append(b.writer)
            deps.extend(b.readers)
            b.writer = o
            b.readers = []
        o.deps = [d for d in dict.fromkeys(deps) if d is not o]
        self.ops.append(o)
        return o

    def dma(self, queue, out, in_, reads=(), writes=(), sem_group=None, **kw):
        o = self.op(
            queue,
            lambda e: e.dma_start(out=out, in_=in_, **kw),
            reads=reads, writes=writes, is_dma=True)
        o.dma_buf = writes[0] if writes else None
        o.sem_group = sem_group
        return o

    def emit_all(self):
        def sem_key(o):
            if not o.is_dma:
                return o.eng
            if getattr(o, "sem_group", None) is not None:
                return ("buf", o.sem_group)
            if getattr(o, "dma_buf", None) is not None:
                return ("buf", id(o.dma_buf))
            return ("store", o.eng)

        for o in self.ops:
            for d in o.deps:
                if d.eng == "tensor" and o.eng == "tensor":
                    continue
                d.inc = True
        cnt = defaultdict(int)
        store_totals = defaultdict(int)
        for o in self.ops:
            if o.inc:
                k = sem_key(o)
                cnt[k] += 16 if o.is_dma else 1
                o.count = cnt[k]
                o.skey = k
                if not isinstance(k, str) and k[0] == "store":
                    store_totals[k] = cnt[k]
        sems = {}
        for k in cnt:
            nm = k if isinstance(k, str) else f"{k[0]}_{k[1]}"
            sems[k] = self.ctx.enter_context(self.nc.semaphore(f"s_{nm}"))
        running_store = defaultdict(int)
        waitsfor = []
        ptr = {}
        for o in self.ops:
            if o.inc and not isinstance(o.skey, str) and o.skey[0] == "store":
                running_store[o.skey] = o.count
            wmap = {}
            for d in o.deps:
                if d.eng == "tensor" and o.eng == "tensor":
                    continue
                k = d.skey if hasattr(d, "skey") and d.inc else None
                if k is None:
                    continue
                c = d.count
                if not isinstance(k, str) and k[0] == "store":
                    c = running_store[k]
                if wmap.get(k, 0) < c:
                    wmap[k] = c
            waitsfor.append(wmap)
        waited = defaultdict(int)
        plans = {e: [] for e in ENGS}
        for o, wmap in zip(self.ops, waitsfor):
            w2 = {}
            for k, c in wmap.items():
                if waited[(o.eng, k)] < c:
                    waited[(o.eng, k)] = c
                    w2[k] = c
            plans[o.eng].append((w2, o))
        block = self.ctx.enter_context(self.nc.Block())

        def run_section(engine, eng_name):
            for wmap, o in plans[eng_name]:
                for k, c in wmap.items():
                    engine.wait_ge(sems[k], c)
                ins = o.emit(engine)
                if o.inc:
                    ins.then_inc(sems[o.skey], 16 if o.is_dma else 1)

        for e in ENGS:
            sec = getattr(block, e)

            def make(e=e):
                def _(engine):
                    run_section(engine, e)
                return _
            sec(make())
        self.ctx.close()
        self.n_sems = len(sems)

    def finalize(self):
        last_dma = {}
        for o in self.ops:
            if o.is_dma:
                last_dma[o.eng] = o
        for e, o in last_dma.items():
            w = Op(e, lambda engine: engine.nop(), [o], False)
            self.ops.append(w)
        self.emit_all()



def ceil_div(a, b):
    return (a + b - 1) // b


T = 2048
D = 1024
DC = D // 128
NTT = T // 512
LN_EPS = 1e-5


def host_constants():
    import ml_dtypes
    bf = ml_dtypes.bfloat16
    lnsel = np.zeros((128, 16 * NTT), dtype=bf)
    for i in range(NTT):
        lnsel[:, 16 * i + i] = 1.0
    bcsel = np.zeros((16, 128 * NTT), dtype=np.float32)
    for i in range(NTT):
        bcsel[i, 128 * i:128 * (i + 1)] = 1.0
    return {"c_lnsel": lnsel, "c_bcsel": bcsel}


class Model:

    def __init__(self, kb: KB):
        self.kb = kb
        nc = kb.nc
        self.X = kb.sbuf("X", [128, DC * T], F32)
        self.Xc = [Buf(self.X.h, f"Xc{c}", win=(128, c * T, T)) for c in range(DC)]
        kb.bufs.extend(self.Xc)
        self.xn = kb.sbuf("xn", [128, DC * T], BF16)
        self.xnc = [Buf(self.xn.h, f"xnc{c}", win=(128, c * T, T)) for c in range(DC)]
        kb.bufs.extend(self.xnc)
        self.psd = []
        for j in range(4):
            h = kb.ctx.enter_context(kb.nc.psum_tensor(f"psd{j}", [128, 1024], F32))
            self.psd.append(h)
        self.ps = []
        for i in range(8):
            b = Buf(self.psd[i // 2], f"ps{i}", win=(128, 512 * (i % 2), 512))
            kb.bufs.append(b)
            self.ps.append(b)
        self.xsq = [kb.sbuf("xsq0", [128, 512], BF16)]
        self.xbb = [kb.sbuf("xbb0", [128, 512], BF16)]
        strow = kb.sbuf("strow", [16, 2 * 512], F32)
        self.strow = [Buf(strow.h, f"strow{i}", win=(16, 512 * (i % 2), 512))
                      for i in range(2)]
        self._tscr = []
        self._mscr = None
        self.ccol = kb.sbuf("ccol", [128, 8], F32)
        self._ccol_vals = {}
        self.lnsel = kb.sbuf("lnsel", [128, 16 * NTT], BF16)
        self.bcsel = kb.sbuf("bcsel", [16, 128 * NTT], F32)

    def arena_win(self, name, off, n, after=()):
        if getattr(self, "_arena", None) is None:
            self._arena = self.kb.sbuf("arena", [128, 40960], BF16)
            self._arena_owners = []
        assert off + n <= 40960, (name, off, n)
        b = Buf(self._arena.h, name, win=(128, off, n))
        aft = list(after)
        for (o2, n2, ob) in self._arena_owners:
            if o2 < off + n and off < o2 + n2:
                aft.append(ob)
        b.after_bufs = aft
        self._arena_owners.append((off, n, b))
        self.kb.bufs.append(b)
        return b

    def arena_win_f32(self, name, off_bf16, n_f32):
        if getattr(self, "_arena", None) is None:
            self.arena_win("_dummy", 0, 1)
        if getattr(self, "_arena_f32h", None) is None:
            self._arena_f32h = self._arena.h.bitcast(F32)
        assert off_bf16 % 2 == 0 and off_bf16 + 2 * n_f32 <= 40960
        b = Buf(self._arena_f32h, name, win=(128, off_bf16 // 2, n_f32))
        aft = []
        for (o2, n2, ob) in self._arena_owners:
            if o2 < off_bf16 + 2 * n_f32 and off_bf16 < o2 + n2:
                aft.append(ob)
        b.after_bufs = aft
        self._arena_owners.append((off_bf16, 2 * n_f32, b))
        self.kb.bufs.append(b)
        return b

    def arena_win_u8(self, name, off_bf16, n_u8):
        if getattr(self, "_arena", None) is None:
            self.arena_win("_dummy", 0, 1)
        if getattr(self, "_arena_u8h", None) is None:
            self._arena_u8h = self._arena.h.bitcast(U8)
        assert n_u8 % 2 == 0 and off_bf16 * 2 + n_u8 <= 81920
        b = Buf(self._arena_u8h, name, win=(128, off_bf16 * 2, n_u8))
        aft = []
        for (o2, n2, ob) in self._arena_owners:
            if o2 < off_bf16 + n_u8 // 2 and off_bf16 < o2 + n2:
                aft.append(ob)
        b.after_bufs = aft
        self._arena_owners.append((off_bf16, n_u8 // 2, b))
        self.kb.bufs.append(b)
        return b

    def buf(self, name, shape, dtype=F32):
        if not hasattr(self, "_bufcache"):
            self._bufcache = {}
        if name not in self._bufcache:
            self._bufcache[name] = self.kb.sbuf(name, shape, dtype)
        return self._bufcache[name]

    def scr(self, i):
        while len(self._tscr) <= i:
            self._tscr.append(self.kb.sbuf(f"tscr{len(self._tscr)}", [128, 512], F32))
        return self._tscr[i]

    @property
    def mscr(self):
        if self._mscr is None:
            self._mscr = self.kb.sbuf("mscr", [128, 512], U8)
        return self._mscr

    def const_col(self, val):
        kb = self.kb
        if val not in self._ccol_vals:
            i = len(self._ccol_vals)
            assert i < 8
            self._ccol_vals[val] = i
            kb.op("vector", lambda e, i=i, val=val: e.memset(
                self.ccol[:, i:i + 1], float(val)),
                writes=[self.ccol])
        i = self._ccol_vals[val]
        return self.ccol[:, i:i + 1]

    def load_consts(self, kb, cmap):
        kb.dma("sync", self.lnsel.ap_all, cmap["c_lnsel"], writes=[self.lnsel])
        kb.dma("sync", self.bcsel.ap_all, cmap["c_bcsel"], writes=[self.bcsel])

    def layernorm(self, dst=None, src=None, eps=LN_EPS, gain_bias=None):
        kb = self.kb
        src = src if src is not None else self.Xc
        dst = dst or self.xn
        if dst is self.xn:
            dsts = self.xnc
            dstap = lambda c, a, b: self.xnc[c][:, a:b]
        else:
            dsts = [dst] * DC
            dstap = lambda c, a, b: dst[:, c * T + a:c * T + b]
        if isinstance(src, list):
            srcs = src
            srcap = lambda c, a, b: srcs[c][:, a:b]
        else:
            srcs = [src] * DC
            srcap = lambda c, a, b: src[:, c * T + a:c * T + b]
        ps_sx, ps_sq = self.ps[6], self.ps[7]
        mrow = self.strow[0]
        qrow = self.strow[1]
        trow = self.strow[1]
        srow = self.strow[1]
        arow = self.strow[1]
        brow = self.strow[0]
        for c in range(DC):
            for i in range(NTT):
                xs = self.xsq[0]
                xb = self.xbb[0]
                off = 512 * i
                kb.op("scalar",
                      lambda e, off=off, xs=xs, c=c: e.activation(
                          xs[:, :], srcap(c, off, off + 512), AF.Square),
                      reads=[srcs[c]], writes=[xs])
                kb.op("vector",
                      lambda e, off=off, xb=xb, c=c: e.tensor_copy(
                          xb[:, :], srcap(c, off, off + 512)),
                      reads=[srcs[c]], writes=[xb])
                st = (c == 0 and i == 0)
                fin = (c == DC - 1 and i == NTT - 1)
                kb.op("tensor",
                      lambda e, i=i, xs=xs, st=st, fin=fin: e.matmul(
                          ps_sq[0:16, :], self.lnsel[:, 16 * i:16 * (i + 1)],
                          xs[:, :], start=st, stop=fin),
                      reads=[xs, self.lnsel], writes=[ps_sq])
                kb.op("tensor",
                      lambda e, i=i, xb=xb, st=st, fin=fin: e.matmul(
                          ps_sx[0:16, :], self.lnsel[:, 16 * i:16 * (i + 1)],
                          xb[:, :], start=st, stop=fin),
                      reads=[xb, self.lnsel], writes=[ps_sx])
        kb.op("vector", lambda e: e.tensor_scalar(
            mrow[:, :], ps_sx[0:16, :], 1.0 / D, None, ALU.mult),
            reads=[ps_sx], writes=[mrow])
        kb.op("vector", lambda e: e.tensor_tensor(
            qrow[:, :], mrow[:, :], mrow[:, :], ALU.mult),
            reads=[mrow], writes=[qrow])
        kb.op("vector", lambda e: e.scalar_tensor_tensor(
            qrow[:, :], ps_sq[0:16, :], 1.0 / D, qrow[:, :],
            ALU.mult, ALU.subtract),
            reads=[ps_sq, qrow], writes=[qrow])
        epsc = self.const_col(float(eps))
        kb.op("scalar", lambda e, epsc=epsc: e.activation(
            srow[:, :], qrow[:, :], AF.Sqrt, bias=epsc[:16, :]),
            reads=[qrow, self.ccol], writes=[srow])
        kb.op("vector", lambda e: e.reciprocal(arow[:, :], srow[:, :]),
              reads=[srow], writes=[arow])
        kb.op("vector", lambda e: e.scalar_tensor_tensor(
            brow[:, :], mrow[:, :], -1.0, arow[:, :], ALU.mult, ALU.mult),
            reads=[mrow, arow], writes=[brow])
        psA, psB = self.ps[4], self.ps[5]
        for i in range(NTT):
            kb.op("tensor", lambda e, i=i: e.matmul(
                psA[:, :], self.bcsel[:, 128 * i:128 * (i + 1)], arow[:, :],
                start=True, stop=True),
                reads=[arow, self.bcsel], writes=[psA])
            kb.op("tensor", lambda e, i=i: e.matmul(
                psB[:, :], self.bcsel[:, 128 * i:128 * (i + 1)], brow[:, :],
                start=True, stop=True),
                reads=[brow, self.bcsel], writes=[psB])
            for c in range(DC):
                tsc = self.scr(c % 2)
                sl = slice(c * T + 512 * i, c * T + 512 * (i + 1))
                kb.op("vector", lambda e, i=i, tsc=tsc, c=c: e.tensor_tensor(
                    tsc[:, 0:512], srcap(c, 512 * i, 512 * (i + 1)), psA[:, :],
                    ALU.mult),
                    reads=[srcs[c], psA], writes=[tsc])
                if gain_bias is None:
                    kb.op("vector", lambda e, i=i, tsc=tsc, c=c: e.tensor_tensor(
                        dstap(c, 512 * i, 512 * (i + 1)), tsc[:, 0:512],
                        psB[:, :], ALU.add),
                        reads=[tsc, psB], writes=[dsts[c]])
                else:
                    gcol, bcol = gain_bias
                    kb.op("vector", lambda e, sl=sl, tsc=tsc: e.tensor_tensor(
                        tsc[:, 0:512], tsc[:, 0:512], psB[:, :], ALU.add),
                        reads=[tsc, psB], writes=[tsc])
                    kb.op("vector", lambda e, i=i, tsc=tsc, c=c: e.tensor_scalar(
                        dstap(c, 512 * i, 512 * (i + 1)), tsc[:, 0:512], gcol[:, c:c + 1],
                        bcol[:, c:c + 1], ALU.mult, ALU.add),
                        reads=[tsc, gcol, bcol], writes=[dsts[c]])

    def ffn1(self, wmap, b1c, b2c):
        kb = self.kb
        HF = 4096
        NHC = HF // 128
        wdep = wmap.get("deps", {})
        hbuf = self.arena_win("f1_hbuf", 0, NHC * 512)
        w1sl = [self.arena_win(f"f1_w1sl{i}", 16384 + 4096 * i, DC * 512)
                for i in range(2)]
        w2ch = [self.arena_win(f"f1_w2ch{i}", 24576 + 2048 * i, 2048)
                for i in range(2)]
        scrsets = [[self.scr(i) for i in range(5)]]
        for k in range(2):
            scrsets.append([self.arena_win_f32(f"f1_scrB{k}_{i}",
                                               28672 + 5120 * k + 1024 * i, 512)
                            for i in range(5)])
        w1r = wmap["w1"].rearrange("(c p) o -> p c o", p=128)
        w2r = wmap["w2"]
        for ti in range(NTT):
            toff = 512 * ti
            for sl_i in range(DC):
                wb = w1sl[sl_i % 2]
                kb.dma("sync", wb.ap_all, w1r[:, :, 512 * sl_i:512 * (sl_i + 1)],
                       reads=wdep.get("w1", ()), writes=[wb])
                for hh in range(4):
                    hc = 4 * sl_i + hh
                    ps = self.ps[hc % 4]
                    for kc in range(DC):
                        kb.op("tensor", lambda e, wb=wb, hh=hh, kc=kc, ps=ps, toff=toff: e.matmul(
                            ps[:, :],
                            wb[:, 512 * kc + 128 * hh: 512 * kc + 128 * hh + 128],
                            self.xnc[kc][:, toff:toff + 512],
                            start=(kc == 0), stop=(kc == DC - 1)),
                            reads=[wb, self.xnc[kc]], writes=[ps])
                    self._pwl(ps, b1c, hc, hbuf, 512 * hc, 512,
                              scrset=scrsets[hc % 3])
            for kcp in range(16):
                wc = w2ch[kcp % 2]
                kb.dma("sync", wc.ap_all,
                       w2r[256 * kcp:256 * (kcp + 1), :]
                       .rearrange("(c p) o -> p c o", p=128),
                       reads=wdep.get("w2", ()), writes=[wc])
                for kk in range(2):
                    kc = 2 * kcp + kk
                    for dc in range(DC):
                        ps = self.ps[dc]
                        kb.op("tensor", lambda e, wc=wc, kc=kc, kk=kk, dc=dc, ps=ps: e.matmul(
                            ps[:, :],
                            wc[:, 1024 * kk + 128 * dc: 1024 * kk + 128 * dc + 128],
                            hbuf[:, kc * 512: (kc + 1) * 512],
                            start=(kc == 0), stop=(kc == NHC - 1)),
                            reads=[wc, hbuf], writes=[ps])
            for dc in range(DC):
                ps = self.ps[dc]
                sl = slice(dc * T + toff, dc * T + toff + 512)
                t0 = self.scr(dc % 4)
                Xc = self.Xc[dc]
                kb.op("vector", lambda e, dc=dc, ps=ps, t0=t0: e.tensor_scalar(
                    t0[:, 0:512], ps[:, :], b2c[:, dc:dc + 1], 256.0,
                    ALU.add, ALU.mult),
                    reads=[ps, b2c], writes=[t0])
                kb.op("vector", lambda e, t0=t0: e.tensor_scalar(
                    t0[:, 0:512], t0[:, 0:512], MAGIC, MAGIC,
                    ALU.add, ALU.subtract),
                    reads=[t0], writes=[t0])
                kb.op("vector", lambda e, t0=t0, toff=toff, Xc=Xc: e.scalar_tensor_tensor(
                    Xc[:, toff:toff + 512], t0[:, 0:512], 1.0 / 512.0,
                    Xc[:, toff:toff + 512], ALU.mult, ALU.add),
                    reads=[t0, Xc], writes=[Xc])
        return

    def _pwl(self, ps, b1c, hc, hbuf, hoff, W, scrset=None):
        kb = self.kb
        GELU4_256 = 256.0 * (2.0 + 2.0 * math.erf(2.0 * math.sqrt(2.0)))
        if scrset is None:
            scrset = [self.scr(i) for i in range(5)]
        nt, ncl, jj, tt_, g0 = scrset
        g1 = jj
        dd = ncl
        yy = tt_
        kb.op("vector", lambda e: e.tensor_scalar(
            nt[:, 0:W], ps[:, 0:W], b1c[:, hc:hc + 1], 256.0, ALU.add, ALU.mult),
            reads=[ps, b1c], writes=[nt])
        kb.op("vector", lambda e: e.tensor_scalar(
            nt[:, 0:W], nt[:, 0:W], MAGIC, MAGIC, ALU.add, ALU.subtract),
            reads=[nt], writes=[nt])
        kb.op("vector", lambda e: e.tensor_scalar(
            ncl[:, 0:W], nt[:, 0:W], -1024.0, 1024.0, ALU.max, ALU.min),
            reads=[nt], writes=[ncl])
        MAGIC64 = float(np.float32(64.0 * 12582912.0))
        kb.op("vector", lambda e: e.tensor_scalar(
            g0[:, 0:W], ncl[:, 0:W], 31.5, None, ALU.subtract),
            reads=[ncl], writes=[g0])
        kb.op("vector", lambda e, MAGIC64=MAGIC64: e.tensor_scalar(
            jj[:, 0:W], g0[:, 0:W], MAGIC64, MAGIC64, ALU.add, ALU.subtract),
            reads=[g0], writes=[jj])
        kb.op("gpsimd", lambda e: e.tensor_tensor(
            tt_[:, 0:W], ncl[:, 0:W], jj[:, 0:W], ALU.subtract),
            reads=[ncl, jj], writes=[tt_])
        kb.op("scalar", lambda e: e.activation(
            g0[:, 0:W], jj[:, 0:W], AF.Gelu, bias=0.0, scale=1.0 / 256.0),
            reads=[jj], writes=[g0])
        qc = self.const_col(0.25)
        kb.op("scalar", lambda e, qc=qc: e.activation(
            g1[:, 0:W], jj[:, 0:W], AF.Gelu, bias=qc, scale=1.0 / 256.0),
            reads=[jj, self.ccol], writes=[g1])
        kb.op("gpsimd", lambda e: e.tensor_tensor(
            dd[:, 0:W], g1[:, 0:W], g0[:, 0:W], ALU.subtract),
            reads=[g1, g0], writes=[dd])
        kb.op("gpsimd", lambda e: e.tensor_tensor(
            dd[:, 0:W], tt_[:, 0:W], dd[:, 0:W], ALU.mult),
            reads=[tt_, dd], writes=[dd])
        kb.op("vector", lambda e: e.scalar_tensor_tensor(
            yy[:, 0:W], dd[:, 0:W], 1.0 / 64.0, g0[:, 0:W], ALU.mult, ALU.add),
            reads=[dd, g0], writes=[yy])
        kb.op("vector", lambda e: e.tensor_scalar(
            nt[:, 0:W], nt[:, 0:W], GELU4_256, 0.0, ALU.subtract, ALU.max),
            reads=[nt], writes=[nt])
        kb.op("vector", lambda e: e.scalar_tensor_tensor(
            yy[:, 0:W], nt[:, 0:W], 1.0 / 256.0, yy[:, 0:W], ALU.mult, ALU.add),
            reads=[nt, yy], writes=[yy])
        kb.op("vector", lambda e: e.tensor_scalar(
            yy[:, 0:W], yy[:, 0:W], 256.0, MAGIC, ALU.mult, ALU.add),
            reads=[yy], writes=[yy])
        kb.op("vector", lambda e: e.tensor_scalar(
            hbuf[:, hoff:hoff + W], yy[:, 0:W], MAGIC, 1.0 / 256.0,
            ALU.subtract, ALU.mult),
            reads=[yy], writes=[hbuf])

    def moe(self, wmap, b1c, b3c, cols):
        kb = self.kb
        HE = 1408
        NHC = 11
        gwt, gbt, b2s, id128, esel = (cols[k] for k in
                                      ("gwt", "gbt", "b2s", "id128", "esel"))
        gscr = self.buf("moe_gscr", [128, 48], F32)
        rw9f = self.buf("moe_rw9f", [128, 16], F32)
        rwT9b = self.buf("moe_rwT9", [9, T], BF16)
        rwT9 = rwT9b
        w1b = self.arena_win("moe_w1b", 0, DC * HE)
        w3b = self.arena_win("moe_w3b", DC * HE, DC * HE)
        w2b = self.arena_win("moe_w2b", 2 * DC * HE, NHC * 1024)
        hst = [self.arena_win("moe_hst", 33792, NHC * 512)]
        cbuf, ubuf = self.scr(0), self.scr(1)
        cbuf2, ubuf2 = self.scr(2), self.scr(3)
        ssb = self.arena_win("moe_ssb", 39424, 512)
        ssb2 = self.arena_win("moe_ssb2", 40448, 512)
        bcsb = self.arena_win("moe_bcsb", 39936, 512)
        psG, psT, psB = self.ps[0], self.ps[1], self.ps[7]

        kb.op("vector", lambda e: e.memset(rw9f[:, 8:9], 1.0), writes=[rw9f])
        for ti in range(16):
            for kc in range(DC):
                kb.op("tensor", lambda e, ti=ti, kc=kc: e.matmul(
                    psG[:, 0:8], self.xnc[kc][:, 128 * ti:128 * (ti + 1)],
                    gwt[:, 8 * kc:8 * (kc + 1)],
                    start=(kc == 0), stop=(kc == DC - 1)),
                    reads=[self.xnc[kc], gwt], writes=[psG])
            kb.op("vector", lambda e: e.reduce_max(
                gscr[:, 0:1], psG[:, 0:8], axis=AX.X, negate=True),
                reads=[psG], writes=[gscr])
            kb.op("scalar", lambda e: e.activation(
                gscr[:, 8:16], psG[:, 0:8], AF.Exp, bias=gscr[:, 0:1],
                accum_out=gscr[:, 1:2]),
                reads=[psG, gscr], writes=[gscr])
            kb.op("vector", lambda e: e.reciprocal(
                gscr[:, 2:3], gscr[:, 1:2]), reads=[gscr], writes=[gscr])
            kb.op("vector", lambda e: e.tensor_scalar(
                gscr[:, 16:24], gscr[:, 8:16], gscr[:, 2:3], None, ALU.mult),
                reads=[gscr], writes=[gscr])
            kb.op("vector", lambda e: e.tensor_tensor(
                gscr[:, 24:32], gscr[:, 16:24], gbt[:, 0:8], ALU.add),
                reads=[gscr, gbt], writes=[gscr])
            kb.op("vector", lambda e: e.tensor_scalar(
                gscr[:, 32:40], gscr[:, 24:32], gscr[:, 24:25], None, ALU.is_lt),
                reads=[gscr], writes=[gscr])
            for e2 in range(1, 8):
                kb.op("vector", lambda e, e2=e2: e.scalar_tensor_tensor(
                    gscr[:, 32:40], gscr[:, 24:32], gscr[:, 24 + e2:25 + e2],
                    gscr[:, 32:40], ALU.is_lt, ALU.add),
                    reads=[gscr], writes=[gscr])
            kb.op("vector", lambda e: e.scalar_tensor_tensor(
                rw9f[:, 0:8], gscr[:, 32:40], 1.5, gscr[:, 16:24],
                ALU.is_lt, ALU.mult),
                reads=[gscr], writes=[rw9f])
            kb.op("tensor", lambda e: e.transpose(
                psT[0:9, 0:128], rw9f[:, 0:9], id128[:, :]),
                reads=[rw9f, id128], writes=[psT])
            kb.op("vector", lambda e, ti=ti: e.tensor_copy(
                rwT9[0:9, 128 * ti:128 * (ti + 1)], psT[0:9, 0:128]),
                reads=[psT], writes=[rwT9])

        w1r = wmap["m_w1"]
        w3r = wmap["m_w3"]
        w2r = wmap["m_w2"]
        wdep = wmap.get("deps", {})
        for ex in range(9):
            kb.dma("sync", w1b.ap_all,
                   w1r[1024 * ex:1024 * (ex + 1), :].rearrange("(c p) h -> p c h", p=128),
                   reads=wdep.get(("m_w1", ex), ()), writes=[w1b])
            kb.dma("sync", w3b.ap_all,
                   w3r[1024 * ex:1024 * (ex + 1), :].rearrange("(c p) h -> p c h", p=128),
                   reads=wdep.get(("m_w3", ex), ()), writes=[w3b])
            kb.dma("sync", w2b.ap_all,
                   w2r[HE * ex:HE * (ex + 1), :].rearrange("(c p) d -> p c d", p=128),
                   reads=wdep.get(("m_w2", ex), ()), writes=[w2b])
            for qt in range(4):
                toff = 512 * qt
                hs = hst[0]
                kb.op("tensor", lambda e, ex=ex, toff=toff: e.matmul(
                    psB[:, :], esel[0:9, 128 * ex:128 * (ex + 1)],
                    rwT9[0:9, toff:toff + 512], start=True, stop=True),
                    reads=[esel, rwT9], writes=[psB])
                kb.op("vector", lambda e: e.tensor_copy(
                    bcsb[:, :], psB[:, :]), reads=[psB], writes=[bcsb])
                for hc in range(NHC):
                    pa = self.ps[hc % 2]
                    pc = self.ps[2 + hc % 2]
                    cb = [cbuf, cbuf2][hc % 2]
                    ub = [ubuf, ubuf2][hc % 2]
                    sbb = [ssb, ssb2][hc % 2]
                    for kc in range(DC):
                        kb.op("tensor", lambda e, hc=hc, kc=kc, toff=toff, pa=pa: e.matmul(
                            pa[:, :], w1b[:, HE * kc + 128 * hc: HE * kc + 128 * hc + 128],
                            self.xnc[kc][:, toff:toff + 512],
                            start=(kc == 0), stop=(kc == DC - 1)),
                            reads=[w1b, self.xnc[kc]], writes=[pa])
                        kb.op("tensor", lambda e, hc=hc, kc=kc, toff=toff, pc=pc: e.matmul(
                            pc[:, :], w3b[:, HE * kc + 128 * hc: HE * kc + 128 * hc + 128],
                            self.xnc[kc][:, toff:toff + 512],
                            start=(kc == 0), stop=(kc == DC - 1)),
                            reads=[w3b, self.xnc[kc]], writes=[pc])
                    bcol1 = 11 * ex + hc
                    kb.op("scalar", lambda e, bcol1=bcol1, pc=pc, cb=cb: e.activation(
                        cb[:, 0:512], pc[:, :], AF.Identity,
                        bias=b3c[:, bcol1:bcol1 + 1]),
                        reads=[pc, b3c], writes=[cb])
                    kb.op("vector", lambda e, bcol1=bcol1, pa=pa, cb=cb, ub=ub: e.scalar_tensor_tensor(
                        ub[:, 0:512], pa[:, :], b1c[:, bcol1:bcol1 + 1],
                        cb[:, 0:512], ALU.add, ALU.mult),
                        reads=[pa, b1c, cb], writes=[ub])
                    kb.op("scalar", lambda e, ub=ub, sbb=sbb: e.activation(
                        sbb[:, 0:512], ub[:, 0:512], AF.Silu),
                        reads=[ub], writes=[sbb])
                    kb.op("gpsimd", lambda e, hc=hc, sbb=sbb: e.tensor_tensor(
                        hs[:, 512 * hc:512 * (hc + 1)], sbb[:, 0:512], bcsb[:, :],
                        ALU.mult),
                        reads=[sbb, bcsb], writes=[hs])
                for dc in range(DC):
                    ps = self.ps[4 + dc % 3]
                    for kc in range(NHC):
                        kb.op("tensor", lambda e, kc=kc, dc=dc, ps=ps: e.matmul(
                            ps[:, :],
                            w2b[:, 1024 * kc + 128 * dc: 1024 * kc + 128 * dc + 128],
                            hs[:, 512 * kc:512 * (kc + 1)],
                            start=(kc == 0), stop=(kc == NHC - 1)),
                            reads=[w2b, hs], writes=[ps])
                    Xc = self.Xc[dc]
                    kb.op("vector", lambda e, ps=ps, toff=toff, Xc=Xc: e.scalar_tensor_tensor(
                        Xc[:, toff:toff + 512], ps[:, :], 0.5,
                        Xc[:, toff:toff + 512], ALU.mult, ALU.add),
                        reads=[ps, Xc], writes=[Xc])
        for qt in range(4):
            toff = 512 * qt
            for dc in range(DC):
                ps = self.ps[4 + dc % 3]
                kb.op("tensor", lambda e, dc=dc, toff=toff, ps=ps: e.matmul(
                    ps[:, :], b2s[0:9, 128 * dc:128 * (dc + 1)],
                    rwT9[0:9, toff:toff + 512], start=True, stop=True),
                    reads=[b2s, rwT9], writes=[ps])
                Xc = self.Xc[dc]
                kb.op("vector", lambda e, ps=ps, toff=toff, Xc=Xc: e.scalar_tensor_tensor(
                    Xc[:, toff:toff + 512], ps[:, :], 0.5,
                    Xc[:, toff:toff + 512], ALU.mult, ALU.add),
                    reads=[ps, Xc], writes=[Xc])
        return

    def attn1(self, wmap, cols):
        kb = self.kb
        bq, bk, bo, b1, b2, g1c, b1nc, g2c, b2nc, ones1 = (
            cols[k] for k in ("bq", "bk", "bo", "b1", "b2",
                              "g1", "b1n", "g2", "b2n", "ones1"))
        SC16 = 1.0 / 8.0
        wdep = wmap.get("deps", {})
        INV_SQRT_S = 1.0 / math.sqrt(T)
        wqkv = wmap["a_wqkv"]
        Vb = self.arena_win("a1_V", 0, 16384)
        wvb = self.arena_win("a1_wvb", 32768, DC * 256)
        Qb = self.arena_win("a1_Q", 32768, T)
        Kb = self.arena_win("a1_K", 32768 + T, T)
        bvrow = Buf(self._arena.h, "a1_bvrow", win=(1, 36864, 1024))
        bvrow.after_bufs = [o for (o2, n2, o) in self._arena_owners
                            if o2 < 36864 + 1024 and 36864 < o2 + n2]
        self._arena_owners.append((36864, 1024, bvrow))
        kb.bufs.append(bvrow)
        kb.dma("gpsimd", bvrow.ap_all, cols["bvrow_d"], writes=[bvrow])
        Pst0 = self.arena_win("a1_P0", 36864, T)
        vt = [kb.sbuf(f"a1_vt{i}", [128, 64], BF16) for i in range(2)]
        dscr = [kb.sbuf("a1_dscr0", [128, 16], F32),
                kb.sbuf("a1_dscr1", [128, 16], F32)]
        nrm = kb.sbuf("a1_nrm", [1, 16], F32)
        ebias = kb.sbuf("a1_eb", [128, 1], F32)
        attnb = self.arena_win("a1_attn", 16384, DC * T)
        sqq = self.xsq[0]
        wqkvr = wqkv.rearrange("(c p) o -> p c o", p=128)

        for vh in range(4):
            kb.dma("sync", wvb.ap_all,
                   wqkvr[:, :, 2048 + 256 * vh:2048 + 256 * (vh + 1)],
                   reads=wdep.get("a_wqkv", ()), writes=[wvb])
            for tc in range(16):
                ps = self.ps[tc % 2]
                for kc in range(DC):
                    kb.op("tensor", lambda e, tc=tc, kc=kc, ps=ps: e.matmul(
                        ps[:, 0:256], self.xnc[kc][:, 128 * tc:128 * (tc + 1)],
                        wvb[:, 256 * kc:256 * (kc + 1)],
                        start=(kc == 0), stop=False),
                        reads=[self.xnc[kc], wvb], writes=[ps])
                kb.op("tensor", lambda e, vh=vh, ps=ps: e.matmul(
                    ps[:, 0:256], ones1[0:1, 0:128], bvrow[0:1, 256 * vh:256 * (vh + 1)],
                    start=False, stop=True),
                    reads=[ones1, bvrow], writes=[ps])
                kb.op("vector", lambda e, tc=tc, vh=vh, ps=ps: e.tensor_copy(
                    Vb[:, 1024 * tc + 256 * vh: 1024 * tc + 256 * (vh + 1)], ps[:, 0:256]),
                    reads=[ps], writes=[Vb])

        for hp in range(DC):
            for which, dstb, bcol in ((0, Qb, bq), (1, Kb, bk)):
                wb = self.arena_win(f"a1_wqk_{hp}_{which}", 38912, DC * 128)
                kb.dma("sync", wb.ap_all,
                       wqkvr[:, :, 1024 * which + 128 * hp:1024 * which + 128 * (hp + 1)],
                       reads=wdep.get("a_wqkv", ()), writes=[wb],
                       sem_group="a1wqk")
                for tt in range(4):
                    ps = self.ps[2 + which]
                    for kc in range(DC):
                        kb.op("tensor", lambda e, kc=kc, tt=tt, ps=ps, wb=wb: e.matmul(
                            ps[:, :], wb[:, 128 * kc:128 * (kc + 1)],
                            self.xnc[kc][:, 512 * tt:512 * (tt + 1)],
                            start=(kc == 0), stop=(kc == DC - 1)),
                            reads=[wb, self.xnc[kc]], writes=[ps])
                    kb.op("scalar", lambda e, tt=tt, ps=ps, dstb=dstb, bcol=bcol, hp=hp: e.activation(
                        dstb[:, 512 * tt:512 * (tt + 1)], ps[:, :], AF.Identity,
                        bias=bcol[:, hp:hp + 1]),
                        reads=[ps, bcol], writes=[dstb])
            ps6 = self.ps[6]
            for which, srcb in ((0, Qb), (1, Kb)):
                for tt in range(4):
                    kb.op("scalar", lambda e, tt=tt, srcb=srcb: e.activation(
                        sqq[:, :], srcb[:, 512 * tt:512 * (tt + 1)], AF.Square),
                        reads=[srcb], writes=[sqq])
                    kb.op("tensor", lambda e: e.matmul(
                        ps6[0:1, 0:512], self.lnsel[:, 0:1], sqq[:, :],
                        start=True, stop=True),
                        reads=[self.lnsel, sqq], writes=[ps6])
                    kb.op("vector", lambda e, tt=tt, which=which: e.reduce_max(
                        nrm[0:1, 4 * which + tt:4 * which + tt + 1],
                        ps6[0:1, 0:512], axis=AX.X),
                        reads=[ps6], writes=[nrm])
            kb.op("vector", lambda e: e.reduce_max(
                nrm[0:1, 8:9], nrm[0:1, 0:4], axis=AX.X), reads=[nrm], writes=[nrm])
            kb.op("vector", lambda e: e.reduce_max(
                nrm[0:1, 9:10], nrm[0:1, 4:8], axis=AX.X), reads=[nrm], writes=[nrm])
            kb.op("vector", lambda e: e.tensor_tensor(
                nrm[0:1, 10:11], nrm[0:1, 8:9], nrm[0:1, 9:10], ALU.add),
                reads=[nrm], writes=[nrm])
            kb.op("vector", lambda e: e.tensor_scalar(
                nrm[0:1, 11:12], nrm[0:1, 10:11], -SC16 / 2.0, None, ALU.mult),
                reads=[nrm], writes=[nrm])
            kb.op("tensor", lambda e: e.matmul(
                ps6[:, 0:1], self.bcsel[0:1, 0:128], nrm[0:1, 11:12],
                start=True, stop=True),
                reads=[self.bcsel, nrm], writes=[ps6])
            kb.op("vector", lambda e: e.tensor_copy(
                ebias[:, 0:1], ps6[:, 0:1]), reads=[ps6], writes=[ebias])
            Pst1 = self.arena_win(f"a1_P1_{hp}", 38912, T)
            Pst = [Pst0, Pst1]
            psPV = [self.ps[2 + i] for i in range(4)]
            for sc in range(16):
                for h2 in range(2):
                    dsc = dscr[sc % 2]
                    pp = Pst[h2]
                    jd = 0 if h2 == 0 else 3
                    for p_ in range(2):
                        for tt in (2 * p_, 2 * p_ + 1):
                            pS = self.ps[[0, 1, 6, 7][2 * h2 + tt % 2]]
                            kb.op("tensor", lambda e, h2=h2, sc=sc, tt=tt, pS=pS: e.matmul(
                                pS[:, :],
                                Kb[64 * h2:64 * (h2 + 1), 128 * sc:128 * (sc + 1)],
                                Qb[64 * h2:64 * (h2 + 1), 512 * tt:512 * (tt + 1)],
                                start=True, stop=True, tile_position=(64 * h2, 0)),
                                reads=[Kb, Qb], writes=[pS])
                        s0 = self.ps[2 * h2 if h2 == 0 else 6]
                        s1 = self.ps[1 if h2 == 0 else 7]
                        kb.op("scalar", lambda e, h2=h2, p_=p_, pp=pp, dsc=dsc, jd=jd:
                            e.activation(
                                pp[:, 1024 * p_:1024 * (p_ + 1)],
                                self.psd[jd][:, 0:1024], AF.Exp,
                                bias=ebias[:, 0:1], scale=SC16,
                                accum_out=dsc[:, 4 * h2 + p_:4 * h2 + p_ + 1]),
                            reads=[s0, s1, ebias], writes=[pp, dsc])
                    kb.op("vector", lambda e, h2=h2, dsc=dsc: e.reduce_sum(
                        dsc[:, 8 + h2:9 + h2], dsc[:, 4 * h2:4 * h2 + 2], axis=AX.X),
                        reads=[dsc], writes=[dsc])
                    kb.op("vector", lambda e, h2=h2, dsc=dsc: e.reciprocal(
                        dsc[:, 10 + h2:11 + h2], dsc[:, 8 + h2:9 + h2]),
                        reads=[dsc], writes=[dsc])
                    kb.op("vector", lambda e, h2=h2, sc=sc, hp=hp, dsc=dsc: e.tensor_scalar(
                        vt[h2][:, :], Vb[:, 1024 * sc + 128 * hp + 64 * h2:
                                         1024 * sc + 128 * hp + 64 * (h2 + 1)],
                        dsc[:, 10 + h2:11 + h2], None, ALU.mult),
                        reads=[Vb, dsc], writes=[vt[h2]])
                    for tt in range(4):
                        kb.op("tensor", lambda e, h2=h2, sc=sc, tt=tt, pp=pp: e.matmul(
                            psPV[tt][64 * h2:64 * (h2 + 1), :], vt[h2][:, :],
                            pp[:, 512 * tt:512 * (tt + 1)],
                            start=(sc == 0 and h2 == 0), stop=(sc == 15 and h2 == 1),
                            tile_position=(0, 64 * h2), skip_group_check=True),
                            reads=[vt[h2], pp], writes=[psPV[tt]])
            for tt in range(4):
                g = self.scr(tt % 2)
                kb.op("scalar", lambda e, tt=tt, g=g: e.activation(
                    g[:, 0:512], psPV[tt][:, :], AF.Gelu),
                    reads=[psPV[tt]], writes=[g])
                kb.op("vector", lambda e, tt=tt, hp=hp, g=g: e.tensor_scalar(
                    attnb[:, hp * T + 512 * tt: hp * T + 512 * (tt + 1)],
                    g[:, 0:512], INV_SQRT_S, None, ALU.mult),
                    reads=[g], writes=[attnb])
        s1b = self.arena_win("a1_s1", 0, DC * T, after=[Vb])
        wsl = [self.arena_win("a1_wsl0", 32768, DC * 256),
               self.arena_win("a1_wsl1", 34816, DC * 256),
               self.arena_win("a1_wsl2", 36864, DC * 256)]
        w2ch = [self.arena_win(f"a1_w2ch{i}", 38912 + 1024 * i, 1024)
                for i in range(2)]
        wo = wmap["a_wo"]
        wor = wo.rearrange("(c p) o -> p c o", p=128)
        for oc in range(DC):
            wb = wsl[oc % 3]
            kb.dma("sync", wb[:, 0:DC * 128],
                   wor[:, :, 128 * oc:128 * (oc + 1)],
                   reads=wdep.get("a_wo", ()), writes=[wb])
            for tt in range(4):
                ps = self.ps[4 + (tt % 2)]
                for ac in range(DC):
                    kb.op("tensor", lambda e, oc=oc, tt=tt, ac=ac, ps=ps, wb=wb: e.matmul(
                        ps[:, :], wb[:, 128 * ac:128 * (ac + 1)],
                        attnb[:, ac * T + 512 * tt: ac * T + 512 * (tt + 1)],
                        start=(ac == 0), stop=(ac == DC - 1)),
                        reads=[wb, attnb], writes=[ps])
                sl = slice(oc * T + 512 * tt, oc * T + 512 * (tt + 1))
                kb.op("vector", lambda e, ps=ps, sl=sl, oc=oc, tt=tt: e.scalar_tensor_tensor(
                    s1b[:, sl], ps[:, :], bo[:, oc:oc + 1],
                    self.xnc[oc][:, 512 * tt:512 * (tt + 1)],
                    ALU.add, ALU.add),
                    reads=[ps, bo, self.xnc[oc]], writes=[s1b])
        self.layernorm(dst=self.xn, src=s1b, gain_bias=(g1c, b1nc))
        h1b = self.arena_win("a1_h1", 16384, 32 * 512, after=[attnb])
        w1 = wmap["a_w1"]
        w2 = wmap["a_w2"]
        w1r = w1.rearrange("(c p) o -> p c o", p=128)
        for ti in range(NTT):
            toff = 512 * ti
            for sl_i in range(16):
                wb = wsl[sl_i % 3]
                kb.dma("sync", wb.ap_all,
                       w1r[:, :, 256 * sl_i:256 * (sl_i + 1)],
                       reads=wdep.get("a_w1", ()), writes=[wb])
                for hh in range(2):
                    hc = 2 * sl_i + hh
                    ps = self.ps[hc % 4]
                    for kc in range(DC):
                        kb.op("tensor", lambda e, hh=hh, kc=kc, ps=ps, wb=wb, toff=toff: e.matmul(
                            ps[:, :],
                            wb[:, 256 * kc + 128 * hh: 256 * kc + 128 * hh + 128],
                            self.xnc[kc][:, toff:toff + 512],
                            start=(kc == 0), stop=(kc == DC - 1)),
                            reads=[wb, self.xnc[kc]], writes=[ps])
                    kb.op("scalar", lambda e, hc=hc, ps=ps: e.activation(
                        h1b[:, 512 * hc:512 * (hc + 1)], ps[:, :], AF.Gelu,
                        bias=b1[:, hc:hc + 1]),
                        reads=[ps, b1], writes=[h1b])
            for kc in range(32):
                wc = w2ch[kc % 2]
                kb.dma("sync", wc.ap_all,
                       w2[128 * kc:128 * (kc + 1), :],
                       reads=wdep.get("a_w2", ()), writes=[wc])
                for dc in range(DC):
                    ps = self.ps[dc]
                    kb.op("tensor", lambda e, wc=wc, kc=kc, dc=dc, ps=ps: e.matmul(
                        ps[:, :], wc[:, 128 * dc:128 * (dc + 1)],
                        h1b[:, 512 * kc:512 * (kc + 1)],
                        start=(kc == 0), stop=(kc == 31)),
                        reads=[wc, h1b], writes=[ps])
            for dc in range(DC):
                ps = self.ps[dc]
                sl = slice(dc * T + toff, dc * T + toff + 512)
                kb.op("vector", lambda e, ps=ps, sl=sl, dc=dc, toff=toff: e.scalar_tensor_tensor(
                    s1b[:, sl], ps[:, :], b2[:, dc:dc + 1],
                    self.xnc[dc][:, toff:toff + 512],
                    ALU.add, ALU.add),
                    reads=[ps, b2, self.xnc[dc]], writes=[s1b])
        ln2out = self.arena_win("a1_ln2o", 16384, DC * T, after=[h1b])
        self.layernorm(dst=ln2out, src=s1b, gain_bias=(g2c, b2nc))
        for c in range(DC):
            Xc = self.Xc[c]
            kb.op("vector", lambda e, c=c, Xc=Xc: e.tensor_tensor(
                Xc[:, 0:T], Xc[:, 0:T],
                ln2out[:, c * T:(c + 1) * T], ALU.add),
                reads=[Xc, ln2out], writes=[Xc])
        return


    def attn0(self, wmap, cols):
        kb = self.kb
        lq1, lk1, lq2, lk2, rmsw, ones1 = (
            cols[k] for k in ("lq1", "lk1", "lq2", "lk2", "rmsw", "ones1"))
        SCL = 1.0 / math.sqrt(32.0)
        wdep = wmap.get("deps", {})
        LAMBDA_INIT = 0.8 - 0.6 * math.exp(-0.3 * 12)
        wq, wkd, wv, wo = (wmap[k] for k in ("l0_wq", "l0_wkd", "l0_wv", "l0_wo"))
        Vb = self.arena_win("a0_V", 0, 16 * 512)
        attnb = self.arena_win("a0_attn", 8192, DC * T)
        Qc = self.arena_win("a0_Qc", 24576, T)
        Kc = self.arena_win("a0_Kc", 26624, T)
        raw = self.arena_win("a0_raw", 28672, 512)
        shf = self.arena_win("a0_shf", 29184, 512)
        v65 = self.arena_win("a0_v65", 29696, 16 * 65)
        wqb = self.arena_win("a0_wqb", 31776, DC * 128)
        wkb = self.arena_win("a0_wkb", 32800, DC * 128)
        wvb = self.arena_win("a0_wvb", 33824, DC * 256)
        Pbp = [self.arena_win("a0_Pb0", 30752, 1024),
               self.arena_win("a0_Pb1", 33824, 1024, after=[wvb]),
               self.arena_win("a0_Pb2", 34848, 1024, after=[wvb])]
        wob = [self.arena_win("a0_wob0", 31776, DC * 128)]
        ropec = self.arena_win("a0_ropec", 35872, T)
        ropes = self.arena_win("a0_ropes", 37920, T)
        cmask = self.arena_win("a0_cmask", 39968, 896)
        kb.dma("sync", ropec.ap_all, cols["ropec_d"], writes=[ropec])
        kb.dma("sync", ropes.ap_all, cols["ropes_d"], writes=[ropes])
        kb.dma("sync", cmask.ap_all, cols["cmask_d"], writes=[cmask])
        oc2 = [[self.scr(2), self.scr(3)], [self.scr(0), self.scr(1)]]
        _dbh = self.scr(4).h.bitcast(BF16)
        dbr2 = []
        for h2 in range(2):
            b = Buf(_dbh, f"a0_dbr{h2}", win=(0, 1, 512 * h2, 512))
            kb.bufs.append(b)
            dbr2.append([b, b])
        sqq2 = [self.xsq[0], self.xbb[0]]
        lsc = self.buf("a0_lsc", [1, 64], F32)
        lcol = self.buf("a0_lcol", [128, 1], F32)
        ebias = self.buf("a0_eb", [128, 1], F32)
        nrm = self.buf("a0_nrm", [1, 16], F32)
        sqq = sqq2[0]
        wqr = wq.rearrange("(c p) o -> p c o", p=128)
        wkr = wkd.rearrange("(c p) o -> p c o", p=128)
        wvr = wv.rearrange("(c p) o -> p c o", p=128)
        wor = wo.rearrange("(c p) o -> p c o", p=128)

        kb.op("vector", lambda e: e.tensor_tensor(
            lsc[0:1, 0:32], lq1[0:1, 0:32], lk1[0:1, 0:32], ALU.mult),
            reads=[lq1, lk1], writes=[lsc])
        kb.op("vector", lambda e: e.tensor_tensor(
            lsc[0:1, 32:64], lq2[0:1, 0:32], lk2[0:1, 0:32], ALU.mult),
            reads=[lq2, lk2], writes=[lsc])
        kb.op("vector", lambda e: e.reduce_sum(
            lsc[0:1, 0:1], lsc[0:1, 0:32], axis=AX.X), reads=[lsc], writes=[lsc])
        kb.op("vector", lambda e: e.reduce_sum(
            lsc[0:1, 1:2], lsc[0:1, 32:64], axis=AX.X), reads=[lsc], writes=[lsc])
        kb.op("scalar", lambda e: e.activation(
            lsc[0:1, 2:4], lsc[0:1, 0:2], AF.Exp), reads=[lsc], writes=[lsc])
        kb.op("vector", lambda e: e.tensor_tensor(
            lsc[0:1, 4:5], lsc[0:1, 2:3], lsc[0:1, 3:4], ALU.subtract),
            reads=[lsc], writes=[lsc])
        kb.op("vector", lambda e: e.tensor_scalar(
            lsc[0:1, 5:6], lsc[0:1, 4:5], LAMBDA_INIT, -1.0, ALU.add, ALU.mult),
            reads=[lsc], writes=[lsc])
        psL = self.ps[6]
        kb.op("tensor", lambda e: e.matmul(
            psL[:, 0:1], self.bcsel[0:1, 0:128], lsc[0:1, 5:6],
            start=True, stop=True), reads=[self.bcsel, lsc], writes=[psL])
        kb.op("vector", lambda e: e.tensor_copy(
            lcol[:, 0:1], psL[:, 0:1]), reads=[psL], writes=[lcol])

        for vh in range(2):
            kb.dma("sync", wvb.ap_all, wvr[:, :, 256 * vh:256 * (vh + 1)],
                   reads=wdep.get("l0_wv", ()), writes=[wvb])
            for tc in range(16):
                ps = self.ps[tc % 2]
                for kc in range(DC):
                    kb.op("tensor", lambda e, tc=tc, kc=kc, ps=ps: e.matmul(
                        ps[:, 0:256],
                        self.xnc[kc][:, 128 * tc:128 * (tc + 1)],
                        wvb[:, 256 * kc:256 * (kc + 1)],
                        start=(kc == 0), stop=(kc == DC - 1)),
                        reads=[self.xnc[kc], wvb], writes=[ps])
                kb.op("vector", lambda e, tc=tc, vh=vh, ps=ps: e.tensor_copy(
                    Vb[:, 512 * tc + 256 * vh:512 * tc + 256 * (vh + 1)],
                    ps[:, 0:256]),
                    reads=[ps], writes=[Vb])

        for hc in range(DC):
            for which, wsb, wsrc, dst in ((1, wkb, wkr, Kc), (0, wqb, wqr, Qc)):
                kb.dma("sync", wsb.ap_all,
                       wsrc[:, :, 128 * hc:128 * (hc + 1)],
                       reads=wdep.get("l0_wkd" if which else "l0_wq", ()),
                       writes=[wsb])
                for tt in range(4):
                    ps = self.ps[2 + which]
                    for kc in range(DC):
                        kb.op("tensor", lambda e, kc=kc, tt=tt, ps=ps, wsb=wsb: e.matmul(
                            ps[:, :], wsb[:, 128 * kc:128 * (kc + 1)],
                            self.xnc[kc][:, 512 * tt:512 * (tt + 1)],
                            start=(kc == 0), stop=(kc == DC - 1)),
                            reads=[wsb, self.xnc[kc]], writes=[ps])
                    tsl = slice(512 * tt, 512 * (tt + 1))
                    kb.op("scalar", lambda e, ps=ps: e.copy(
                        raw[:, 0:512], ps[:, :]), reads=[ps], writes=[raw])
                    mask = list(range(16, 32)) + list(range(0, 16))
                    kb.op("vector", lambda e, mask=mask: e.stream_shuffle(
                        shf[:, 0:512], raw[:, 0:512], mask), reads=[raw], writes=[shf])
                    kb.op("gpsimd", lambda e, tsl=tsl: e.tensor_tensor(
                        raw[:, 0:512], raw[:, 0:512], ropec[:, tsl], ALU.mult),
                        reads=[raw, ropec], writes=[raw])
                    kb.op("gpsimd", lambda e, tsl=tsl: e.tensor_tensor(
                        shf[:, 0:512], shf[:, 0:512], ropes[:, tsl], ALU.mult),
                        reads=[shf, ropes], writes=[shf])
                    kb.op("vector", lambda e, tsl=tsl, dst=dst: e.tensor_tensor(
                        dst[:, tsl], raw[:, 0:512], shf[:, 0:512], ALU.add),
                        reads=[raw, shf], writes=[dst])
            ps6 = self.ps[6]
            for which, srcb in ((0, Qc), (1, Kc)):
                for tt in range(4):
                    kb.op("scalar", lambda e, tt=tt, srcb=srcb: e.activation(
                        sqq[:, :], srcb[:, 512 * tt:512 * (tt + 1)], AF.Square),
                        reads=[srcb], writes=[sqq])
                    kb.op("tensor", lambda e: e.matmul(
                        ps6[0:1, 0:512], self.lnsel[:, 0:1], sqq[:, :],
                        start=True, stop=True),
                        reads=[self.lnsel, sqq], writes=[ps6])
                    kb.op("vector", lambda e, tt=tt, which=which: e.reduce_max(
                        nrm[0:1, 4 * which + tt:4 * which + tt + 1],
                        ps6[0:1, 0:512], axis=AX.X),
                        reads=[ps6], writes=[nrm])
            kb.op("vector", lambda e: e.reduce_max(
                nrm[0:1, 8:9], nrm[0:1, 0:4], axis=AX.X), reads=[nrm], writes=[nrm])
            kb.op("vector", lambda e: e.reduce_max(
                nrm[0:1, 9:10], nrm[0:1, 4:8], axis=AX.X), reads=[nrm], writes=[nrm])
            kb.op("vector", lambda e: e.tensor_tensor(
                nrm[0:1, 10:11], nrm[0:1, 8:9], nrm[0:1, 9:10], ALU.add),
                reads=[nrm], writes=[nrm])
            kb.op("vector", lambda e: e.tensor_scalar(
                nrm[0:1, 11:12], nrm[0:1, 10:11], -SCL / 2.0, None, ALU.mult),
                reads=[nrm], writes=[nrm])
            kb.op("tensor", lambda e: e.matmul(
                ps6[:, 0:1], self.bcsel[0:1, 0:128], nrm[0:1, 11:12],
                start=True, stop=True), reads=[self.bcsel, nrm], writes=[ps6])
            kb.op("vector", lambda e: e.tensor_copy(
                ebias[:, 0:1], ps6[:, 0:1]), reads=[ps6], writes=[ebias])
            for sc in range(16):
                kb.op("vector", lambda e, sc=sc, hc=hc: e.tensor_copy(
                    v65[:, 65 * sc:65 * sc + 64],
                    Vb[:, 512 * sc + 64 * hc: 512 * sc + 64 * (hc + 1)]),
                    reads=[Vb], writes=[v65])
                kb.op("vector", lambda e, sc=sc: e.memset(
                    v65[:, 65 * sc + 64:65 * (sc + 1)], 1.0), writes=[v65])
            psS = [self.ps[i] for i in range(4)]
            psPV = [self.ps[4 + i] for i in range(4)]
            for tt in range(4):
                nsc = 4 * (tt + 1)
                for sc in range(nsc):
                    for u in range(4):
                        pS = psS[u]
                        kb.op("tensor", lambda e, sc=sc, tt=tt, u=u, pS=pS: e.matmul(
                            pS[:, :],
                            Kc[32 * u:32 * (u + 1), 128 * sc:128 * (sc + 1)],
                            Qc[32 * u:32 * (u + 1), 512 * tt:512 * (tt + 1)],
                            start=True, stop=True, tile_position=(32 * u, 0)),
                            reads=[Kc, Qc], writes=[pS])
                    diag = 512 * tt - 128 * sc
                    for j in range(2):
                        pbp = Pbp[(2 * sc + j) % 3]
                        kb.op("scalar", lambda e, j=j, pbp=pbp: e.activation(
                            pbp[:, 0:1024], self.psd[j][:, 0:1024], AF.Exp,
                            bias=ebias[:, 0:1], scale=SCL),
                            reads=[psS[2 * j], psS[2 * j + 1], ebias], writes=[pbp])
                        for i, u in ((0, 2 * j), (1, 2 * j + 1)):
                            if diag < 127:
                                kb.op("gpsimd", lambda e, pbp=pbp, i=i, diag=diag:
                                    e.tensor_tensor(
                                        pbp[:, 512 * i:512 * (i + 1)],
                                        pbp[:, 512 * i:512 * (i + 1)],
                                        cmask[:, diag + 384:diag + 384 + 512],
                                        ALU.mult),
                                    reads=[pbp, cmask], writes=[pbp])
                            kb.op("tensor", lambda e, u=u, i=i, sc=sc, nsc=nsc, pbp=pbp:
                                e.matmul(
                                    psPV[u][0:65, :], v65[:, 65 * sc:65 * (sc + 1)],
                                    pbp[:, 512 * i:512 * (i + 1)],
                                    start=(sc == 0), stop=(sc == nsc - 1)),
                                reads=[v65, pbp], writes=[psPV[u]])
                chains = []
                for h2 in range(2):
                    u0, u1 = 2 * h2, 2 * h2 + 1
                    bc0, bc1 = psPV[u0], psPV[u1]
                    oc = oc2[h2]
                    sqh = sqq2[h2]
                    dbr = dbr2[h2]
                    ch = []
                    for i, u, bc in ((0, u0, bc0), (1, u1, bc1)):
                        def _rcp(e, i=i, u=u, dbr=dbr):
                            with e.bass.allow_low_precision(reason="1/d bf16 ok"):
                                return e.reciprocal(
                                    dbr[i][0:1, 0:512], psPV[u][64:65, :])
                        ch.append(("vector", _rcp, [psPV[u]], [dbr[i]]))
                        ch.append(("scalar", lambda e, u=u, oci=oc[i]: e.copy(
                            oci[0:64, 0:512], psPV[u][0:64, :]),
                            [psPV[u]], [oc[i]]))
                        ch.append(("tensor", lambda e, bc=bc, db=dbr[i]: e.matmul(
                            bc[:, :], ones1[0:1, 0:128], db[0:1, 0:512],
                            start=True, stop=True),
                            [ones1, dbr[i]], [bc]))
                    ch.append(("vector", lambda e, oc0=oc[0], bc0=bc0:
                        e.tensor_tensor(
                            oc0[0:64, 0:512], oc0[0:64, 0:512], bc0[0:64, :],
                            ALU.mult),
                        [oc[0], bc0], [oc[0]]))
                    ch.append(("vector", lambda e, oc1=oc[1], bc1=bc1:
                        e.tensor_tensor(
                            oc1[0:64, 0:512], oc1[0:64, 0:512], bc1[0:64, :],
                            ALU.mult),
                        [oc[1], bc1], [oc[1]]))
                    ch.append(("vector", lambda e, oc0=oc[0], oc1=oc[1]:
                        e.scalar_tensor_tensor(
                            oc0[0:64, 0:512], oc1[0:64, 0:512], lcol[0:64, 0:1],
                            oc0[0:64, 0:512], ALU.mult, ALU.add),
                        [oc[1], lcol, oc[0]], [oc[0]]))
                    ch.append(("scalar", lambda e, sqh=sqh, oc0=oc[0]: e.activation(
                        sqh[0:64, :], oc0[0:64, 0:512], AF.Square),
                        [oc[0]], [sqh]))
                    ch.append(("tensor", lambda e, bc0=bc0, sqh=sqh: e.matmul(
                        bc0[0:1, :], self.lnsel[0:64, 0:1], sqh[0:64, :],
                        start=True, stop=True),
                        [self.lnsel, sqh], [bc0]))
                    epsc = self.const_col(1e-6)
                    ch.append(("scalar", lambda e, bc0=bc0, epsc=epsc: e.activation(
                        bc0[0:1, :], bc0[0:1, :], AF.Ln,
                        bias=epsc[0:1, 0:1], scale=1.0 / 64.0),
                        [bc0, self.ccol], [bc0]))
                    ch.append(("scalar", lambda e, bc0=bc0, db=dbr[0]: e.activation(
                        db[0:1, 0:512], bc0[0:1, :], AF.Exp, scale=-0.5),
                        [bc0], [dbr[0]]))
                    ch.append(("tensor", lambda e, bc1=bc1, db=dbr[0]: e.matmul(
                        bc1[:, :], ones1[0:1, 0:128], db[0:1, 0:512],
                        start=True, stop=True),
                        [ones1, dbr[0]], [bc1]))
                    ch.append(("vector", lambda e, oc0=oc[0], bc1=bc1:
                        e.tensor_tensor(
                            oc0[0:64, 0:512], oc0[0:64, 0:512], bc1[0:64, :],
                            ALU.mult),
                        [oc[0], bc1], [oc[0]]))
                    osl = slice(hc * T + 512 * tt, hc * T + 512 * (tt + 1))
                    ch.append(("vector", lambda e, h2=h2, osl=osl, oc0=oc[0]:
                        e.tensor_scalar(
                            attnb[64 * h2:64 * (h2 + 1), osl], oc0[0:64, 0:512],
                            rmsw[0:64, 0:1], 1.0 - LAMBDA_INIT, ALU.mult, ALU.mult),
                        [oc[0], rmsw], [attnb]))
                    chains.append(ch)
                for pair in itertools.zip_longest(*chains):
                    for item in pair:
                        if item is not None:
                            eng, emit, rd, wr = item
                            kb.op(eng, emit, reads=rd, writes=wr)
        for oc2 in range(DC):
            wb = wob[0]
            kb.dma("sync", wb.ap_all,
                   wor[:, :, 128 * oc2:128 * (oc2 + 1)],
                   reads=wdep.get("l0_wo", ()), writes=[wb])
            for tt in range(4):
                ps = self.ps[4 + (tt % 2)]
                for ac in range(DC):
                    kb.op("tensor", lambda e, tt=tt, ac=ac, ps=ps, wb=wb: e.matmul(
                        ps[:, :], wb[:, 128 * ac:128 * (ac + 1)],
                        attnb[:, ac * T + 512 * tt: ac * T + 512 * (tt + 1)],
                        start=(ac == 0), stop=(ac == DC - 1)),
                        reads=[wb, attnb], writes=[ps])
                Xc = self.Xc[oc2]
                kb.op("vector", lambda e, ps=ps, tt=tt, Xc=Xc: e.tensor_tensor(
                    Xc[:, 512 * tt:512 * (tt + 1)], Xc[:, 512 * tt:512 * (tt + 1)],
                    ps[:, :], ALU.add),
                    reads=[ps, Xc], writes=[Xc])
        return



def build_ffn1_program():
    kb = KB()
    xT = kb.dram_in("xT", [D, T])
    w1 = kb.dram_in("f1_w1", [1024, 4096])
    w2 = kb.dram_in("f1_w2", [4096, 1024])
    b1cd = kb.dram_in("f1_b1c", [128, 32])
    b2cd = kb.dram_in("f1_b2c", [128, 8])
    clnsel = kb.dram_in("c_lnsel", [128, 16 * NTT], BF16)
    cbcsel = kb.dram_in("c_bcsel", [16, 128 * NTT])
    yT = kb.dram_out("yT", [D, T])

    m = Model(kb)
    b1c = kb.sbuf("b1c", [128, 32], F32)
    b2c = kb.sbuf("b2c", [128, 8], F32)
    _xTr = xT.rearrange("(c p) t -> p c t", p=128)
    for _c in range(DC):
        kb.dma("sync", m.Xc[_c][:, 0:T], _xTr[:, _c, :], writes=[m.Xc[_c]])
    kb.dma("sync", m.lnsel.ap_all, clnsel, writes=[m.lnsel])
    kb.dma("sync", m.bcsel.ap_all, cbcsel, writes=[m.bcsel])
    kb.dma("sync", b1c.ap_all, b1cd, writes=[b1c])
    kb.dma("sync", b2c.ap_all, b2cd, writes=[b2c])
    m.layernorm()
    m.ffn1({"w1": w1, "w2": w2}, b1c, b2c)
    yTr = yT.rearrange("(c p) t -> p c t", p=128)
    for c in range(DC):
        kb.dma("sync", yTr[:, c, :], m.Xc[c][:, 0:T], reads=[m.Xc[c]])
    kb.finalize()
    return kb


def prep_ffn1_inputs(w1, b1, w2, b2):
    return {
        "f1_w1": np.ascontiguousarray(w1, dtype=np.float32),
        "f1_w2": np.ascontiguousarray(w2, dtype=np.float32),
        "f1_b1c": np.ascontiguousarray(np.asarray(b1, np.float32).reshape(32, 128).T),
        "f1_b2c": np.ascontiguousarray(np.asarray(b2, np.float32).reshape(8, 128).T),
    }

    def _pwl(self, psm, b1c, hc, hbuf, hoff):
        kb = self.kb
        nt, ncl, jj, tt_ = self.scr(0), self.scr(1), self.scr(2), self.scr(3)
        g0, g1 = self.scr(4), self.scr(2)
        dd = self.scr(1)
        yy = self.scr(3)
        msk = self.mscr
        p0, p1 = psm
        kb.op("vector", lambda e: e.tensor_scalar(
            nt[:, 0:512], p0.h[:, :], b1c[:, hc:hc + 1], 256.0, ALU.add, ALU.mult),
            reads=[p0, b1c], writes=[nt])
        kb.op("vector", lambda e: e.tensor_scalar(
            nt[:, 512:1024], p1.h[:, :], b1c[:, hc:hc + 1], 256.0, ALU.add, ALU.mult),
            reads=[p1, b1c], writes=[nt])
        kb.op("vector", lambda e: e.tensor_scalar(
            nt[:, :], nt[:, :], MAGIC, MAGIC, ALU.add, ALU.subtract),
            reads=[nt], writes=[nt])
        kb.op("vector", lambda e: e.tensor_scalar(
            ncl[:, :], nt[:, :], -1024.0, 1024.0, ALU.max, ALU.min),
            reads=[nt], writes=[ncl])
        kb.op("vector", lambda e: e.tensor_scalar(
            ncl[:, :], ncl[:, :], 1.0 / 64.0, 63.0 / 128.0, ALU.mult, ALU.subtract),
            reads=[ncl], writes=[ncl])
        kb.op("vector", lambda e: e.tensor_scalar(
            jj[:, :], ncl[:, :], MAGIC, MAGIC, ALU.add, ALU.subtract),
            reads=[ncl], writes=[jj])
        kb.op("vector", lambda e: e.tensor_scalar(
            jj[:, :], jj[:, :], 15.0, None, ALU.min),
            reads=[jj], writes=[jj])
        kb.op("vector", lambda e: e.tensor_scalar(
            ncl[:, :], ncl[:, :], 63.0 / 128.0, None, ALU.add),
            reads=[ncl], writes=[ncl])
        kb.op("gpsimd", lambda e: e.tensor_tensor(
            tt_[:, :], ncl[:, :], jj[:, :], ALU.subtract),
            reads=[ncl, jj], writes=[tt_])
        kb.op("scalar", lambda e: e.activation(
            g0[:, :], jj[:, :], AF.Gelu, bias=0.0, scale=0.25),
            reads=[jj], writes=[g0])
        kb.op("scalar", lambda e: e.activation(
            g1[:, :], jj[:, :], AF.Gelu, bias=0.25, scale=0.25),
            reads=[jj], writes=[g1])
        kb.op("gpsimd", lambda e: e.tensor_tensor(
            dd[:, :], g1[:, :], g0[:, :], ALU.subtract),
            reads=[g1, g0], writes=[dd])
        kb.op("gpsimd", lambda e: e.tensor_tensor(
            dd[:, :], tt_[:, :], dd[:, :], ALU.mult),
            reads=[tt_, dd], writes=[dd])
        kb.op("gpsimd", lambda e: e.tensor_tensor(
            yy[:, :], dd[:, :], g0[:, :], ALU.add),
            reads=[dd, g0], writes=[yy])
        kb.op("vector", lambda e: e.tensor_scalar(
            yy[:, :], yy[:, :], 256.0, MAGIC, ALU.mult, ALU.add),
            reads=[yy], writes=[yy])
        kb.op("vector", lambda e: e.tensor_scalar(
            yy[:, :], yy[:, :], MAGIC, None, ALU.subtract),
            reads=[yy], writes=[yy])
        kb.op("vector", lambda e: e.tensor_scalar(
            msk[:, :], nt[:, :], 1024.0, None, ALU.is_gt),
            reads=[nt], writes=[msk])
        kb.op("vector", lambda e: e.select(
            yy[:, :], msk[:, :], nt[:, :], yy[:, :]),
            reads=[msk, nt, yy], writes=[yy])
        kb.op("vector", lambda e: e.tensor_scalar(
            hbuf[:, hoff:hoff + 1024], yy[:, :], 1.0 / 256.0, None, ALU.mult),
            reads=[yy], writes=[hbuf])


def build_moe_program():
    kb = KB()
    xT = kb.dram_in("xT", [D, T])
    w1 = kb.dram_in("m_w1", [9 * 1024, 1408])
    w3 = kb.dram_in("m_w3", [9 * 1024, 1408])
    w2 = kb.dram_in("m_w2", [9 * 1408, 1024])
    b1cd = kb.dram_in("m_b1c", [128, 99])
    b3cd = kb.dram_in("m_b3c", [128, 99])
    b2sd = kb.dram_in("m_b2s", [9, 1024])
    gwtd = kb.dram_in("g_gwt", [128, 64])
    gbtd = kb.dram_in("g_gbt", [128, 8])
    id128d = kb.dram_in("c_id128", [128, 128])
    eseld = kb.dram_in("c_esel", [9, 9 * 128], BF16)
    clnsel = kb.dram_in("c_lnsel", [128, 16 * NTT], BF16)
    cbcsel = kb.dram_in("c_bcsel", [16, 128 * NTT])
    yT = kb.dram_out("yT", [D, T])

    m = Model(kb)
    b1c = kb.sbuf("b1c", [128, 99], F32)
    b3c = kb.sbuf("b3c", [128, 99], F32)
    b2s = kb.sbuf("b2s", [9, 1024], BF16)
    gwt = kb.sbuf("gwt", [128, 64], BF16)
    gbt = kb.sbuf("gbt", [128, 8], F32)
    id128 = kb.sbuf("id128", [128, 128], F32)
    esel = kb.sbuf("esel", [9, 9 * 128], BF16)
    _xTr = xT.rearrange("(c p) t -> p c t", p=128)
    for _c in range(DC):
        kb.dma("sync", m.Xc[_c][:, 0:T], _xTr[:, _c, :], writes=[m.Xc[_c]])
    kb.dma("sync", m.lnsel.ap_all, clnsel, writes=[m.lnsel])
    kb.dma("sync", m.bcsel.ap_all, cbcsel, writes=[m.bcsel])
    kb.dma("sync", b1c.ap_all, b1cd, writes=[b1c])
    kb.dma("sync", b3c.ap_all, b3cd, writes=[b3c])
    kb.dma("gpsimd", b2s.ap_all, b2sd, writes=[b2s])
    kb.dma("gpsimd", gwt.ap_all, gwtd, writes=[gwt])
    kb.dma("sync", gbt.ap_all, gbtd, writes=[gbt])
    kb.dma("sync", id128.ap_all, id128d, writes=[id128])
    kb.dma("sync", esel.ap_all, eseld, writes=[esel])
    m.layernorm()
    m.moe({"m_w1": w1, "m_w3": w3, "m_w2": w2}, b1c, b3c,
          {"gwt": gwt, "gbt": gbt, "b2s": b2s, "id128": id128, "esel": esel})
    yTr = yT.rearrange("(c p) t -> p c t", p=128)
    for c in range(DC):
        kb.dma("sync", yTr[:, c, :], m.Xc[c][:, 0:T], reads=[m.Xc[c]])
    kb.finalize()
    return kb


def _esel_const():
    import ml_dtypes
    esel = np.zeros((9, 9 * 128), dtype=ml_dtypes.bfloat16)
    for e in range(9):
        esel[e, 128 * e:128 * (e + 1)] = 1.0
    return esel


def prep_moe_inputs(gate_w, gate_b, w1, b1, w3, b3, w2, b2, sw1, sb1, sw3, sb3, sw2, sb2):
    f = np.float32
    w1a = np.concatenate([np.asarray(w1, f).reshape(8 * 1024, 1408),
                          np.asarray(sw1, f)], axis=0)
    w3a = np.concatenate([np.asarray(w3, f).reshape(8 * 1024, 1408),
                          np.asarray(sw3, f)], axis=0)
    w2a = np.concatenate([np.asarray(w2, f).reshape(8 * 1408, 1024),
                          np.asarray(sw2, f)], axis=0)
    b1a = np.concatenate([np.asarray(b1, f), np.asarray(sb1, f)[None]], axis=0)
    b3a = np.concatenate([np.asarray(b3, f), np.asarray(sb3, f)[None]], axis=0)
    b2a = np.concatenate([np.asarray(b2, f), np.asarray(sb2, f)[None]], axis=0)
    b1c = np.concatenate([b1a[e].reshape(11, 128).T for e in range(9)], axis=1)
    b3c = np.concatenate([b3a[e].reshape(11, 128).T for e in range(9)], axis=1)
    gw = np.asarray(gate_w, f)
    gwt = np.zeros((128, 64), f)
    for kc in range(8):
        gwt[:, 8 * kc:8 * (kc + 1)] = gw[:, 128 * kc:128 * (kc + 1)].T
    gbt = np.broadcast_to(np.asarray(gate_b, f)[None, :], (128, 8)).copy()
    return {
        "m_w1": np.ascontiguousarray(w1a), "m_w3": np.ascontiguousarray(w3a),
        "m_w2": np.ascontiguousarray(w2a),
        "m_b1c": np.ascontiguousarray(b1c), "m_b3c": np.ascontiguousarray(b3c),
        "m_b2s": np.ascontiguousarray(b2a),
        "g_gwt": gwt, "g_gbt": gbt,
        "c_id128": np.eye(128, dtype=f),
        "c_esel": _esel_const(),
    }


def build_attn1_program():
    kb = KB()
    xT = kb.dram_in("xT", [D, T])
    wqkv = kb.dram_in("a_wqkv", [1024, 3072])
    wo = kb.dram_in("a_wo", [1024, 1024])
    w1 = kb.dram_in("a_w1", [1024, 4096])
    w2 = kb.dram_in("a_w2", [4096, 1024])
    colsd = {}
    for nm, sh in (("a_bq", [128, 8]), ("a_bk", [128, 8]), ("a_bo", [128, 8]),
                   ("a_b1", [128, 32]), ("a_b2", [128, 8]),
                   ("a_g1", [128, 8]), ("a_b1n", [128, 8]),
                   ("a_g2", [128, 8]), ("a_b2n", [128, 8])):
        colsd[nm] = kb.dram_in(nm, sh)
    bvrowd = kb.dram_in("a_bvrow", [1, 1024])
    ones1d = kb.dram_in("c_ones1", [1, 128])
    clnsel = kb.dram_in("c_lnsel", [128, 16 * NTT], BF16)
    cbcsel = kb.dram_in("c_bcsel", [16, 128 * NTT])
    yT = kb.dram_out("yT", [D, T])

    m = Model(kb)
    cols = {}
    for nm, key in (("a_bq", "bq"), ("a_bk", "bk"), ("a_bo", "bo"),
                    ("a_b1", "b1"), ("a_b2", "b2"), ("a_g1", "g1"),
                    ("a_b1n", "b1n"), ("a_g2", "g2"), ("a_b2n", "b2n")):
        b = kb.sbuf("sb_" + nm, list(colsd[nm].shape), F32)
        kb.dma("sync", b.ap_all, colsd[nm], writes=[b])
        cols[key] = b
    cols["bvrow_d"] = bvrowd
    ones1 = kb.sbuf("sb_ones1", [1, 128], BF16)
    kb.dma("gpsimd", ones1.ap_all, ones1d, writes=[ones1])
    cols["ones1"] = ones1
    _xTr = xT.rearrange("(c p) t -> p c t", p=128)
    for _c in range(DC):
        kb.dma("sync", m.Xc[_c][:, 0:T], _xTr[:, _c, :], writes=[m.Xc[_c]])
    kb.dma("sync", m.lnsel.ap_all, clnsel, writes=[m.lnsel])
    kb.dma("sync", m.bcsel.ap_all, cbcsel, writes=[m.bcsel])
    m.layernorm()
    m.attn1({"a_wqkv": wqkv, "a_wo": wo, "a_w1": w1, "a_w2": w2}, cols)
    yTr = yT.rearrange("(c p) t -> p c t", p=128)
    for c in range(DC):
        kb.dma("sync", yTr[:, c, :], m.Xc[c][:, 0:T], reads=[m.Xc[c]])
    kb.finalize()
    return kb


def prep_attn1_inputs(in_proj_w, in_proj_b, out_w, out_b, lin1_w, lin1_b,
                      lin2_w, lin2_b, ln1_g, ln1_b, ln2_g, ln2_b):
    f = np.float32
    import ml_dtypes
    col = lambda v, n: np.ascontiguousarray(np.asarray(v, f).reshape(n, 128).T)
    return {
        "a_wqkv": np.ascontiguousarray(np.asarray(in_proj_w, f).T),
        "a_wo": np.ascontiguousarray(np.asarray(out_w, f).T),
        "a_w1": np.ascontiguousarray(np.asarray(lin1_w, f).T),
        "a_w2": np.ascontiguousarray(np.asarray(lin2_w, f).T),
        "a_bq": col(np.asarray(in_proj_b, f)[0:1024], 8),
        "a_bk": col(np.asarray(in_proj_b, f)[1024:2048], 8),
        "a_bvrow": np.asarray(in_proj_b, f)[None, 2048:3072].copy(),
        "a_bo": col(out_b, 8),
        "a_b1": col(lin1_b, 32),
        "a_b2": col(lin2_b, 8),
        "a_g1": col(ln1_g, 8), "a_b1n": col(ln1_b_ := ln1_b, 8),
        "a_g2": col(ln2_g, 8), "a_b2n": col(ln2_b, 8),
        "c_ones1": np.ones((1, 128), f),
    }


def build_attn0_program():
    kb = KB()
    xT = kb.dram_in("xT", [D, T])
    wq = kb.dram_in("l0_wq", [1024, 1024])
    wkd = kb.dram_in("l0_wkd", [1024, 1024])
    wv = kb.dram_in("l0_wv", [1024, 512])
    wo = kb.dram_in("l0_wo", [1024, 1024])
    lqd = {nm: kb.dram_in(nm, [1, 32]) for nm in ("l0_lq1", "l0_lk1", "l0_lq2", "l0_lk2")}
    rmswd = kb.dram_in("l0_rmsw", [128, 1])
    ropecd = kb.dram_in("c_ropec", [128, T], BF16)
    ropesd = kb.dram_in("c_ropes", [128, T], BF16)
    cmaskd = kb.dram_in("c_cmask", [128, 896], BF16)
    ones1d = kb.dram_in("c_ones1", [1, 128])
    clnsel = kb.dram_in("c_lnsel", [128, 16 * NTT], BF16)
    cbcsel = kb.dram_in("c_bcsel", [16, 128 * NTT])
    yT = kb.dram_out("yT", [D, T])

    m = Model(kb)
    cols = {}
    for nm, key in (("l0_lq1", "lq1"), ("l0_lk1", "lk1"),
                    ("l0_lq2", "lq2"), ("l0_lk2", "lk2")):
        b = kb.sbuf("sb_" + nm, [1, 32], F32)
        kb.dma("sync", b.ap_all, lqd[nm], writes=[b])
        cols[key] = b
    rmsw = kb.sbuf("sb_rmsw", [128, 1], F32)
    kb.dma("sync", rmsw.ap_all, rmswd, writes=[rmsw])
    cols["rmsw"] = rmsw
    cols["ropec_d"] = ropecd
    cols["ropes_d"] = ropesd
    cols["cmask_d"] = cmaskd
    ones1 = kb.sbuf("sb_ones1", [1, 128], BF16)
    kb.dma("gpsimd", ones1.ap_all, ones1d, writes=[ones1])
    cols["ones1"] = ones1
    _xTr = xT.rearrange("(c p) t -> p c t", p=128)
    for _c in range(DC):
        kb.dma("sync", m.Xc[_c][:, 0:T], _xTr[:, _c, :], writes=[m.Xc[_c]])
    kb.dma("sync", m.lnsel.ap_all, clnsel, writes=[m.lnsel])
    kb.dma("sync", m.bcsel.ap_all, cbcsel, writes=[m.bcsel])
    m.layernorm()
    m.attn0({"l0_wq": wq, "l0_wkd": wkd, "l0_wv": wv, "l0_wo": wo}, cols)
    yTr = yT.rearrange("(c p) t -> p c t", p=128)
    for c in range(DC):
        kb.dma("sync", yTr[:, c, :], m.Xc[c][:, 0:T], reads=[m.Xc[c]])
    kb.finalize()
    return kb


def rope_consts():
    import ml_dtypes
    bf = ml_dtypes.bfloat16
    HD = 32
    inv = 1.0 / 10000.0 ** (np.arange(0, HD, 2, dtype=np.float32) / HD)
    t = np.arange(T, dtype=np.float32)
    fr = np.outer(t, inv)
    cosv, sinv = np.cos(fr), np.sin(fr)
    C = np.zeros((128, T), np.float32)
    S = np.zeros((128, T), np.float32)
    for p in range(128):
        C[p] = cosv[:, p % 16]
        S[p] = sinv[:, p % 16] * (1.0 if (p % 32) < 16 else -1.0)
    cm = np.zeros((128, 896), np.float32)
    for p in range(128):
        cm[p] = (p <= np.arange(896) - 384)
    return {"c_ropec": C.astype(bf), "c_ropes": S.astype(bf),
            "c_cmask": cm.astype(bf)}


def prep_attn0_inputs(Wq, Wk, Wv, Wo, lq1, lk1, lq2, lk2, rms_w):
    f = np.float32
    Wk = np.asarray(Wk, f)
    wkd = np.zeros((1024, 1024), f)
    for hc in range(8):
        wkd[:, 128 * hc:128 * hc + 64] = Wk[:, 64 * hc:64 * (hc + 1)]
        wkd[:, 128 * hc + 64:128 * (hc + 1)] = Wk[:, 64 * hc:64 * (hc + 1)]
    return {
        "l0_wq": np.ascontiguousarray(np.asarray(Wq, f)),
        "l0_wkd": wkd,
        "l0_wv": np.ascontiguousarray(np.asarray(Wv, f)),
        "l0_wo": np.ascontiguousarray(np.asarray(Wo, f)),
        "l0_lq1": np.asarray(lq1, f)[None, :].copy(),
        "l0_lk1": np.asarray(lk1, f)[None, :].copy(),
        "l0_lq2": np.asarray(lq2, f)[None, :].copy(),
        "l0_lk2": np.asarray(lk2, f)[None, :].copy(),
        "l0_rmsw": np.ascontiguousarray(np.tile(np.asarray(rms_w, f), 2)[:, None]),
        "c_ones1": np.ones((1, 128), f),
    }



_PROG_CACHE = {}


def _prog(name, builder):
    if name not in _PROG_CACHE:
        _PROG_CACHE[name] = builder()
    return _PROG_CACHE[name]


def kernel(**inp):
    consts = {**host_constants(), **rope_consts(),
              "c_id128": np.eye(128, dtype=np.float32),
              "c_esel": _esel_const(),
              "c_ones1": np.ones((1, 128), np.float32)}
    moe_w = prep_moe_inputs(
        inp["L0_ffn_gate_w"], inp["L0_ffn_gate_b"], inp["L0_ffn_w1"],
        inp["L0_ffn_b1"], inp["L0_ffn_w3"], inp["L0_ffn_b3"], inp["L0_ffn_w2"],
        inp["L0_ffn_b2"], inp["L0_ffn_sw1"], inp["L0_ffn_sb1"],
        inp["L0_ffn_sw3"], inp["L0_ffn_sb3"], inp["L0_ffn_sw2"],
        inp["L0_ffn_sb2"])
    a0_w = prep_attn0_inputs(
        inp["L0_attn_Wq"], inp["L0_attn_Wk"], inp["L0_attn_Wv"],
        inp["L0_attn_Wo"], inp["L0_attn_lambda_q1"], inp["L0_attn_lambda_k1"],
        inp["L0_attn_lambda_q2"], inp["L0_attn_lambda_k2"], inp["L0_attn_rms_w"])
    f1_w = prep_ffn1_inputs(inp["L1_ffn_w1"], inp["L1_ffn_b1"],
                            inp["L1_ffn_w2"], inp["L1_ffn_b2"])
    a1_w = prep_attn1_inputs(
        inp["L1_attn_in_proj_w"], inp["L1_attn_in_proj_b"],
        inp["L1_attn_out_w"], inp["L1_attn_out_b"], inp["L1_attn_lin1_w"],
        inp["L1_attn_lin1_b"], inp["L1_attn_lin2_w"], inp["L1_attn_lin2_b"],
        inp["L1_attn_ln1_g"], inp["L1_attn_ln1_b"], inp["L1_attn_ln2_g"],
        inp["L1_attn_ln2_b"])
    allw = {**consts, **moe_w, **a0_w, **f1_w, **a1_w}

    x = np.asarray(inp["L0_attn_x"], np.float32)
    cur = [np.ascontiguousarray(x[c].T) for c in range(8)]
    kb = _prog("full", build_full_program)
    names = [n for n in kb.input_names if n != "xT"]
    in_maps = [dict(xT=cur[c], **{n: allw[n] for n in names}) for c in range(8)]
    res = run_bass_kernel_spmd(kb.nc, in_maps, core_ids=list(range(8)))
    return np.stack([np.asarray(res.results[c]["yT"], np.float32).T
                     for c in range(8)])


def full_in_map(inp_or_xT=None, **inp):
    consts = {**host_constants(), **rope_consts(),
              "c_id128": np.eye(128, dtype=np.float32),
              "c_esel": _esel_const(),
              "c_ones1": np.ones((1, 128), np.float32)}
    return consts


def build_full_program():
    kb = KB()
    xT = kb.dram_in("xT", [D, T])
    yT = kb.dram_out("yT", [D, T])
    m_w1 = kb.dram_in("m_w1", [9 * 1024, 1408])
    m_w3 = kb.dram_in("m_w3", [9 * 1024, 1408])
    m_w2 = kb.dram_in("m_w2", [9 * 1408, 1024])
    dins = {}
    for nm, sh, dt in (
            ("m_b1c", [128, 99], F32), ("m_b3c", [128, 99], F32),
            ("m_b2s", [9, 1024], F32), ("g_gwt", [128, 64], F32),
            ("g_gbt", [128, 8], F32), ("c_id128", [128, 128], F32),
            ("c_esel", [9, 9 * 128], BF16),
            ("l0_wq", [1024, 1024], F32), ("l0_wkd", [1024, 1024], F32),
            ("l0_wv", [1024, 512], F32), ("l0_wo", [1024, 1024], F32),
            ("l0_lq1", [1, 32], F32), ("l0_lk1", [1, 32], F32),
            ("l0_lq2", [1, 32], F32), ("l0_lk2", [1, 32], F32),
            ("l0_rmsw", [128, 1], F32),
            ("c_ropec", [128, T], BF16), ("c_ropes", [128, T], BF16),
            ("c_cmask", [128, 896], BF16), ("c_ones1", [1, 128], F32),
            ("f1_w1", [1024, 4096], F32), ("f1_w2", [4096, 1024], F32),
            ("f1_b1c", [128, 32], F32), ("f1_b2c", [128, 8], F32),
            ("a_wqkv", [1024, 3072], F32), ("a_wo", [1024, 1024], F32),
            ("a_w1", [1024, 4096], F32), ("a_w2", [4096, 1024], F32),
            ("a_bq", [128, 8], F32), ("a_bk", [128, 8], F32),
            ("a_bo", [128, 8], F32),
            ("a_b1", [128, 32], F32), ("a_b2", [128, 8], F32),
            ("a_g1", [128, 8], F32), ("a_b1n", [128, 8], F32),
            ("a_g2", [128, 8], F32), ("a_b2n", [128, 8], F32),
            ("c_lnsel", [128, 16 * NTT], BF16), ("c_bcsel", [16, 128 * NTT], F32)):
        dins[nm] = kb.dram_in(nm, sh, dt)
    dins["a_bvrow"] = kb.dram_in("a_bvrow", [1, 1024], F32)

    m = Model(kb)
    sb = {}
    bf_ins = {"c_esel"}
    cast_bf = {"m_b2s", "g_gwt", "c_ones1"}
    streamed = {"l0_wq", "l0_wkd", "l0_wv", "l0_wo", "f1_w1", "f1_w2",
                "a_wqkv", "a_wo", "a_w1", "a_w2", "c_ropec", "c_ropes",
                "c_cmask", "a_bvrow"}
    prevc = {"sync": None, "gpsimd": None}
    for nm in dins:
        if nm in ("c_lnsel", "c_bcsel") or nm in streamed:
            continue
        sh = list(dins[nm].shape)
        dt = BF16 if (nm in bf_ins or nm in cast_bf) else F32
        b = kb.sbuf("sb_" + nm, sh, dt)
        q = "gpsimd" if (nm in cast_bf) else "sync"
        rd = [prevc[q]] if prevc[q] is not None else []
        kb.dma(q, b.ap_all, dins[nm], reads=rd, writes=[b],
               sem_group="consts_" + q)
        prevc[q] = b
        sb[nm] = b
    xTr = xT.rearrange("(c p) t -> p c t", p=128)
    for c in range(DC):
        kb.dma("sync", m.Xc[c][:, 0:T], xTr[:, c, :], writes=[m.Xc[c]])
    kb.dma("sync", m.lnsel.ap_all, dins["c_lnsel"], writes=[m.lnsel])
    kb.dma("sync", m.bcsel.ap_all, dins["c_bcsel"], writes=[m.bcsel])

    bfw = {}
    deps = {}

    prev_cast = [None]

    def cast_w(name, src, chunks):
        shp = list(src.shape)
        dst = kb.dram_internal("bfw_" + name, shp)
        bufs = []
        for i, (r0, r1) in enumerate(chunks):
            b = Buf(None, f"cw_{name}_{i}")
            kb.bufs.append(b)
            rd = [prev_cast[0]] if prev_cast[0] is not None else []
            o = kb.op("gpsimd",
                      lambda e, src=src, dst=dst, r0=r0, r1=r1: e.dma_start(
                          out=dst[r0:r1, :], in_=src[r0:r1, :]),
                      reads=rd, writes=[b], is_dma=True)
            o.dma_buf = b
            o.sem_group = "castq"
            prev_cast[0] = b
            bufs.append(b)
        bfw[name] = dst
        return bufs

    for nm, src, rw in (("m_w1", m_w1, 1024), ("m_w3", m_w3, 1024),
                        ("m_w2", m_w2, 1408)):
        bl = cast_w(nm, src, [(rw * ex, rw * (ex + 1)) for ex in range(9)])
        for ex in range(9):
            deps[(nm, ex)] = [bl[ex]]
    for nm in ("l0_wq", "l0_wkd", "l0_wv", "l0_wo", "f1_w1", "f1_w2",
               "a_wqkv", "a_wo", "a_w1", "a_w2"):
        deps[nm] = cast_w(nm, dins[nm], [(0, dins[nm].shape[0])])

    moe_cols = {"gwt": sb["g_gwt"], "gbt": sb["g_gbt"], "b2s": sb["m_b2s"],
                "id128": sb["c_id128"], "esel": sb["c_esel"]}
    moe_deps = {}
    for ex in range(9):
        for nm in ("m_w1", "m_w3", "m_w2"):
            moe_deps[(nm, ex)] = deps[(nm, ex)]
    moe_w = {"m_w1": bfw["m_w1"], "m_w3": bfw["m_w3"],
             "m_w2": bfw["m_w2"], "deps": moe_deps}
    a0_cols = {"lq1": sb["l0_lq1"], "lk1": sb["l0_lk1"], "lq2": sb["l0_lq2"],
               "lk2": sb["l0_lk2"], "rmsw": sb["l0_rmsw"],
               "ropec_d": dins["c_ropec"], "ropes_d": dins["c_ropes"],
               "cmask_d": dins["c_cmask"], "ones1": sb["c_ones1"]}
    a0_w = {"l0_wq": bfw["l0_wq"], "l0_wkd": bfw["l0_wkd"],
            "l0_wv": bfw["l0_wv"], "l0_wo": bfw["l0_wo"],
            "deps": {k: deps[k] for k in ("l0_wq", "l0_wkd", "l0_wv", "l0_wo")}}
    a1_cols = {"bq": sb["a_bq"], "bk": sb["a_bk"], "bvrow_d": dins["a_bvrow"],
               "bo": sb["a_bo"], "b1": sb["a_b1"], "b2": sb["a_b2"],
               "g1": sb["a_g1"], "b1n": sb["a_b1n"], "g2": sb["a_g2"],
               "b2n": sb["a_b2n"], "ones1": sb["c_ones1"]}
    a1_w = {"a_wqkv": bfw["a_wqkv"], "a_wo": bfw["a_wo"],
            "a_w1": bfw["a_w1"], "a_w2": bfw["a_w2"],
            "deps": {k: deps[k] for k in ("a_wqkv", "a_wo", "a_w1", "a_w2")}}

    m.layernorm()
    m.moe(moe_w, sb["m_b1c"], sb["m_b3c"], moe_cols)
    m.layernorm()
    m.attn0(a0_w, a0_cols)
    m.layernorm()
    m.moe(moe_w, sb["m_b1c"], sb["m_b3c"], moe_cols)
    m.layernorm()
    f1w = {"w1": bfw["f1_w1"], "w2": bfw["f1_w2"],
           "deps": {"w1": deps["f1_w1"], "w2": deps["f1_w2"]}}
    m.ffn1(f1w, sb["f1_b1c"], sb["f1_b2c"])
    m.layernorm()
    m.attn1(a1_w, a1_cols)
    m.layernorm()
    m.ffn1(f1w, sb["f1_b1c"], sb["f1_b2c"])
    yTr = yT.rearrange("(c p) t -> p c t", p=128)
    for c in range(DC):
        kb.dma("sync", yTr[:, c, :], m.Xc[c][:, 0:T], reads=[m.Xc[c]])
    kb.finalize()
    return kb


# version 14
# speedup vs baseline: 1.0327x; 1.0327x over previous
import math
import itertools
import contextlib
from collections import defaultdict

import numpy as np
import concourse.bass as bass
import concourse.mybir as mybir
from concourse.bass_utils import run_bass_kernel_spmd

F32 = mybir.dt.float32
BF16 = mybir.dt.bfloat16
U8 = mybir.dt.uint8
AF = mybir.ActivationFunctionType
ALU = mybir.AluOpType
AX = mybir.AxisListType

MAGIC = float(np.float32(12582912.0))


ENGS = ("tensor", "vector", "scalar", "gpsimd", "sync")


class Op:
    __slots__ = ("eng", "emit", "deps", "inc", "count", "is_dma", "dma_buf",
                 "skey", "sem_group")

    def __init__(self, eng, emit, deps, is_dma):
        self.eng = eng
        self.emit = emit
        self.deps = deps
        self.is_dma = is_dma
        self.inc = is_dma
        self.count = None
        self.sem_group = None


class Buf:

    __slots__ = ("h", "writer", "readers", "name", "win", "after_bufs")

    def __init__(self, h, name, win=None):
        self.h = h
        self.name = name
        self.writer = None
        self.readers = []
        self.win = win
        self.after_bufs = None

    def __getitem__(self, sl):
        if self.win is None:
            return self.h[sl]
        if len(self.win) == 4:
            ro, rows, off, n = self.win
        else:
            rows, off, n = self.win
            ro = 0
        rs, cs = sl
        start = (cs.start or 0) + off
        stop = (cs.stop if cs.stop is not None else n) + off
        assert stop - off <= n
        r0 = 0 if rs == slice(None) else (rs.start or 0)
        r1 = rows if rs == slice(None) else rs.stop
        return self.h[ro + r0:ro + r1, start:stop]

    @property
    def ap_all(self):
        if self.win is None:
            return self.h[:, :]
        if len(self.win) == 4:
            ro, rows, off, n = self.win
            return self.h[ro:ro + rows, off:off + n]
        rows, off, n = self.win
        return self.h[0:rows, off:off + n]


class KB:

    def __init__(self, name="mk"):
        self.nc = bass.Bass("TRN2", target_bir_lowering=False)
        self.ctx = contextlib.ExitStack()
        self.ops = []
        self.bufs = []
        self._uid = 0

    def uid(self):
        self._uid += 1
        return self._uid

    def dram_in(self, name, shape, dtype=F32):
        if not hasattr(self, "input_names"):
            self.input_names = []
        self.input_names.append(name)
        return self.nc.dram_tensor(name, shape, dtype, kind="ExternalInput").ap()

    def dram_out(self, name, shape, dtype=F32):
        return self.nc.dram_tensor(name, shape, dtype, kind="ExternalOutput").ap()

    def dram_internal(self, name, shape, dtype=BF16):
        return self.nc.dram_tensor(name, shape, dtype).ap()

    def sbuf(self, name, shape, dtype=F32):
        h = self.ctx.enter_context(self.nc.sbuf_tensor(name, shape, dtype))
        b = Buf(h, name)
        self.bufs.append(b)
        return b

    def psum(self, name, shape, dtype=F32):
        per = {F32: 512, BF16: 1024}[dtype]
        assert shape[1] <= per
        h = self.ctx.enter_context(self.nc.psum_tensor(name, [128, per], dtype))
        b = Buf(h, name)
        self.bufs.append(b)
        return b

    def op(self, eng, emit, reads=(), writes=(), is_dma=False):
        deps = []
        for b in reads:
            if b.writer is not None:
                deps.append(b.writer)
        o = Op(eng, emit, deps, is_dma)
        for b in reads:
            b.readers.append(o)
        for b in writes:
            if b.after_bufs:
                for ob in b.after_bufs:
                    deps.extend(ob.readers)
                    if ob.writer is not None:
                        deps.append(ob.writer)
                b.after_bufs = None
            if b.writer is not None:
                deps.append(b.writer)
            deps.extend(b.readers)
            b.writer = o
            b.readers = []
        o.deps = [d for d in dict.fromkeys(deps) if d is not o]
        self.ops.append(o)
        return o

    def dma(self, queue, out, in_, reads=(), writes=(), sem_group=None, **kw):
        o = self.op(
            queue,
            lambda e: e.dma_start(out=out, in_=in_, **kw),
            reads=reads, writes=writes, is_dma=True)
        o.dma_buf = writes[0] if writes else None
        o.sem_group = sem_group
        return o

    def emit_all(self):
        def sem_key(o):
            if not o.is_dma:
                return o.eng
            if getattr(o, "sem_group", None) is not None:
                return ("buf", o.sem_group)
            if getattr(o, "dma_buf", None) is not None:
                return ("buf", id(o.dma_buf))
            return ("store", o.eng)

        for o in self.ops:
            for d in o.deps:
                if d.eng == "tensor" and o.eng == "tensor":
                    continue
                d.inc = True
        cnt = defaultdict(int)
        store_totals = defaultdict(int)
        for o in self.ops:
            if o.inc:
                k = sem_key(o)
                cnt[k] += 16 if o.is_dma else 1
                o.count = cnt[k]
                o.skey = k
                if not isinstance(k, str) and k[0] == "store":
                    store_totals[k] = cnt[k]
        sems = {}
        for k in cnt:
            nm = k if isinstance(k, str) else f"{k[0]}_{k[1]}"
            sems[k] = self.ctx.enter_context(self.nc.semaphore(f"s_{nm}"))
        running_store = defaultdict(int)
        waitsfor = []
        ptr = {}
        for o in self.ops:
            if o.inc and not isinstance(o.skey, str) and o.skey[0] == "store":
                running_store[o.skey] = o.count
            wmap = {}
            for d in o.deps:
                if d.eng == "tensor" and o.eng == "tensor":
                    continue
                k = d.skey if hasattr(d, "skey") and d.inc else None
                if k is None:
                    continue
                c = d.count
                if not isinstance(k, str) and k[0] == "store":
                    c = running_store[k]
                if wmap.get(k, 0) < c:
                    wmap[k] = c
            waitsfor.append(wmap)
        waited = defaultdict(int)
        plans = {e: [] for e in ENGS}
        for o, wmap in zip(self.ops, waitsfor):
            w2 = {}
            for k, c in wmap.items():
                if waited[(o.eng, k)] < c:
                    waited[(o.eng, k)] = c
                    w2[k] = c
            plans[o.eng].append((w2, o))
        block = self.ctx.enter_context(self.nc.Block())

        def run_section(engine, eng_name):
            for wmap, o in plans[eng_name]:
                for k, c in wmap.items():
                    engine.wait_ge(sems[k], c)
                ins = o.emit(engine)
                if o.inc:
                    ins.then_inc(sems[o.skey], 16 if o.is_dma else 1)

        for e in ENGS:
            sec = getattr(block, e)

            def make(e=e):
                def _(engine):
                    run_section(engine, e)
                return _
            sec(make())
        self.ctx.close()
        self.n_sems = len(sems)

    def finalize(self):
        last_dma = {}
        for o in self.ops:
            if o.is_dma:
                last_dma[o.eng] = o
        for e, o in last_dma.items():
            w = Op(e, lambda engine: engine.nop(), [o], False)
            self.ops.append(w)
        self.emit_all()



def ceil_div(a, b):
    return (a + b - 1) // b


T = 2048
D = 1024
DC = D // 128
NTT = T // 512
LN_EPS = 1e-5


def host_constants():
    import ml_dtypes
    bf = ml_dtypes.bfloat16
    lnsel = np.zeros((128, 16 * NTT), dtype=bf)
    for i in range(NTT):
        lnsel[:, 16 * i + i] = 1.0
    bcsel = np.zeros((16, 128 * NTT), dtype=np.float32)
    for i in range(NTT):
        bcsel[i, 128 * i:128 * (i + 1)] = 1.0
    return {"c_lnsel": lnsel, "c_bcsel": bcsel}


class Model:

    def __init__(self, kb: KB):
        self.kb = kb
        nc = kb.nc
        self.X = kb.sbuf("X", [128, DC * T], F32)
        self.Xc = [Buf(self.X.h, f"Xc{c}", win=(128, c * T, T)) for c in range(DC)]
        kb.bufs.extend(self.Xc)
        self.xn = kb.sbuf("xn", [128, DC * T], BF16)
        self.xnc = [Buf(self.xn.h, f"xnc{c}", win=(128, c * T, T)) for c in range(DC)]
        kb.bufs.extend(self.xnc)
        self.psd = []
        for j in range(4):
            h = kb.ctx.enter_context(kb.nc.psum_tensor(f"psd{j}", [128, 1024], F32))
            self.psd.append(h)
        self.ps = []
        for i in range(8):
            b = Buf(self.psd[i // 2], f"ps{i}", win=(128, 512 * (i % 2), 512))
            kb.bufs.append(b)
            self.ps.append(b)
        self.xsq = [kb.sbuf("xsq0", [128, 512], BF16)]
        self.xbb = [kb.sbuf("xbb0", [128, 512], BF16)]
        strow = kb.sbuf("strow", [16, 2 * 512], F32)
        self.strow = [Buf(strow.h, f"strow{i}", win=(16, 512 * (i % 2), 512))
                      for i in range(2)]
        self._tscr = []
        self._mscr = None
        self.ccol = kb.sbuf("ccol", [128, 8], F32)
        self._ccol_vals = {}
        self.lnsel = kb.sbuf("lnsel", [128, 16 * NTT], BF16)
        self.bcsel = kb.sbuf("bcsel", [16, 128 * NTT], F32)

    def arena_win(self, name, off, n, after=()):
        if getattr(self, "_arena", None) is None:
            self._arena = self.kb.sbuf("arena", [128, 40960], BF16)
            self._arena_owners = []
        assert off + n <= 40960, (name, off, n)
        b = Buf(self._arena.h, name, win=(128, off, n))
        aft = list(after)
        for (o2, n2, ob) in self._arena_owners:
            if o2 < off + n and off < o2 + n2:
                aft.append(ob)
        b.after_bufs = aft
        self._arena_owners.append((off, n, b))
        self.kb.bufs.append(b)
        return b

    def arena_win_f32(self, name, off_bf16, n_f32):
        if getattr(self, "_arena", None) is None:
            self.arena_win("_dummy", 0, 1)
        if getattr(self, "_arena_f32h", None) is None:
            self._arena_f32h = self._arena.h.bitcast(F32)
        assert off_bf16 % 2 == 0 and off_bf16 + 2 * n_f32 <= 40960
        b = Buf(self._arena_f32h, name, win=(128, off_bf16 // 2, n_f32))
        aft = []
        for (o2, n2, ob) in self._arena_owners:
            if o2 < off_bf16 + 2 * n_f32 and off_bf16 < o2 + n2:
                aft.append(ob)
        b.after_bufs = aft
        self._arena_owners.append((off_bf16, 2 * n_f32, b))
        self.kb.bufs.append(b)
        return b

    def arena_win_u8(self, name, off_bf16, n_u8):
        if getattr(self, "_arena", None) is None:
            self.arena_win("_dummy", 0, 1)
        if getattr(self, "_arena_u8h", None) is None:
            self._arena_u8h = self._arena.h.bitcast(U8)
        assert n_u8 % 2 == 0 and off_bf16 * 2 + n_u8 <= 81920
        b = Buf(self._arena_u8h, name, win=(128, off_bf16 * 2, n_u8))
        aft = []
        for (o2, n2, ob) in self._arena_owners:
            if o2 < off_bf16 + n_u8 // 2 and off_bf16 < o2 + n2:
                aft.append(ob)
        b.after_bufs = aft
        self._arena_owners.append((off_bf16, n_u8 // 2, b))
        self.kb.bufs.append(b)
        return b

    def buf(self, name, shape, dtype=F32):
        if not hasattr(self, "_bufcache"):
            self._bufcache = {}
        if name not in self._bufcache:
            self._bufcache[name] = self.kb.sbuf(name, shape, dtype)
        return self._bufcache[name]

    def scr(self, i):
        while len(self._tscr) <= i:
            self._tscr.append(self.kb.sbuf(f"tscr{len(self._tscr)}", [128, 512], F32))
        return self._tscr[i]

    @property
    def mscr(self):
        if self._mscr is None:
            self._mscr = self.kb.sbuf("mscr", [128, 512], U8)
        return self._mscr

    def const_col(self, val):
        kb = self.kb
        if val not in self._ccol_vals:
            i = len(self._ccol_vals)
            assert i < 8
            self._ccol_vals[val] = i
            kb.op("vector", lambda e, i=i, val=val: e.memset(
                self.ccol[:, i:i + 1], float(val)),
                writes=[self.ccol])
        i = self._ccol_vals[val]
        return self.ccol[:, i:i + 1]

    def load_consts(self, kb, cmap):
        kb.dma("sync", self.lnsel.ap_all, cmap["c_lnsel"], writes=[self.lnsel])
        kb.dma("sync", self.bcsel.ap_all, cmap["c_bcsel"], writes=[self.bcsel])

    def layernorm(self, dst=None, src=None, eps=LN_EPS, gain_bias=None):
        kb = self.kb
        src = src if src is not None else self.Xc
        dst = dst or self.xn
        if dst is self.xn:
            dsts = self.xnc
            dstap = lambda c, a, b: self.xnc[c][:, a:b]
        else:
            dsts = [dst] * DC
            dstap = lambda c, a, b: dst[:, c * T + a:c * T + b]
        if isinstance(src, list):
            srcs = src
            srcap = lambda c, a, b: srcs[c][:, a:b]
        else:
            srcs = [src] * DC
            srcap = lambda c, a, b: src[:, c * T + a:c * T + b]
        ps_sx, ps_sq = self.ps[6], self.ps[7]
        mrow = self.strow[0]
        qrow = self.strow[1]
        trow = self.strow[1]
        srow = self.strow[1]
        arow = self.strow[1]
        brow = self.strow[0]
        for c in range(DC):
            for i in range(NTT):
                xs = self.xsq[0]
                xb = self.xbb[0]
                off = 512 * i
                kb.op("scalar",
                      lambda e, off=off, xs=xs, c=c: e.activation(
                          xs[:, :], srcap(c, off, off + 512), AF.Square),
                      reads=[srcs[c]], writes=[xs])
                kb.op("vector",
                      lambda e, off=off, xb=xb, c=c: e.tensor_copy(
                          xb[:, :], srcap(c, off, off + 512)),
                      reads=[srcs[c]], writes=[xb])
                st = (c == 0 and i == 0)
                fin = (c == DC - 1 and i == NTT - 1)
                kb.op("tensor",
                      lambda e, i=i, xs=xs, st=st, fin=fin: e.matmul(
                          ps_sq[0:16, :], self.lnsel[:, 16 * i:16 * (i + 1)],
                          xs[:, :], start=st, stop=fin),
                      reads=[xs, self.lnsel], writes=[ps_sq])
                kb.op("tensor",
                      lambda e, i=i, xb=xb, st=st, fin=fin: e.matmul(
                          ps_sx[0:16, :], self.lnsel[:, 16 * i:16 * (i + 1)],
                          xb[:, :], start=st, stop=fin),
                      reads=[xb, self.lnsel], writes=[ps_sx])
        kb.op("vector", lambda e: e.tensor_scalar(
            mrow[:, :], ps_sx[0:16, :], 1.0 / D, None, ALU.mult),
            reads=[ps_sx], writes=[mrow])
        kb.op("vector", lambda e: e.tensor_tensor(
            qrow[:, :], mrow[:, :], mrow[:, :], ALU.mult),
            reads=[mrow], writes=[qrow])
        kb.op("vector", lambda e: e.scalar_tensor_tensor(
            qrow[:, :], ps_sq[0:16, :], 1.0 / D, qrow[:, :],
            ALU.mult, ALU.subtract),
            reads=[ps_sq, qrow], writes=[qrow])
        epsc = self.const_col(float(eps))
        kb.op("scalar", lambda e, epsc=epsc: e.activation(
            srow[:, :], qrow[:, :], AF.Sqrt, bias=epsc[:16, :]),
            reads=[qrow, self.ccol], writes=[srow])
        kb.op("vector", lambda e: e.reciprocal(arow[:, :], srow[:, :]),
              reads=[srow], writes=[arow])
        kb.op("vector", lambda e: e.scalar_tensor_tensor(
            brow[:, :], mrow[:, :], -1.0, arow[:, :], ALU.mult, ALU.mult),
            reads=[mrow, arow], writes=[brow])
        psA, psB = self.ps[4], self.ps[5]
        for i in range(NTT):
            kb.op("tensor", lambda e, i=i: e.matmul(
                psA[:, :], self.bcsel[:, 128 * i:128 * (i + 1)], arow[:, :],
                start=True, stop=True),
                reads=[arow, self.bcsel], writes=[psA])
            kb.op("tensor", lambda e, i=i: e.matmul(
                psB[:, :], self.bcsel[:, 128 * i:128 * (i + 1)], brow[:, :],
                start=True, stop=True),
                reads=[brow, self.bcsel], writes=[psB])
            for c in range(DC):
                tsc = self.scr(c % 2)
                sl = slice(c * T + 512 * i, c * T + 512 * (i + 1))
                kb.op("vector", lambda e, i=i, tsc=tsc, c=c: e.tensor_tensor(
                    tsc[:, 0:512], srcap(c, 512 * i, 512 * (i + 1)), psA[:, :],
                    ALU.mult),
                    reads=[srcs[c], psA], writes=[tsc])
                if gain_bias is None:
                    kb.op("vector", lambda e, i=i, tsc=tsc, c=c: e.tensor_tensor(
                        dstap(c, 512 * i, 512 * (i + 1)), tsc[:, 0:512],
                        psB[:, :], ALU.add),
                        reads=[tsc, psB], writes=[dsts[c]])
                else:
                    gcol, bcol = gain_bias
                    kb.op("vector", lambda e, sl=sl, tsc=tsc: e.tensor_tensor(
                        tsc[:, 0:512], tsc[:, 0:512], psB[:, :], ALU.add),
                        reads=[tsc, psB], writes=[tsc])
                    kb.op("vector", lambda e, i=i, tsc=tsc, c=c: e.tensor_scalar(
                        dstap(c, 512 * i, 512 * (i + 1)), tsc[:, 0:512], gcol[:, c:c + 1],
                        bcol[:, c:c + 1], ALU.mult, ALU.add),
                        reads=[tsc, gcol, bcol], writes=[dsts[c]])

    def ffn1(self, wmap, b1c, b2c):
        kb = self.kb
        HF = 4096
        NHC = HF // 128
        wdep = wmap.get("deps", {})
        hbuf = self.arena_win("f1_hbuf", 0, NHC * 512)
        w1sl = [self.arena_win(f"f1_w1sl{i}", 16384 + 4096 * i, DC * 512)
                for i in range(2)]
        w2ch = [self.arena_win(f"f1_w2ch{i}", 24576 + 2048 * i, 2048)
                for i in range(2)]
        scrsets = [[self.scr(i) for i in range(5)]]
        for k in range(2):
            scrsets.append([self.arena_win_f32(f"f1_scrB{k}_{i}",
                                               28672 + 5120 * k + 1024 * i, 512)
                            for i in range(5)])
        w1r = wmap["w1"].rearrange("(c p) o -> p c o", p=128)
        w2r = wmap["w2"]
        for ti in range(NTT):
            toff = 512 * ti
            for sl_i in range(DC):
                wb = w1sl[sl_i % 2]
                kb.dma("sync", wb.ap_all, w1r[:, :, 512 * sl_i:512 * (sl_i + 1)],
                       reads=wdep.get("w1", ()), writes=[wb])
                for hh in range(4):
                    hc = 4 * sl_i + hh
                    ps = self.ps[hc % 4]
                    for kc in range(DC):
                        kb.op("tensor", lambda e, wb=wb, hh=hh, kc=kc, ps=ps, toff=toff: e.matmul(
                            ps[:, :],
                            wb[:, 512 * kc + 128 * hh: 512 * kc + 128 * hh + 128],
                            self.xnc[kc][:, toff:toff + 512],
                            start=(kc == 0), stop=(kc == DC - 1)),
                            reads=[wb, self.xnc[kc]], writes=[ps])
                    self._pwl(ps, b1c, hc, hbuf, 512 * hc, 512,
                              scrset=scrsets[hc % 3])
            for kcp in range(16):
                wc = w2ch[kcp % 2]
                kb.dma("sync", wc.ap_all,
                       w2r[256 * kcp:256 * (kcp + 1), :]
                       .rearrange("(c p) o -> p c o", p=128),
                       reads=wdep.get("w2", ()), writes=[wc])
                for kk in range(2):
                    kc = 2 * kcp + kk
                    for dc in range(DC):
                        ps = self.ps[dc]
                        kb.op("tensor", lambda e, wc=wc, kc=kc, kk=kk, dc=dc, ps=ps: e.matmul(
                            ps[:, :],
                            wc[:, 1024 * kk + 128 * dc: 1024 * kk + 128 * dc + 128],
                            hbuf[:, kc * 512: (kc + 1) * 512],
                            start=(kc == 0), stop=(kc == NHC - 1)),
                            reads=[wc, hbuf], writes=[ps])
            for dc in range(DC):
                ps = self.ps[dc]
                sl = slice(dc * T + toff, dc * T + toff + 512)
                t0 = self.scr(dc % 4)
                Xc = self.Xc[dc]
                kb.op("vector", lambda e, dc=dc, ps=ps, t0=t0: e.tensor_scalar(
                    t0[:, 0:512], ps[:, :], b2c[:, dc:dc + 1], 256.0,
                    ALU.add, ALU.mult),
                    reads=[ps, b2c], writes=[t0])
                kb.op("vector", lambda e, t0=t0: e.tensor_scalar(
                    t0[:, 0:512], t0[:, 0:512], MAGIC, MAGIC,
                    ALU.add, ALU.subtract),
                    reads=[t0], writes=[t0])
                kb.op("vector", lambda e, t0=t0, toff=toff, Xc=Xc: e.scalar_tensor_tensor(
                    Xc[:, toff:toff + 512], t0[:, 0:512], 1.0 / 512.0,
                    Xc[:, toff:toff + 512], ALU.mult, ALU.add),
                    reads=[t0, Xc], writes=[Xc])
        return

    def _pwl(self, ps, b1c, hc, hbuf, hoff, W, scrset=None):
        kb = self.kb
        GELU4_256 = 256.0 * (2.0 + 2.0 * math.erf(2.0 * math.sqrt(2.0)))
        if scrset is None:
            scrset = [self.scr(i) for i in range(5)]
        nt, ncl, jj, tt_, g0 = scrset
        g1 = jj
        dd = ncl
        yy = tt_
        kb.op("vector", lambda e: e.tensor_scalar(
            nt[:, 0:W], ps[:, 0:W], b1c[:, hc:hc + 1], 256.0, ALU.add, ALU.mult),
            reads=[ps, b1c], writes=[nt])
        kb.op("vector", lambda e: e.tensor_scalar(
            nt[:, 0:W], nt[:, 0:W], MAGIC, MAGIC, ALU.add, ALU.subtract),
            reads=[nt], writes=[nt])
        kb.op("vector", lambda e: e.tensor_scalar(
            ncl[:, 0:W], nt[:, 0:W], -1024.0, 1024.0, ALU.max, ALU.min),
            reads=[nt], writes=[ncl])
        MAGIC64 = float(np.float32(64.0 * 12582912.0))
        kb.op("vector", lambda e: e.tensor_scalar(
            g0[:, 0:W], ncl[:, 0:W], 31.5, None, ALU.subtract),
            reads=[ncl], writes=[g0])
        kb.op("vector", lambda e, MAGIC64=MAGIC64: e.tensor_scalar(
            jj[:, 0:W], g0[:, 0:W], MAGIC64, MAGIC64, ALU.add, ALU.subtract),
            reads=[g0], writes=[jj])
        kb.op("gpsimd", lambda e: e.tensor_tensor(
            tt_[:, 0:W], ncl[:, 0:W], jj[:, 0:W], ALU.subtract),
            reads=[ncl, jj], writes=[tt_])
        kb.op("scalar", lambda e: e.activation(
            g0[:, 0:W], jj[:, 0:W], AF.Gelu, bias=0.0, scale=1.0 / 256.0),
            reads=[jj], writes=[g0])
        qc = self.const_col(0.25)
        kb.op("scalar", lambda e, qc=qc: e.activation(
            g1[:, 0:W], jj[:, 0:W], AF.Gelu, bias=qc, scale=1.0 / 256.0),
            reads=[jj, self.ccol], writes=[g1])
        kb.op("gpsimd", lambda e: e.tensor_tensor(
            dd[:, 0:W], g1[:, 0:W], g0[:, 0:W], ALU.subtract),
            reads=[g1, g0], writes=[dd])
        kb.op("gpsimd", lambda e: e.tensor_tensor(
            dd[:, 0:W], tt_[:, 0:W], dd[:, 0:W], ALU.mult),
            reads=[tt_, dd], writes=[dd])
        kb.op("vector", lambda e: e.scalar_tensor_tensor(
            yy[:, 0:W], dd[:, 0:W], 1.0 / 64.0, g0[:, 0:W], ALU.mult, ALU.add),
            reads=[dd, g0], writes=[yy])
        kb.op("vector", lambda e: e.tensor_scalar(
            nt[:, 0:W], nt[:, 0:W], GELU4_256, 0.0, ALU.subtract, ALU.max),
            reads=[nt], writes=[nt])
        kb.op("vector", lambda e: e.scalar_tensor_tensor(
            yy[:, 0:W], nt[:, 0:W], 1.0 / 256.0, yy[:, 0:W], ALU.mult, ALU.add),
            reads=[nt, yy], writes=[yy])
        kb.op("vector", lambda e: e.tensor_scalar(
            yy[:, 0:W], yy[:, 0:W], 256.0, MAGIC, ALU.mult, ALU.add),
            reads=[yy], writes=[yy])
        kb.op("vector", lambda e: e.tensor_scalar(
            hbuf[:, hoff:hoff + W], yy[:, 0:W], MAGIC, 1.0 / 256.0,
            ALU.subtract, ALU.mult),
            reads=[yy], writes=[hbuf])

    def moe(self, wmap, b1c, b3c, cols):
        kb = self.kb
        HE = 1408
        NHC = 11
        gwt, gbt, b2s, id128, esel = (cols[k] for k in
                                      ("gwt", "gbt", "b2s", "id128", "esel"))
        gscr = self.buf("moe_gscr", [128, 48], F32)
        rw9f = self.buf("moe_rw9f", [128, 16], F32)
        rwT9b = self.buf("moe_rwT9", [9, T], BF16)
        rwT9 = rwT9b
        w1b = self.arena_win("moe_w1b", 0, DC * HE)
        w3b = self.arena_win("moe_w3b", DC * HE, DC * HE)
        w2b = self.arena_win("moe_w2b", 2 * DC * HE, NHC * 1024)
        hst = [self.arena_win("moe_hst", 33792, NHC * 512)]
        cbuf, ubuf = self.scr(0), self.scr(1)
        cbuf2, ubuf2 = self.scr(2), self.scr(3)
        ssb = self.arena_win("moe_ssb", 39424, 512)
        ssb2 = self.arena_win("moe_ssb2", 40448, 512)
        bcsb = self.arena_win("moe_bcsb", 39936, 512)
        psG, psT, psB = self.ps[0], self.ps[1], self.ps[7]

        kb.op("vector", lambda e: e.memset(rw9f[:, 8:9], 1.0), writes=[rw9f])
        for ti in range(16):
            for kc in range(DC):
                kb.op("tensor", lambda e, ti=ti, kc=kc: e.matmul(
                    psG[:, 0:8], self.xnc[kc][:, 128 * ti:128 * (ti + 1)],
                    gwt[:, 8 * kc:8 * (kc + 1)],
                    start=(kc == 0), stop=(kc == DC - 1)),
                    reads=[self.xnc[kc], gwt], writes=[psG])
            kb.op("vector", lambda e: e.reduce_max(
                gscr[:, 0:1], psG[:, 0:8], axis=AX.X, negate=True),
                reads=[psG], writes=[gscr])
            kb.op("scalar", lambda e: e.activation(
                gscr[:, 8:16], psG[:, 0:8], AF.Exp, bias=gscr[:, 0:1],
                accum_out=gscr[:, 1:2]),
                reads=[psG, gscr], writes=[gscr])
            kb.op("vector", lambda e: e.reciprocal(
                gscr[:, 2:3], gscr[:, 1:2]), reads=[gscr], writes=[gscr])
            kb.op("vector", lambda e: e.tensor_scalar(
                gscr[:, 16:24], gscr[:, 8:16], gscr[:, 2:3], None, ALU.mult),
                reads=[gscr], writes=[gscr])
            kb.op("vector", lambda e: e.tensor_tensor(
                gscr[:, 24:32], gscr[:, 16:24], gbt[:, 0:8], ALU.add),
                reads=[gscr, gbt], writes=[gscr])
            kb.op("vector", lambda e: e.tensor_scalar(
                gscr[:, 32:40], gscr[:, 24:32], gscr[:, 24:25], None, ALU.is_lt),
                reads=[gscr], writes=[gscr])
            for e2 in range(1, 8):
                kb.op("vector", lambda e, e2=e2: e.scalar_tensor_tensor(
                    gscr[:, 32:40], gscr[:, 24:32], gscr[:, 24 + e2:25 + e2],
                    gscr[:, 32:40], ALU.is_lt, ALU.add),
                    reads=[gscr], writes=[gscr])
            kb.op("vector", lambda e: e.scalar_tensor_tensor(
                rw9f[:, 0:8], gscr[:, 32:40], 1.5, gscr[:, 16:24],
                ALU.is_lt, ALU.mult),
                reads=[gscr], writes=[rw9f])
            kb.op("tensor", lambda e: e.transpose(
                psT[0:9, 0:128], rw9f[:, 0:9], id128[:, :]),
                reads=[rw9f, id128], writes=[psT])
            kb.op("vector", lambda e, ti=ti: e.tensor_copy(
                rwT9[0:9, 128 * ti:128 * (ti + 1)], psT[0:9, 0:128]),
                reads=[psT], writes=[rwT9])

        w1r = wmap["m_w1"]
        w3r = wmap["m_w3"]
        w2r = wmap["m_w2"]
        wdep = wmap.get("deps", {})
        for ex in range(9):
            kb.dma("sync", w1b.ap_all,
                   w1r[1024 * ex:1024 * (ex + 1), :].rearrange("(c p) h -> p c h", p=128),
                   reads=wdep.get(("m_w1", ex), ()), writes=[w1b])
            kb.dma("sync", w3b.ap_all,
                   w3r[1024 * ex:1024 * (ex + 1), :].rearrange("(c p) h -> p c h", p=128),
                   reads=wdep.get(("m_w3", ex), ()), writes=[w3b])
            kb.dma("sync", w2b.ap_all,
                   w2r[HE * ex:HE * (ex + 1), :].rearrange("(c p) d -> p c d", p=128),
                   reads=wdep.get(("m_w2", ex), ()), writes=[w2b])
            for qt in range(4):
                toff = 512 * qt
                hs = hst[0]
                kb.op("tensor", lambda e, ex=ex, toff=toff: e.matmul(
                    psB[:, :], esel[0:9, 128 * ex:128 * (ex + 1)],
                    rwT9[0:9, toff:toff + 512], start=True, stop=True),
                    reads=[esel, rwT9], writes=[psB])
                kb.op("vector", lambda e: e.tensor_copy(
                    bcsb[:, :], psB[:, :]), reads=[psB], writes=[bcsb])
                for hc in range(NHC):
                    pa = self.ps[hc % 2]
                    pc = self.ps[2 + hc % 2]
                    cb = [cbuf, cbuf2][hc % 2]
                    ub = [ubuf, ubuf2][hc % 2]
                    sbb = [ssb, ssb2][hc % 2]
                    for kc in range(DC):
                        kb.op("tensor", lambda e, hc=hc, kc=kc, toff=toff, pa=pa: e.matmul(
                            pa[:, :], w1b[:, HE * kc + 128 * hc: HE * kc + 128 * hc + 128],
                            self.xnc[kc][:, toff:toff + 512],
                            start=(kc == 0), stop=(kc == DC - 1)),
                            reads=[w1b, self.xnc[kc]], writes=[pa])
                        kb.op("tensor", lambda e, hc=hc, kc=kc, toff=toff, pc=pc: e.matmul(
                            pc[:, :], w3b[:, HE * kc + 128 * hc: HE * kc + 128 * hc + 128],
                            self.xnc[kc][:, toff:toff + 512],
                            start=(kc == 0), stop=(kc == DC - 1)),
                            reads=[w3b, self.xnc[kc]], writes=[pc])
                    bcol1 = 11 * ex + hc
                    kb.op("scalar", lambda e, bcol1=bcol1, pc=pc, cb=cb: e.activation(
                        cb[:, 0:512], pc[:, :], AF.Identity,
                        bias=b3c[:, bcol1:bcol1 + 1]),
                        reads=[pc, b3c], writes=[cb])
                    kb.op("vector", lambda e, bcol1=bcol1, pa=pa, cb=cb, ub=ub: e.scalar_tensor_tensor(
                        ub[:, 0:512], pa[:, :], b1c[:, bcol1:bcol1 + 1],
                        cb[:, 0:512], ALU.add, ALU.mult),
                        reads=[pa, b1c, cb], writes=[ub])
                    kb.op("scalar", lambda e, ub=ub, sbb=sbb: e.activation(
                        sbb[:, 0:512], ub[:, 0:512], AF.Silu),
                        reads=[ub], writes=[sbb])
                    kb.op("gpsimd", lambda e, hc=hc, sbb=sbb: e.tensor_tensor(
                        hs[:, 512 * hc:512 * (hc + 1)], sbb[:, 0:512], bcsb[:, :],
                        ALU.mult),
                        reads=[sbb, bcsb], writes=[hs])
                for dc in range(DC):
                    ps = self.ps[4 + dc % 3]
                    for kc in range(NHC):
                        kb.op("tensor", lambda e, kc=kc, dc=dc, ps=ps: e.matmul(
                            ps[:, :],
                            w2b[:, 1024 * kc + 128 * dc: 1024 * kc + 128 * dc + 128],
                            hs[:, 512 * kc:512 * (kc + 1)],
                            start=(kc == 0), stop=(kc == NHC - 1)),
                            reads=[w2b, hs], writes=[ps])
                    Xc = self.Xc[dc]
                    kb.op("vector", lambda e, ps=ps, toff=toff, Xc=Xc: e.scalar_tensor_tensor(
                        Xc[:, toff:toff + 512], ps[:, :], 0.5,
                        Xc[:, toff:toff + 512], ALU.mult, ALU.add),
                        reads=[ps, Xc], writes=[Xc])
        for qt in range(4):
            toff = 512 * qt
            for dc in range(DC):
                ps = self.ps[4 + dc % 3]
                kb.op("tensor", lambda e, dc=dc, toff=toff, ps=ps: e.matmul(
                    ps[:, :], b2s[0:9, 128 * dc:128 * (dc + 1)],
                    rwT9[0:9, toff:toff + 512], start=True, stop=True),
                    reads=[b2s, rwT9], writes=[ps])
                Xc = self.Xc[dc]
                kb.op("vector", lambda e, ps=ps, toff=toff, Xc=Xc: e.scalar_tensor_tensor(
                    Xc[:, toff:toff + 512], ps[:, :], 0.5,
                    Xc[:, toff:toff + 512], ALU.mult, ALU.add),
                    reads=[ps, Xc], writes=[Xc])
        return

    def attn1(self, wmap, cols):
        kb = self.kb
        bq, bk, bo, b1, b2, g1c, b1nc, g2c, b2nc, ones1 = (
            cols[k] for k in ("bq", "bk", "bo", "b1", "b2",
                              "g1", "b1n", "g2", "b2n", "ones1"))
        SC16 = 1.0 / 8.0
        wdep = wmap.get("deps", {})
        INV_SQRT_S = 1.0 / math.sqrt(T)
        wqkv = wmap["a_wqkv"]
        Vb = self.arena_win("a1_V", 0, 16384)
        wvb = self.arena_win("a1_wvb", 32768, DC * 256)
        Qb = self.arena_win("a1_Q", 32768, T)
        Kb = self.arena_win("a1_K", 32768 + T, T)
        bvrow = Buf(self._arena.h, "a1_bvrow", win=(1, 36864, 1024))
        bvrow.after_bufs = [o for (o2, n2, o) in self._arena_owners
                            if o2 < 36864 + 1024 and 36864 < o2 + n2]
        self._arena_owners.append((36864, 1024, bvrow))
        kb.bufs.append(bvrow)
        kb.dma("gpsimd", bvrow.ap_all, cols["bvrow_d"], writes=[bvrow])
        Pst0 = self.arena_win("a1_P0", 36864, T)
        vt = [kb.sbuf(f"a1_vt{i}", [128, 64], BF16) for i in range(2)]
        dscr = [kb.sbuf("a1_dscr0", [128, 16], F32),
                kb.sbuf("a1_dscr1", [128, 16], F32)]
        nrm = kb.sbuf("a1_nrm", [1, 16], F32)
        ebias = kb.sbuf("a1_eb", [128, 1], F32)
        attnb = self.arena_win("a1_attn", 16384, DC * T)
        sqq = self.xsq[0]
        wqkvr = wqkv.rearrange("(c p) o -> p c o", p=128)

        for vh in range(4):
            kb.dma("sync", wvb.ap_all,
                   wqkvr[:, :, 2048 + 256 * vh:2048 + 256 * (vh + 1)],
                   reads=wdep.get("a_wqkv", ()), writes=[wvb])
            for tc in range(16):
                ps = self.ps[tc % 2]
                for kc in range(DC):
                    kb.op("tensor", lambda e, tc=tc, kc=kc, ps=ps: e.matmul(
                        ps[:, 0:256], self.xnc[kc][:, 128 * tc:128 * (tc + 1)],
                        wvb[:, 256 * kc:256 * (kc + 1)],
                        start=(kc == 0), stop=False),
                        reads=[self.xnc[kc], wvb], writes=[ps])
                kb.op("tensor", lambda e, vh=vh, ps=ps: e.matmul(
                    ps[:, 0:256], ones1[0:1, 0:128], bvrow[0:1, 256 * vh:256 * (vh + 1)],
                    start=False, stop=True),
                    reads=[ones1, bvrow], writes=[ps])
                kb.op("vector", lambda e, tc=tc, vh=vh, ps=ps: e.tensor_copy(
                    Vb[:, 1024 * tc + 256 * vh: 1024 * tc + 256 * (vh + 1)], ps[:, 0:256]),
                    reads=[ps], writes=[Vb])

        for hp in range(DC):
            for which, dstb, bcol in ((0, Qb, bq), (1, Kb, bk)):
                wb = self.arena_win(f"a1_wqk_{hp}_{which}", 38912, DC * 128)
                kb.dma("sync", wb.ap_all,
                       wqkvr[:, :, 1024 * which + 128 * hp:1024 * which + 128 * (hp + 1)],
                       reads=wdep.get("a_wqkv", ()), writes=[wb],
                       sem_group="a1wqk")
                for tt in range(4):
                    ps = self.ps[2 + which]
                    for kc in range(DC):
                        kb.op("tensor", lambda e, kc=kc, tt=tt, ps=ps, wb=wb: e.matmul(
                            ps[:, :], wb[:, 128 * kc:128 * (kc + 1)],
                            self.xnc[kc][:, 512 * tt:512 * (tt + 1)],
                            start=(kc == 0), stop=(kc == DC - 1)),
                            reads=[wb, self.xnc[kc]], writes=[ps])
                    kb.op("scalar", lambda e, tt=tt, ps=ps, dstb=dstb, bcol=bcol, hp=hp: e.activation(
                        dstb[:, 512 * tt:512 * (tt + 1)], ps[:, :], AF.Identity,
                        bias=bcol[:, hp:hp + 1]),
                        reads=[ps, bcol], writes=[dstb])
            ps6 = self.ps[6]
            for which, srcb in ((0, Qb), (1, Kb)):
                for tt in range(4):
                    kb.op("scalar", lambda e, tt=tt, srcb=srcb: e.activation(
                        sqq[:, :], srcb[:, 512 * tt:512 * (tt + 1)], AF.Square),
                        reads=[srcb], writes=[sqq])
                    kb.op("tensor", lambda e: e.matmul(
                        ps6[0:1, 0:512], self.lnsel[:, 0:1], sqq[:, :],
                        start=True, stop=True),
                        reads=[self.lnsel, sqq], writes=[ps6])
                    kb.op("vector", lambda e, tt=tt, which=which: e.reduce_max(
                        nrm[0:1, 4 * which + tt:4 * which + tt + 1],
                        ps6[0:1, 0:512], axis=AX.X),
                        reads=[ps6], writes=[nrm])
            kb.op("vector", lambda e: e.reduce_max(
                nrm[0:1, 8:9], nrm[0:1, 0:4], axis=AX.X), reads=[nrm], writes=[nrm])
            kb.op("vector", lambda e: e.reduce_max(
                nrm[0:1, 9:10], nrm[0:1, 4:8], axis=AX.X), reads=[nrm], writes=[nrm])
            kb.op("vector", lambda e: e.tensor_tensor(
                nrm[0:1, 10:11], nrm[0:1, 8:9], nrm[0:1, 9:10], ALU.add),
                reads=[nrm], writes=[nrm])
            kb.op("vector", lambda e: e.tensor_scalar(
                nrm[0:1, 11:12], nrm[0:1, 10:11], -SC16 / 2.0, None, ALU.mult),
                reads=[nrm], writes=[nrm])
            kb.op("tensor", lambda e: e.matmul(
                ps6[:, 0:1], self.bcsel[0:1, 0:128], nrm[0:1, 11:12],
                start=True, stop=True),
                reads=[self.bcsel, nrm], writes=[ps6])
            kb.op("vector", lambda e: e.tensor_copy(
                ebias[:, 0:1], ps6[:, 0:1]), reads=[ps6], writes=[ebias])
            Pst1 = self.arena_win(f"a1_P1_{hp}", 38912, T)
            Pst = [Pst0, Pst1]
            psPV = [self.ps[2 + i] for i in range(4)]
            for sc in range(16):
                for h2 in range(2):
                    dsc = dscr[sc % 2]
                    pp = Pst[h2]
                    for tt in range(4):
                        pS = self.ps[[0, 1, 6, 7][2 * h2 + tt % 2]]
                        kb.op("tensor", lambda e, h2=h2, sc=sc, tt=tt, pS=pS: e.matmul(
                            pS[:, :],
                            Kb[64 * h2:64 * (h2 + 1), 128 * sc:128 * (sc + 1)],
                            Qb[64 * h2:64 * (h2 + 1), 512 * tt:512 * (tt + 1)],
                            start=True, stop=True, tile_position=(64 * h2, 0)),
                            reads=[Kb, Qb], writes=[pS])
                        kb.op("scalar", lambda e, h2=h2, tt=tt, pS=pS, pp=pp, dsc=dsc: e.activation(
                            pp[:, 512 * tt:512 * (tt + 1)], pS[:, :], AF.Exp,
                            bias=ebias[:, 0:1], scale=SC16,
                            accum_out=dsc[:, 4 * h2 + tt:4 * h2 + tt + 1]),
                            reads=[pS, ebias], writes=[pp, dsc])
                    kb.op("vector", lambda e, h2=h2, dsc=dsc: e.reduce_sum(
                        dsc[:, 8 + h2:9 + h2], dsc[:, 4 * h2:4 * h2 + 4], axis=AX.X),
                        reads=[dsc], writes=[dsc])
                    kb.op("vector", lambda e, h2=h2, dsc=dsc: e.reciprocal(
                        dsc[:, 10 + h2:11 + h2], dsc[:, 8 + h2:9 + h2]),
                        reads=[dsc], writes=[dsc])
                    kb.op("vector", lambda e, h2=h2, sc=sc, hp=hp, dsc=dsc: e.tensor_scalar(
                        vt[h2][:, :], Vb[:, 1024 * sc + 128 * hp + 64 * h2:
                                         1024 * sc + 128 * hp + 64 * (h2 + 1)],
                        dsc[:, 10 + h2:11 + h2], None, ALU.mult),
                        reads=[Vb, dsc], writes=[vt[h2]])
                    for tt in range(4):
                        kb.op("tensor", lambda e, h2=h2, sc=sc, tt=tt, pp=pp: e.matmul(
                            psPV[tt][64 * h2:64 * (h2 + 1), :], vt[h2][:, :],
                            pp[:, 512 * tt:512 * (tt + 1)],
                            start=(sc == 0 and h2 == 0), stop=(sc == 15 and h2 == 1),
                            tile_position=(0, 64 * h2), skip_group_check=True),
                            reads=[vt[h2], pp], writes=[psPV[tt]])
            for tt in range(4):
                g = self.scr(tt % 2)
                kb.op("scalar", lambda e, tt=tt, g=g: e.activation(
                    g[:, 0:512], psPV[tt][:, :], AF.Gelu),
                    reads=[psPV[tt]], writes=[g])
                kb.op("vector", lambda e, tt=tt, hp=hp, g=g: e.tensor_scalar(
                    attnb[:, hp * T + 512 * tt: hp * T + 512 * (tt + 1)],
                    g[:, 0:512], INV_SQRT_S, None, ALU.mult),
                    reads=[g], writes=[attnb])
        s1b = self.arena_win("a1_s1", 0, DC * T, after=[Vb])
        wsl = [self.arena_win("a1_wsl0", 32768, DC * 256),
               self.arena_win("a1_wsl1", 34816, DC * 256),
               self.arena_win("a1_wsl2", 36864, DC * 256)]
        w2ch = [self.arena_win(f"a1_w2ch{i}", 38912 + 1024 * i, 1024)
                for i in range(2)]
        wo = wmap["a_wo"]
        wor = wo.rearrange("(c p) o -> p c o", p=128)
        for oc in range(DC):
            wb = wsl[oc % 3]
            kb.dma("sync", wb[:, 0:DC * 128],
                   wor[:, :, 128 * oc:128 * (oc + 1)],
                   reads=wdep.get("a_wo", ()), writes=[wb])
            for tt in range(4):
                ps = self.ps[4 + (tt % 2)]
                for ac in range(DC):
                    kb.op("tensor", lambda e, oc=oc, tt=tt, ac=ac, ps=ps, wb=wb: e.matmul(
                        ps[:, :], wb[:, 128 * ac:128 * (ac + 1)],
                        attnb[:, ac * T + 512 * tt: ac * T + 512 * (tt + 1)],
                        start=(ac == 0), stop=(ac == DC - 1)),
                        reads=[wb, attnb], writes=[ps])
                sl = slice(oc * T + 512 * tt, oc * T + 512 * (tt + 1))
                kb.op("vector", lambda e, ps=ps, sl=sl, oc=oc, tt=tt: e.scalar_tensor_tensor(
                    s1b[:, sl], ps[:, :], bo[:, oc:oc + 1],
                    self.xnc[oc][:, 512 * tt:512 * (tt + 1)],
                    ALU.add, ALU.add),
                    reads=[ps, bo, self.xnc[oc]], writes=[s1b])
        self.layernorm(dst=self.xn, src=s1b, gain_bias=(g1c, b1nc))
        h1b = self.arena_win("a1_h1", 16384, 32 * 512, after=[attnb])
        w1 = wmap["a_w1"]
        w2 = wmap["a_w2"]
        w1r = w1.rearrange("(c p) o -> p c o", p=128)
        for ti in range(NTT):
            toff = 512 * ti
            for sl_i in range(16):
                wb = wsl[sl_i % 3]
                kb.dma("sync", wb.ap_all,
                       w1r[:, :, 256 * sl_i:256 * (sl_i + 1)],
                       reads=wdep.get("a_w1", ()), writes=[wb])
                for hh in range(2):
                    hc = 2 * sl_i + hh
                    ps = self.ps[hc % 4]
                    for kc in range(DC):
                        kb.op("tensor", lambda e, hh=hh, kc=kc, ps=ps, wb=wb, toff=toff: e.matmul(
                            ps[:, :],
                            wb[:, 256 * kc + 128 * hh: 256 * kc + 128 * hh + 128],
                            self.xnc[kc][:, toff:toff + 512],
                            start=(kc == 0), stop=(kc == DC - 1)),
                            reads=[wb, self.xnc[kc]], writes=[ps])
                    kb.op("scalar", lambda e, hc=hc, ps=ps: e.activation(
                        h1b[:, 512 * hc:512 * (hc + 1)], ps[:, :], AF.Gelu,
                        bias=b1[:, hc:hc + 1]),
                        reads=[ps, b1], writes=[h1b])
            for kc in range(32):
                wc = w2ch[kc % 2]
                kb.dma("sync", wc.ap_all,
                       w2[128 * kc:128 * (kc + 1), :],
                       reads=wdep.get("a_w2", ()), writes=[wc])
                for dc in range(DC):
                    ps = self.ps[dc]
                    kb.op("tensor", lambda e, wc=wc, kc=kc, dc=dc, ps=ps: e.matmul(
                        ps[:, :], wc[:, 128 * dc:128 * (dc + 1)],
                        h1b[:, 512 * kc:512 * (kc + 1)],
                        start=(kc == 0), stop=(kc == 31)),
                        reads=[wc, h1b], writes=[ps])
            for dc in range(DC):
                ps = self.ps[dc]
                sl = slice(dc * T + toff, dc * T + toff + 512)
                kb.op("vector", lambda e, ps=ps, sl=sl, dc=dc, toff=toff: e.scalar_tensor_tensor(
                    s1b[:, sl], ps[:, :], b2[:, dc:dc + 1],
                    self.xnc[dc][:, toff:toff + 512],
                    ALU.add, ALU.add),
                    reads=[ps, b2, self.xnc[dc]], writes=[s1b])
        ln2out = self.arena_win("a1_ln2o", 16384, DC * T, after=[h1b])
        self.layernorm(dst=ln2out, src=s1b, gain_bias=(g2c, b2nc))
        for c in range(DC):
            Xc = self.Xc[c]
            kb.op("vector", lambda e, c=c, Xc=Xc: e.tensor_tensor(
                Xc[:, 0:T], Xc[:, 0:T],
                ln2out[:, c * T:(c + 1) * T], ALU.add),
                reads=[Xc, ln2out], writes=[Xc])
        return


    def attn0(self, wmap, cols):
        kb = self.kb
        lq1, lk1, lq2, lk2, rmsw, ones1 = (
            cols[k] for k in ("lq1", "lk1", "lq2", "lk2", "rmsw", "ones1"))
        SCL = 1.0 / math.sqrt(32.0)
        wdep = wmap.get("deps", {})
        LAMBDA_INIT = 0.8 - 0.6 * math.exp(-0.3 * 12)
        wq, wkd, wv, wo = (wmap[k] for k in ("l0_wq", "l0_wkd", "l0_wv", "l0_wo"))
        Vb = self.arena_win("a0_V", 0, 16 * 512)
        attnb = self.arena_win("a0_attn", 8192, DC * T)
        Qc = self.arena_win("a0_Qc", 24576, T)
        Kc = self.arena_win("a0_Kc", 26624, T)
        raw = self.arena_win("a0_raw", 28672, 512)
        shf = self.arena_win("a0_shf", 29184, 512)
        v65 = self.arena_win("a0_v65", 29696, 16 * 65)
        wqb = self.arena_win("a0_wqb", 31776, DC * 128)
        wkb = self.arena_win("a0_wkb", 32800, DC * 128)
        wvb = self.arena_win("a0_wvb", 33824, DC * 256)
        Pbp = [self.arena_win("a0_Pb0", 30752, 1024),
               self.arena_win("a0_Pb1", 33824, 1024, after=[wvb]),
               self.arena_win("a0_Pb2", 34848, 1024, after=[wvb])]
        wob = [self.arena_win("a0_wob0", 31776, DC * 128)]
        ropec = self.arena_win("a0_ropec", 35872, T)
        ropes = self.arena_win("a0_ropes", 37920, T)
        cmask = self.arena_win("a0_cmask", 39968, 896)
        kb.dma("sync", ropec.ap_all, cols["ropec_d"], writes=[ropec])
        kb.dma("sync", ropes.ap_all, cols["ropes_d"], writes=[ropes])
        kb.dma("sync", cmask.ap_all, cols["cmask_d"], writes=[cmask])
        oc2 = [[self.scr(2), self.scr(3)], [self.scr(0), self.scr(1)]]
        _dbh = self.scr(4).h.bitcast(BF16)
        dbr2 = []
        for h2 in range(2):
            b = Buf(_dbh, f"a0_dbr{h2}", win=(0, 1, 512 * h2, 512))
            kb.bufs.append(b)
            dbr2.append([b, b])
        sqq2 = [self.xsq[0], self.xbb[0]]
        lsc = self.buf("a0_lsc", [1, 64], F32)
        lcol = self.buf("a0_lcol", [128, 1], F32)
        ebias = self.buf("a0_eb", [128, 1], F32)
        nrm = self.buf("a0_nrm", [1, 16], F32)
        sqq = sqq2[0]
        wqr = wq.rearrange("(c p) o -> p c o", p=128)
        wkr = wkd.rearrange("(c p) o -> p c o", p=128)
        wvr = wv.rearrange("(c p) o -> p c o", p=128)
        wor = wo.rearrange("(c p) o -> p c o", p=128)

        kb.op("vector", lambda e: e.tensor_tensor(
            lsc[0:1, 0:32], lq1[0:1, 0:32], lk1[0:1, 0:32], ALU.mult),
            reads=[lq1, lk1], writes=[lsc])
        kb.op("vector", lambda e: e.tensor_tensor(
            lsc[0:1, 32:64], lq2[0:1, 0:32], lk2[0:1, 0:32], ALU.mult),
            reads=[lq2, lk2], writes=[lsc])
        kb.op("vector", lambda e: e.reduce_sum(
            lsc[0:1, 0:1], lsc[0:1, 0:32], axis=AX.X), reads=[lsc], writes=[lsc])
        kb.op("vector", lambda e: e.reduce_sum(
            lsc[0:1, 1:2], lsc[0:1, 32:64], axis=AX.X), reads=[lsc], writes=[lsc])
        kb.op("scalar", lambda e: e.activation(
            lsc[0:1, 2:4], lsc[0:1, 0:2], AF.Exp), reads=[lsc], writes=[lsc])
        kb.op("vector", lambda e: e.tensor_tensor(
            lsc[0:1, 4:5], lsc[0:1, 2:3], lsc[0:1, 3:4], ALU.subtract),
            reads=[lsc], writes=[lsc])
        kb.op("vector", lambda e: e.tensor_scalar(
            lsc[0:1, 5:6], lsc[0:1, 4:5], LAMBDA_INIT, -1.0, ALU.add, ALU.mult),
            reads=[lsc], writes=[lsc])
        psL = self.ps[6]
        kb.op("tensor", lambda e: e.matmul(
            psL[:, 0:1], self.bcsel[0:1, 0:128], lsc[0:1, 5:6],
            start=True, stop=True), reads=[self.bcsel, lsc], writes=[psL])
        kb.op("vector", lambda e: e.tensor_copy(
            lcol[:, 0:1], psL[:, 0:1]), reads=[psL], writes=[lcol])

        for vh in range(2):
            kb.dma("sync", wvb.ap_all, wvr[:, :, 256 * vh:256 * (vh + 1)],
                   reads=wdep.get("l0_wv", ()), writes=[wvb])
            for tc in range(16):
                ps = self.ps[tc % 2]
                for kc in range(DC):
                    kb.op("tensor", lambda e, tc=tc, kc=kc, ps=ps: e.matmul(
                        ps[:, 0:256],
                        self.xnc[kc][:, 128 * tc:128 * (tc + 1)],
                        wvb[:, 256 * kc:256 * (kc + 1)],
                        start=(kc == 0), stop=(kc == DC - 1)),
                        reads=[self.xnc[kc], wvb], writes=[ps])
                kb.op("vector", lambda e, tc=tc, vh=vh, ps=ps: e.tensor_copy(
                    Vb[:, 512 * tc + 256 * vh:512 * tc + 256 * (vh + 1)],
                    ps[:, 0:256]),
                    reads=[ps], writes=[Vb])

        for hc in range(DC):
            for which, wsb, wsrc, dst in ((1, wkb, wkr, Kc), (0, wqb, wqr, Qc)):
                kb.dma("sync", wsb.ap_all,
                       wsrc[:, :, 128 * hc:128 * (hc + 1)],
                       reads=wdep.get("l0_wkd" if which else "l0_wq", ()),
                       writes=[wsb])
                for tt in range(4):
                    ps = self.ps[2 + which]
                    for kc in range(DC):
                        kb.op("tensor", lambda e, kc=kc, tt=tt, ps=ps, wsb=wsb: e.matmul(
                            ps[:, :], wsb[:, 128 * kc:128 * (kc + 1)],
                            self.xnc[kc][:, 512 * tt:512 * (tt + 1)],
                            start=(kc == 0), stop=(kc == DC - 1)),
                            reads=[wsb, self.xnc[kc]], writes=[ps])
                    tsl = slice(512 * tt, 512 * (tt + 1))
                    kb.op("scalar", lambda e, ps=ps: e.copy(
                        raw[:, 0:512], ps[:, :]), reads=[ps], writes=[raw])
                    mask = list(range(16, 32)) + list(range(0, 16))
                    kb.op("vector", lambda e, mask=mask: e.stream_shuffle(
                        shf[:, 0:512], raw[:, 0:512], mask), reads=[raw], writes=[shf])
                    kb.op("gpsimd", lambda e, tsl=tsl: e.tensor_tensor(
                        raw[:, 0:512], raw[:, 0:512], ropec[:, tsl], ALU.mult),
                        reads=[raw, ropec], writes=[raw])
                    kb.op("gpsimd", lambda e, tsl=tsl: e.tensor_tensor(
                        shf[:, 0:512], shf[:, 0:512], ropes[:, tsl], ALU.mult),
                        reads=[shf, ropes], writes=[shf])
                    kb.op("vector", lambda e, tsl=tsl, dst=dst: e.tensor_tensor(
                        dst[:, tsl], raw[:, 0:512], shf[:, 0:512], ALU.add),
                        reads=[raw, shf], writes=[dst])
            ps6 = self.ps[6]
            for which, srcb in ((0, Qc), (1, Kc)):
                for tt in range(4):
                    kb.op("scalar", lambda e, tt=tt, srcb=srcb: e.activation(
                        sqq[:, :], srcb[:, 512 * tt:512 * (tt + 1)], AF.Square),
                        reads=[srcb], writes=[sqq])
                    kb.op("tensor", lambda e: e.matmul(
                        ps6[0:1, 0:512], self.lnsel[:, 0:1], sqq[:, :],
                        start=True, stop=True),
                        reads=[self.lnsel, sqq], writes=[ps6])
                    kb.op("vector", lambda e, tt=tt, which=which: e.reduce_max(
                        nrm[0:1, 4 * which + tt:4 * which + tt + 1],
                        ps6[0:1, 0:512], axis=AX.X),
                        reads=[ps6], writes=[nrm])
            kb.op("vector", lambda e: e.reduce_max(
                nrm[0:1, 8:9], nrm[0:1, 0:4], axis=AX.X), reads=[nrm], writes=[nrm])
            kb.op("vector", lambda e: e.reduce_max(
                nrm[0:1, 9:10], nrm[0:1, 4:8], axis=AX.X), reads=[nrm], writes=[nrm])
            kb.op("vector", lambda e: e.tensor_tensor(
                nrm[0:1, 10:11], nrm[0:1, 8:9], nrm[0:1, 9:10], ALU.add),
                reads=[nrm], writes=[nrm])
            kb.op("vector", lambda e: e.tensor_scalar(
                nrm[0:1, 11:12], nrm[0:1, 10:11], -SCL / 2.0, None, ALU.mult),
                reads=[nrm], writes=[nrm])
            kb.op("tensor", lambda e: e.matmul(
                ps6[:, 0:1], self.bcsel[0:1, 0:128], nrm[0:1, 11:12],
                start=True, stop=True), reads=[self.bcsel, nrm], writes=[ps6])
            kb.op("vector", lambda e: e.tensor_copy(
                ebias[:, 0:1], ps6[:, 0:1]), reads=[ps6], writes=[ebias])
            for sc in range(16):
                kb.op("vector", lambda e, sc=sc, hc=hc: e.tensor_copy(
                    v65[:, 65 * sc:65 * sc + 64],
                    Vb[:, 512 * sc + 64 * hc: 512 * sc + 64 * (hc + 1)]),
                    reads=[Vb], writes=[v65])
                kb.op("vector", lambda e, sc=sc: e.memset(
                    v65[:, 65 * sc + 64:65 * (sc + 1)], 1.0), writes=[v65])
            psS = [self.ps[i] for i in range(4)]
            psPV = [self.ps[4 + i] for i in range(4)]
            for tt in range(4):
                nsc = 4 * (tt + 1)
                for sc in range(nsc):
                    for u in range(4):
                        pS = psS[u]
                        kb.op("tensor", lambda e, sc=sc, tt=tt, u=u, pS=pS: e.matmul(
                            pS[:, :],
                            Kc[32 * u:32 * (u + 1), 128 * sc:128 * (sc + 1)],
                            Qc[32 * u:32 * (u + 1), 512 * tt:512 * (tt + 1)],
                            start=True, stop=True, tile_position=(32 * u, 0)),
                            reads=[Kc, Qc], writes=[pS])
                    diag = 512 * tt - 128 * sc
                    for j in range(2):
                        pbp = Pbp[(2 * sc + j) % 3]
                        kb.op("scalar", lambda e, j=j, pbp=pbp: e.activation(
                            pbp[:, 0:1024], self.psd[j][:, 0:1024], AF.Exp,
                            bias=ebias[:, 0:1], scale=SCL),
                            reads=[psS[2 * j], psS[2 * j + 1], ebias], writes=[pbp])
                        for i, u in ((0, 2 * j), (1, 2 * j + 1)):
                            if diag < 127:
                                kb.op("gpsimd", lambda e, pbp=pbp, i=i, diag=diag:
                                    e.tensor_tensor(
                                        pbp[:, 512 * i:512 * (i + 1)],
                                        pbp[:, 512 * i:512 * (i + 1)],
                                        cmask[:, diag + 384:diag + 384 + 512],
                                        ALU.mult),
                                    reads=[pbp, cmask], writes=[pbp])
                            kb.op("tensor", lambda e, u=u, i=i, sc=sc, nsc=nsc, pbp=pbp:
                                e.matmul(
                                    psPV[u][0:65, :], v65[:, 65 * sc:65 * (sc + 1)],
                                    pbp[:, 512 * i:512 * (i + 1)],
                                    start=(sc == 0), stop=(sc == nsc - 1)),
                                reads=[v65, pbp], writes=[psPV[u]])
                chains = []
                for h2 in range(2):
                    u0, u1 = 2 * h2, 2 * h2 + 1
                    bc0, bc1 = psPV[u0], psPV[u1]
                    oc = oc2[h2]
                    sqh = sqq2[h2]
                    dbr = dbr2[h2]
                    ch = []
                    for i, u, bc in ((0, u0, bc0), (1, u1, bc1)):
                        def _rcp(e, i=i, u=u, dbr=dbr):
                            with e.bass.allow_low_precision(reason="1/d bf16 ok"):
                                return e.reciprocal(
                                    dbr[i][0:1, 0:512], psPV[u][64:65, :])
                        ch.append(("vector", _rcp, [psPV[u]], [dbr[i]]))
                        ch.append(("scalar", lambda e, u=u, oci=oc[i]: e.copy(
                            oci[0:64, 0:512], psPV[u][0:64, :]),
                            [psPV[u]], [oc[i]]))
                        ch.append(("tensor", lambda e, bc=bc, db=dbr[i]: e.matmul(
                            bc[:, :], ones1[0:1, 0:128], db[0:1, 0:512],
                            start=True, stop=True),
                            [ones1, dbr[i]], [bc]))
                    ch.append(("vector", lambda e, oc0=oc[0], bc0=bc0:
                        e.tensor_tensor(
                            oc0[0:64, 0:512], oc0[0:64, 0:512], bc0[0:64, :],
                            ALU.mult),
                        [oc[0], bc0], [oc[0]]))
                    ch.append(("vector", lambda e, oc1=oc[1], bc1=bc1:
                        e.tensor_tensor(
                            oc1[0:64, 0:512], oc1[0:64, 0:512], bc1[0:64, :],
                            ALU.mult),
                        [oc[1], bc1], [oc[1]]))
                    ch.append(("vector", lambda e, oc0=oc[0], oc1=oc[1]:
                        e.scalar_tensor_tensor(
                            oc0[0:64, 0:512], oc1[0:64, 0:512], lcol[0:64, 0:1],
                            oc0[0:64, 0:512], ALU.mult, ALU.add),
                        [oc[1], lcol, oc[0]], [oc[0]]))
                    ch.append(("scalar", lambda e, sqh=sqh, oc0=oc[0]: e.activation(
                        sqh[0:64, :], oc0[0:64, 0:512], AF.Square),
                        [oc[0]], [sqh]))
                    ch.append(("tensor", lambda e, bc0=bc0, sqh=sqh: e.matmul(
                        bc0[0:1, :], self.lnsel[0:64, 0:1], sqh[0:64, :],
                        start=True, stop=True),
                        [self.lnsel, sqh], [bc0]))
                    epsc = self.const_col(1e-6)
                    ch.append(("scalar", lambda e, bc0=bc0, epsc=epsc: e.activation(
                        bc0[0:1, :], bc0[0:1, :], AF.Ln,
                        bias=epsc[0:1, 0:1], scale=1.0 / 64.0),
                        [bc0, self.ccol], [bc0]))
                    ch.append(("scalar", lambda e, bc0=bc0, db=dbr[0]: e.activation(
                        db[0:1, 0:512], bc0[0:1, :], AF.Exp, scale=-0.5),
                        [bc0], [dbr[0]]))
                    ch.append(("tensor", lambda e, bc1=bc1, db=dbr[0]: e.matmul(
                        bc1[:, :], ones1[0:1, 0:128], db[0:1, 0:512],
                        start=True, stop=True),
                        [ones1, dbr[0]], [bc1]))
                    ch.append(("vector", lambda e, oc0=oc[0], bc1=bc1:
                        e.tensor_tensor(
                            oc0[0:64, 0:512], oc0[0:64, 0:512], bc1[0:64, :],
                            ALU.mult),
                        [oc[0], bc1], [oc[0]]))
                    osl = slice(hc * T + 512 * tt, hc * T + 512 * (tt + 1))
                    ch.append(("vector", lambda e, h2=h2, osl=osl, oc0=oc[0]:
                        e.tensor_scalar(
                            attnb[64 * h2:64 * (h2 + 1), osl], oc0[0:64, 0:512],
                            rmsw[0:64, 0:1], 1.0 - LAMBDA_INIT, ALU.mult, ALU.mult),
                        [oc[0], rmsw], [attnb]))
                    chains.append(ch)
                for pair in itertools.zip_longest(*chains):
                    for item in pair:
                        if item is not None:
                            eng, emit, rd, wr = item
                            kb.op(eng, emit, reads=rd, writes=wr)
        for oc2 in range(DC):
            wb = wob[0]
            kb.dma("sync", wb.ap_all,
                   wor[:, :, 128 * oc2:128 * (oc2 + 1)],
                   reads=wdep.get("l0_wo", ()), writes=[wb])
            for tt in range(4):
                ps = self.ps[4 + (tt % 2)]
                for ac in range(DC):
                    kb.op("tensor", lambda e, tt=tt, ac=ac, ps=ps, wb=wb: e.matmul(
                        ps[:, :], wb[:, 128 * ac:128 * (ac + 1)],
                        attnb[:, ac * T + 512 * tt: ac * T + 512 * (tt + 1)],
                        start=(ac == 0), stop=(ac == DC - 1)),
                        reads=[wb, attnb], writes=[ps])
                Xc = self.Xc[oc2]
                kb.op("vector", lambda e, ps=ps, tt=tt, Xc=Xc: e.tensor_tensor(
                    Xc[:, 512 * tt:512 * (tt + 1)], Xc[:, 512 * tt:512 * (tt + 1)],
                    ps[:, :], ALU.add),
                    reads=[ps, Xc], writes=[Xc])
        return



def build_ffn1_program():
    kb = KB()
    xT = kb.dram_in("xT", [D, T])
    w1 = kb.dram_in("f1_w1", [1024, 4096])
    w2 = kb.dram_in("f1_w2", [4096, 1024])
    b1cd = kb.dram_in("f1_b1c", [128, 32])
    b2cd = kb.dram_in("f1_b2c", [128, 8])
    clnsel = kb.dram_in("c_lnsel", [128, 16 * NTT], BF16)
    cbcsel = kb.dram_in("c_bcsel", [16, 128 * NTT])
    yT = kb.dram_out("yT", [D, T])

    m = Model(kb)
    b1c = kb.sbuf("b1c", [128, 32], F32)
    b2c = kb.sbuf("b2c", [128, 8], F32)
    _xTr = xT.rearrange("(c p) t -> p c t", p=128)
    for _c in range(DC):
        kb.dma("sync", m.Xc[_c][:, 0:T], _xTr[:, _c, :], writes=[m.Xc[_c]])
    kb.dma("sync", m.lnsel.ap_all, clnsel, writes=[m.lnsel])
    kb.dma("sync", m.bcsel.ap_all, cbcsel, writes=[m.bcsel])
    kb.dma("sync", b1c.ap_all, b1cd, writes=[b1c])
    kb.dma("sync", b2c.ap_all, b2cd, writes=[b2c])
    m.layernorm()
    m.ffn1({"w1": w1, "w2": w2}, b1c, b2c)
    yTr = yT.rearrange("(c p) t -> p c t", p=128)
    for c in range(DC):
        kb.dma("sync", yTr[:, c, :], m.Xc[c][:, 0:T], reads=[m.Xc[c]])
    kb.finalize()
    return kb


def prep_ffn1_inputs(w1, b1, w2, b2):
    return {
        "f1_w1": np.ascontiguousarray(w1, dtype=np.float32),
        "f1_w2": np.ascontiguousarray(w2, dtype=np.float32),
        "f1_b1c": np.ascontiguousarray(np.asarray(b1, np.float32).reshape(32, 128).T),
        "f1_b2c": np.ascontiguousarray(np.asarray(b2, np.float32).reshape(8, 128).T),
    }

    def _pwl(self, psm, b1c, hc, hbuf, hoff):
        kb = self.kb
        nt, ncl, jj, tt_ = self.scr(0), self.scr(1), self.scr(2), self.scr(3)
        g0, g1 = self.scr(4), self.scr(2)
        dd = self.scr(1)
        yy = self.scr(3)
        msk = self.mscr
        p0, p1 = psm
        kb.op("vector", lambda e: e.tensor_scalar(
            nt[:, 0:512], p0.h[:, :], b1c[:, hc:hc + 1], 256.0, ALU.add, ALU.mult),
            reads=[p0, b1c], writes=[nt])
        kb.op("vector", lambda e: e.tensor_scalar(
            nt[:, 512:1024], p1.h[:, :], b1c[:, hc:hc + 1], 256.0, ALU.add, ALU.mult),
            reads=[p1, b1c], writes=[nt])
        kb.op("vector", lambda e: e.tensor_scalar(
            nt[:, :], nt[:, :], MAGIC, MAGIC, ALU.add, ALU.subtract),
            reads=[nt], writes=[nt])
        kb.op("vector", lambda e: e.tensor_scalar(
            ncl[:, :], nt[:, :], -1024.0, 1024.0, ALU.max, ALU.min),
            reads=[nt], writes=[ncl])
        kb.op("vector", lambda e: e.tensor_scalar(
            ncl[:, :], ncl[:, :], 1.0 / 64.0, 63.0 / 128.0, ALU.mult, ALU.subtract),
            reads=[ncl], writes=[ncl])
        kb.op("vector", lambda e: e.tensor_scalar(
            jj[:, :], ncl[:, :], MAGIC, MAGIC, ALU.add, ALU.subtract),
            reads=[ncl], writes=[jj])
        kb.op("vector", lambda e: e.tensor_scalar(
            jj[:, :], jj[:, :], 15.0, None, ALU.min),
            reads=[jj], writes=[jj])
        kb.op("vector", lambda e: e.tensor_scalar(
            ncl[:, :], ncl[:, :], 63.0 / 128.0, None, ALU.add),
            reads=[ncl], writes=[ncl])
        kb.op("gpsimd", lambda e: e.tensor_tensor(
            tt_[:, :], ncl[:, :], jj[:, :], ALU.subtract),
            reads=[ncl, jj], writes=[tt_])
        kb.op("scalar", lambda e: e.activation(
            g0[:, :], jj[:, :], AF.Gelu, bias=0.0, scale=0.25),
            reads=[jj], writes=[g0])
        kb.op("scalar", lambda e: e.activation(
            g1[:, :], jj[:, :], AF.Gelu, bias=0.25, scale=0.25),
            reads=[jj], writes=[g1])
        kb.op("gpsimd", lambda e: e.tensor_tensor(
            dd[:, :], g1[:, :], g0[:, :], ALU.subtract),
            reads=[g1, g0], writes=[dd])
        kb.op("gpsimd", lambda e: e.tensor_tensor(
            dd[:, :], tt_[:, :], dd[:, :], ALU.mult),
            reads=[tt_, dd], writes=[dd])
        kb.op("gpsimd", lambda e: e.tensor_tensor(
            yy[:, :], dd[:, :], g0[:, :], ALU.add),
            reads=[dd, g0], writes=[yy])
        kb.op("vector", lambda e: e.tensor_scalar(
            yy[:, :], yy[:, :], 256.0, MAGIC, ALU.mult, ALU.add),
            reads=[yy], writes=[yy])
        kb.op("vector", lambda e: e.tensor_scalar(
            yy[:, :], yy[:, :], MAGIC, None, ALU.subtract),
            reads=[yy], writes=[yy])
        kb.op("vector", lambda e: e.tensor_scalar(
            msk[:, :], nt[:, :], 1024.0, None, ALU.is_gt),
            reads=[nt], writes=[msk])
        kb.op("vector", lambda e: e.select(
            yy[:, :], msk[:, :], nt[:, :], yy[:, :]),
            reads=[msk, nt, yy], writes=[yy])
        kb.op("vector", lambda e: e.tensor_scalar(
            hbuf[:, hoff:hoff + 1024], yy[:, :], 1.0 / 256.0, None, ALU.mult),
            reads=[yy], writes=[hbuf])


def build_moe_program():
    kb = KB()
    xT = kb.dram_in("xT", [D, T])
    w1 = kb.dram_in("m_w1", [9 * 1024, 1408])
    w3 = kb.dram_in("m_w3", [9 * 1024, 1408])
    w2 = kb.dram_in("m_w2", [9 * 1408, 1024])
    b1cd = kb.dram_in("m_b1c", [128, 99])
    b3cd = kb.dram_in("m_b3c", [128, 99])
    b2sd = kb.dram_in("m_b2s", [9, 1024])
    gwtd = kb.dram_in("g_gwt", [128, 64])
    gbtd = kb.dram_in("g_gbt", [128, 8])
    id128d = kb.dram_in("c_id128", [128, 128])
    eseld = kb.dram_in("c_esel", [9, 9 * 128], BF16)
    clnsel = kb.dram_in("c_lnsel", [128, 16 * NTT], BF16)
    cbcsel = kb.dram_in("c_bcsel", [16, 128 * NTT])
    yT = kb.dram_out("yT", [D, T])

    m = Model(kb)
    b1c = kb.sbuf("b1c", [128, 99], F32)
    b3c = kb.sbuf("b3c", [128, 99], F32)
    b2s = kb.sbuf("b2s", [9, 1024], BF16)
    gwt = kb.sbuf("gwt", [128, 64], BF16)
    gbt = kb.sbuf("gbt", [128, 8], F32)
    id128 = kb.sbuf("id128", [128, 128], F32)
    esel = kb.sbuf("esel", [9, 9 * 128], BF16)
    _xTr = xT.rearrange("(c p) t -> p c t", p=128)
    for _c in range(DC):
        kb.dma("sync", m.Xc[_c][:, 0:T], _xTr[:, _c, :], writes=[m.Xc[_c]])
    kb.dma("sync", m.lnsel.ap_all, clnsel, writes=[m.lnsel])
    kb.dma("sync", m.bcsel.ap_all, cbcsel, writes=[m.bcsel])
    kb.dma("sync", b1c.ap_all, b1cd, writes=[b1c])
    kb.dma("sync", b3c.ap_all, b3cd, writes=[b3c])
    kb.dma("gpsimd", b2s.ap_all, b2sd, writes=[b2s])
    kb.dma("gpsimd", gwt.ap_all, gwtd, writes=[gwt])
    kb.dma("sync", gbt.ap_all, gbtd, writes=[gbt])
    kb.dma("sync", id128.ap_all, id128d, writes=[id128])
    kb.dma("sync", esel.ap_all, eseld, writes=[esel])
    m.layernorm()
    m.moe({"m_w1": w1, "m_w3": w3, "m_w2": w2}, b1c, b3c,
          {"gwt": gwt, "gbt": gbt, "b2s": b2s, "id128": id128, "esel": esel})
    yTr = yT.rearrange("(c p) t -> p c t", p=128)
    for c in range(DC):
        kb.dma("sync", yTr[:, c, :], m.Xc[c][:, 0:T], reads=[m.Xc[c]])
    kb.finalize()
    return kb


def _esel_const():
    import ml_dtypes
    esel = np.zeros((9, 9 * 128), dtype=ml_dtypes.bfloat16)
    for e in range(9):
        esel[e, 128 * e:128 * (e + 1)] = 1.0
    return esel


def prep_moe_inputs(gate_w, gate_b, w1, b1, w3, b3, w2, b2, sw1, sb1, sw3, sb3, sw2, sb2):
    f = np.float32
    w1a = np.concatenate([np.asarray(w1, f).reshape(8 * 1024, 1408),
                          np.asarray(sw1, f)], axis=0)
    w3a = np.concatenate([np.asarray(w3, f).reshape(8 * 1024, 1408),
                          np.asarray(sw3, f)], axis=0)
    w2a = np.concatenate([np.asarray(w2, f).reshape(8 * 1408, 1024),
                          np.asarray(sw2, f)], axis=0)
    b1a = np.concatenate([np.asarray(b1, f), np.asarray(sb1, f)[None]], axis=0)
    b3a = np.concatenate([np.asarray(b3, f), np.asarray(sb3, f)[None]], axis=0)
    b2a = np.concatenate([np.asarray(b2, f), np.asarray(sb2, f)[None]], axis=0)
    b1c = np.concatenate([b1a[e].reshape(11, 128).T for e in range(9)], axis=1)
    b3c = np.concatenate([b3a[e].reshape(11, 128).T for e in range(9)], axis=1)
    gw = np.asarray(gate_w, f)
    gwt = np.zeros((128, 64), f)
    for kc in range(8):
        gwt[:, 8 * kc:8 * (kc + 1)] = gw[:, 128 * kc:128 * (kc + 1)].T
    gbt = np.broadcast_to(np.asarray(gate_b, f)[None, :], (128, 8)).copy()
    return {
        "m_w1": np.ascontiguousarray(w1a), "m_w3": np.ascontiguousarray(w3a),
        "m_w2": np.ascontiguousarray(w2a),
        "m_b1c": np.ascontiguousarray(b1c), "m_b3c": np.ascontiguousarray(b3c),
        "m_b2s": np.ascontiguousarray(b2a),
        "g_gwt": gwt, "g_gbt": gbt,
        "c_id128": np.eye(128, dtype=f),
        "c_esel": _esel_const(),
    }


def build_attn1_program():
    kb = KB()
    xT = kb.dram_in("xT", [D, T])
    wqkv = kb.dram_in("a_wqkv", [1024, 3072])
    wo = kb.dram_in("a_wo", [1024, 1024])
    w1 = kb.dram_in("a_w1", [1024, 4096])
    w2 = kb.dram_in("a_w2", [4096, 1024])
    colsd = {}
    for nm, sh in (("a_bq", [128, 8]), ("a_bk", [128, 8]), ("a_bo", [128, 8]),
                   ("a_b1", [128, 32]), ("a_b2", [128, 8]),
                   ("a_g1", [128, 8]), ("a_b1n", [128, 8]),
                   ("a_g2", [128, 8]), ("a_b2n", [128, 8])):
        colsd[nm] = kb.dram_in(nm, sh)
    bvrowd = kb.dram_in("a_bvrow", [1, 1024])
    ones1d = kb.dram_in("c_ones1", [1, 128])
    clnsel = kb.dram_in("c_lnsel", [128, 16 * NTT], BF16)
    cbcsel = kb.dram_in("c_bcsel", [16, 128 * NTT])
    yT = kb.dram_out("yT", [D, T])

    m = Model(kb)
    cols = {}
    for nm, key in (("a_bq", "bq"), ("a_bk", "bk"), ("a_bo", "bo"),
                    ("a_b1", "b1"), ("a_b2", "b2"), ("a_g1", "g1"),
                    ("a_b1n", "b1n"), ("a_g2", "g2"), ("a_b2n", "b2n")):
        b = kb.sbuf("sb_" + nm, list(colsd[nm].shape), F32)
        kb.dma("sync", b.ap_all, colsd[nm], writes=[b])
        cols[key] = b
    cols["bvrow_d"] = bvrowd
    ones1 = kb.sbuf("sb_ones1", [1, 128], BF16)
    kb.dma("gpsimd", ones1.ap_all, ones1d, writes=[ones1])
    cols["ones1"] = ones1
    _xTr = xT.rearrange("(c p) t -> p c t", p=128)
    for _c in range(DC):
        kb.dma("sync", m.Xc[_c][:, 0:T], _xTr[:, _c, :], writes=[m.Xc[_c]])
    kb.dma("sync", m.lnsel.ap_all, clnsel, writes=[m.lnsel])
    kb.dma("sync", m.bcsel.ap_all, cbcsel, writes=[m.bcsel])
    m.layernorm()
    m.attn1({"a_wqkv": wqkv, "a_wo": wo, "a_w1": w1, "a_w2": w2}, cols)
    yTr = yT.rearrange("(c p) t -> p c t", p=128)
    for c in range(DC):
        kb.dma("sync", yTr[:, c, :], m.Xc[c][:, 0:T], reads=[m.Xc[c]])
    kb.finalize()
    return kb


def prep_attn1_inputs(in_proj_w, in_proj_b, out_w, out_b, lin1_w, lin1_b,
                      lin2_w, lin2_b, ln1_g, ln1_b, ln2_g, ln2_b):
    f = np.float32
    import ml_dtypes
    col = lambda v, n: np.ascontiguousarray(np.asarray(v, f).reshape(n, 128).T)
    return {
        "a_wqkv": np.ascontiguousarray(np.asarray(in_proj_w, f).T),
        "a_wo": np.ascontiguousarray(np.asarray(out_w, f).T),
        "a_w1": np.ascontiguousarray(np.asarray(lin1_w, f).T),
        "a_w2": np.ascontiguousarray(np.asarray(lin2_w, f).T),
        "a_bq": col(np.asarray(in_proj_b, f)[0:1024], 8),
        "a_bk": col(np.asarray(in_proj_b, f)[1024:2048], 8),
        "a_bvrow": np.asarray(in_proj_b, f)[None, 2048:3072].copy(),
        "a_bo": col(out_b, 8),
        "a_b1": col(lin1_b, 32),
        "a_b2": col(lin2_b, 8),
        "a_g1": col(ln1_g, 8), "a_b1n": col(ln1_b_ := ln1_b, 8),
        "a_g2": col(ln2_g, 8), "a_b2n": col(ln2_b, 8),
        "c_ones1": np.ones((1, 128), f),
    }


def build_attn0_program():
    kb = KB()
    xT = kb.dram_in("xT", [D, T])
    wq = kb.dram_in("l0_wq", [1024, 1024])
    wkd = kb.dram_in("l0_wkd", [1024, 1024])
    wv = kb.dram_in("l0_wv", [1024, 512])
    wo = kb.dram_in("l0_wo", [1024, 1024])
    lqd = {nm: kb.dram_in(nm, [1, 32]) for nm in ("l0_lq1", "l0_lk1", "l0_lq2", "l0_lk2")}
    rmswd = kb.dram_in("l0_rmsw", [128, 1])
    ropecd = kb.dram_in("c_ropec", [128, T], BF16)
    ropesd = kb.dram_in("c_ropes", [128, T], BF16)
    cmaskd = kb.dram_in("c_cmask", [128, 896], BF16)
    ones1d = kb.dram_in("c_ones1", [1, 128])
    clnsel = kb.dram_in("c_lnsel", [128, 16 * NTT], BF16)
    cbcsel = kb.dram_in("c_bcsel", [16, 128 * NTT])
    yT = kb.dram_out("yT", [D, T])

    m = Model(kb)
    cols = {}
    for nm, key in (("l0_lq1", "lq1"), ("l0_lk1", "lk1"),
                    ("l0_lq2", "lq2"), ("l0_lk2", "lk2")):
        b = kb.sbuf("sb_" + nm, [1, 32], F32)
        kb.dma("sync", b.ap_all, lqd[nm], writes=[b])
        cols[key] = b
    rmsw = kb.sbuf("sb_rmsw", [128, 1], F32)
    kb.dma("sync", rmsw.ap_all, rmswd, writes=[rmsw])
    cols["rmsw"] = rmsw
    cols["ropec_d"] = ropecd
    cols["ropes_d"] = ropesd
    cols["cmask_d"] = cmaskd
    ones1 = kb.sbuf("sb_ones1", [1, 128], BF16)
    kb.dma("gpsimd", ones1.ap_all, ones1d, writes=[ones1])
    cols["ones1"] = ones1
    _xTr = xT.rearrange("(c p) t -> p c t", p=128)
    for _c in range(DC):
        kb.dma("sync", m.Xc[_c][:, 0:T], _xTr[:, _c, :], writes=[m.Xc[_c]])
    kb.dma("sync", m.lnsel.ap_all, clnsel, writes=[m.lnsel])
    kb.dma("sync", m.bcsel.ap_all, cbcsel, writes=[m.bcsel])
    m.layernorm()
    m.attn0({"l0_wq": wq, "l0_wkd": wkd, "l0_wv": wv, "l0_wo": wo}, cols)
    yTr = yT.rearrange("(c p) t -> p c t", p=128)
    for c in range(DC):
        kb.dma("sync", yTr[:, c, :], m.Xc[c][:, 0:T], reads=[m.Xc[c]])
    kb.finalize()
    return kb


def rope_consts():
    import ml_dtypes
    bf = ml_dtypes.bfloat16
    HD = 32
    inv = 1.0 / 10000.0 ** (np.arange(0, HD, 2, dtype=np.float32) / HD)
    t = np.arange(T, dtype=np.float32)
    fr = np.outer(t, inv)
    cosv, sinv = np.cos(fr), np.sin(fr)
    C = np.zeros((128, T), np.float32)
    S = np.zeros((128, T), np.float32)
    for p in range(128):
        C[p] = cosv[:, p % 16]
        S[p] = sinv[:, p % 16] * (1.0 if (p % 32) < 16 else -1.0)
    cm = np.zeros((128, 896), np.float32)
    for p in range(128):
        cm[p] = (p <= np.arange(896) - 384)
    return {"c_ropec": C.astype(bf), "c_ropes": S.astype(bf),
            "c_cmask": cm.astype(bf)}


def prep_attn0_inputs(Wq, Wk, Wv, Wo, lq1, lk1, lq2, lk2, rms_w):
    f = np.float32
    Wk = np.asarray(Wk, f)
    wkd = np.zeros((1024, 1024), f)
    for hc in range(8):
        wkd[:, 128 * hc:128 * hc + 64] = Wk[:, 64 * hc:64 * (hc + 1)]
        wkd[:, 128 * hc + 64:128 * (hc + 1)] = Wk[:, 64 * hc:64 * (hc + 1)]
    return {
        "l0_wq": np.ascontiguousarray(np.asarray(Wq, f)),
        "l0_wkd": wkd,
        "l0_wv": np.ascontiguousarray(np.asarray(Wv, f)),
        "l0_wo": np.ascontiguousarray(np.asarray(Wo, f)),
        "l0_lq1": np.asarray(lq1, f)[None, :].copy(),
        "l0_lk1": np.asarray(lk1, f)[None, :].copy(),
        "l0_lq2": np.asarray(lq2, f)[None, :].copy(),
        "l0_lk2": np.asarray(lk2, f)[None, :].copy(),
        "l0_rmsw": np.ascontiguousarray(np.tile(np.asarray(rms_w, f), 2)[:, None]),
        "c_ones1": np.ones((1, 128), f),
    }



_PROG_CACHE = {}


def _prog(name, builder):
    if name not in _PROG_CACHE:
        _PROG_CACHE[name] = builder()
    return _PROG_CACHE[name]


def kernel(**inp):
    consts = {**host_constants(), **rope_consts(),
              "c_id128": np.eye(128, dtype=np.float32),
              "c_esel": _esel_const(),
              "c_ones1": np.ones((1, 128), np.float32)}
    moe_w = prep_moe_inputs(
        inp["L0_ffn_gate_w"], inp["L0_ffn_gate_b"], inp["L0_ffn_w1"],
        inp["L0_ffn_b1"], inp["L0_ffn_w3"], inp["L0_ffn_b3"], inp["L0_ffn_w2"],
        inp["L0_ffn_b2"], inp["L0_ffn_sw1"], inp["L0_ffn_sb1"],
        inp["L0_ffn_sw3"], inp["L0_ffn_sb3"], inp["L0_ffn_sw2"],
        inp["L0_ffn_sb2"])
    a0_w = prep_attn0_inputs(
        inp["L0_attn_Wq"], inp["L0_attn_Wk"], inp["L0_attn_Wv"],
        inp["L0_attn_Wo"], inp["L0_attn_lambda_q1"], inp["L0_attn_lambda_k1"],
        inp["L0_attn_lambda_q2"], inp["L0_attn_lambda_k2"], inp["L0_attn_rms_w"])
    f1_w = prep_ffn1_inputs(inp["L1_ffn_w1"], inp["L1_ffn_b1"],
                            inp["L1_ffn_w2"], inp["L1_ffn_b2"])
    a1_w = prep_attn1_inputs(
        inp["L1_attn_in_proj_w"], inp["L1_attn_in_proj_b"],
        inp["L1_attn_out_w"], inp["L1_attn_out_b"], inp["L1_attn_lin1_w"],
        inp["L1_attn_lin1_b"], inp["L1_attn_lin2_w"], inp["L1_attn_lin2_b"],
        inp["L1_attn_ln1_g"], inp["L1_attn_ln1_b"], inp["L1_attn_ln2_g"],
        inp["L1_attn_ln2_b"])
    allw = {**consts, **moe_w, **a0_w, **f1_w, **a1_w}

    x = np.asarray(inp["L0_attn_x"], np.float32)
    cur = [np.ascontiguousarray(x[c].T) for c in range(8)]
    kb = _prog("full", build_full_program)
    names = [n for n in kb.input_names if n != "xT"]
    in_maps = [dict(xT=cur[c], **{n: allw[n] for n in names}) for c in range(8)]
    res = run_bass_kernel_spmd(kb.nc, in_maps, core_ids=list(range(8)))
    return np.stack([np.asarray(res.results[c]["yT"], np.float32).T
                     for c in range(8)])


def full_in_map(inp_or_xT=None, **inp):
    consts = {**host_constants(), **rope_consts(),
              "c_id128": np.eye(128, dtype=np.float32),
              "c_esel": _esel_const(),
              "c_ones1": np.ones((1, 128), np.float32)}
    return consts


def build_full_program():
    kb = KB()
    xT = kb.dram_in("xT", [D, T])
    yT = kb.dram_out("yT", [D, T])
    m_w1 = kb.dram_in("m_w1", [9 * 1024, 1408])
    m_w3 = kb.dram_in("m_w3", [9 * 1024, 1408])
    m_w2 = kb.dram_in("m_w2", [9 * 1408, 1024])
    dins = {}
    for nm, sh, dt in (
            ("m_b1c", [128, 99], F32), ("m_b3c", [128, 99], F32),
            ("m_b2s", [9, 1024], F32), ("g_gwt", [128, 64], F32),
            ("g_gbt", [128, 8], F32), ("c_id128", [128, 128], F32),
            ("c_esel", [9, 9 * 128], BF16),
            ("l0_wq", [1024, 1024], F32), ("l0_wkd", [1024, 1024], F32),
            ("l0_wv", [1024, 512], F32), ("l0_wo", [1024, 1024], F32),
            ("l0_lq1", [1, 32], F32), ("l0_lk1", [1, 32], F32),
            ("l0_lq2", [1, 32], F32), ("l0_lk2", [1, 32], F32),
            ("l0_rmsw", [128, 1], F32),
            ("c_ropec", [128, T], BF16), ("c_ropes", [128, T], BF16),
            ("c_cmask", [128, 896], BF16), ("c_ones1", [1, 128], F32),
            ("f1_w1", [1024, 4096], F32), ("f1_w2", [4096, 1024], F32),
            ("f1_b1c", [128, 32], F32), ("f1_b2c", [128, 8], F32),
            ("a_wqkv", [1024, 3072], F32), ("a_wo", [1024, 1024], F32),
            ("a_w1", [1024, 4096], F32), ("a_w2", [4096, 1024], F32),
            ("a_bq", [128, 8], F32), ("a_bk", [128, 8], F32),
            ("a_bo", [128, 8], F32),
            ("a_b1", [128, 32], F32), ("a_b2", [128, 8], F32),
            ("a_g1", [128, 8], F32), ("a_b1n", [128, 8], F32),
            ("a_g2", [128, 8], F32), ("a_b2n", [128, 8], F32),
            ("c_lnsel", [128, 16 * NTT], BF16), ("c_bcsel", [16, 128 * NTT], F32)):
        dins[nm] = kb.dram_in(nm, sh, dt)
    dins["a_bvrow"] = kb.dram_in("a_bvrow", [1, 1024], F32)

    m = Model(kb)
    sb = {}
    bf_ins = {"c_esel"}
    cast_bf = {"m_b2s", "g_gwt", "c_ones1"}
    streamed = {"l0_wq", "l0_wkd", "l0_wv", "l0_wo", "f1_w1", "f1_w2",
                "a_wqkv", "a_wo", "a_w1", "a_w2", "c_ropec", "c_ropes",
                "c_cmask", "a_bvrow"}
    prevc = {"sync": None, "gpsimd": None}
    for nm in dins:
        if nm in ("c_lnsel", "c_bcsel") or nm in streamed:
            continue
        sh = list(dins[nm].shape)
        dt = BF16 if (nm in bf_ins or nm in cast_bf) else F32
        b = kb.sbuf("sb_" + nm, sh, dt)
        q = "gpsimd" if (nm in cast_bf) else "sync"
        rd = [prevc[q]] if prevc[q] is not None else []
        kb.dma(q, b.ap_all, dins[nm], reads=rd, writes=[b],
               sem_group="consts_" + q)
        prevc[q] = b
        sb[nm] = b
    xTr = xT.rearrange("(c p) t -> p c t", p=128)
    for c in range(DC):
        kb.dma("sync", m.Xc[c][:, 0:T], xTr[:, c, :], writes=[m.Xc[c]])
    kb.dma("sync", m.lnsel.ap_all, dins["c_lnsel"], writes=[m.lnsel])
    kb.dma("sync", m.bcsel.ap_all, dins["c_bcsel"], writes=[m.bcsel])

    bfw = {}
    deps = {}

    prev_cast = [None]

    def cast_w(name, src, chunks):
        shp = list(src.shape)
        dst = kb.dram_internal("bfw_" + name, shp)
        bufs = []
        for i, (r0, r1) in enumerate(chunks):
            b = Buf(None, f"cw_{name}_{i}")
            kb.bufs.append(b)
            rd = [prev_cast[0]] if prev_cast[0] is not None else []
            o = kb.op("gpsimd",
                      lambda e, src=src, dst=dst, r0=r0, r1=r1: e.dma_start(
                          out=dst[r0:r1, :], in_=src[r0:r1, :]),
                      reads=rd, writes=[b], is_dma=True)
            o.dma_buf = b
            o.sem_group = "castq"
            prev_cast[0] = b
            bufs.append(b)
        bfw[name] = dst
        return bufs

    for nm, src, rw in (("m_w1", m_w1, 1024), ("m_w3", m_w3, 1024),
                        ("m_w2", m_w2, 1408)):
        bl = cast_w(nm, src, [(rw * ex, rw * (ex + 1)) for ex in range(9)])
        for ex in range(9):
            deps[(nm, ex)] = [bl[ex]]
    for nm in ("l0_wq", "l0_wkd", "l0_wv", "l0_wo", "f1_w1", "f1_w2",
               "a_wqkv", "a_wo", "a_w1", "a_w2"):
        deps[nm] = cast_w(nm, dins[nm], [(0, dins[nm].shape[0])])

    moe_cols = {"gwt": sb["g_gwt"], "gbt": sb["g_gbt"], "b2s": sb["m_b2s"],
                "id128": sb["c_id128"], "esel": sb["c_esel"]}
    moe_deps = {}
    for ex in range(9):
        for nm in ("m_w1", "m_w3", "m_w2"):
            moe_deps[(nm, ex)] = deps[(nm, ex)]
    moe_w = {"m_w1": bfw["m_w1"], "m_w3": bfw["m_w3"],
             "m_w2": bfw["m_w2"], "deps": moe_deps}
    a0_cols = {"lq1": sb["l0_lq1"], "lk1": sb["l0_lk1"], "lq2": sb["l0_lq2"],
               "lk2": sb["l0_lk2"], "rmsw": sb["l0_rmsw"],
               "ropec_d": dins["c_ropec"], "ropes_d": dins["c_ropes"],
               "cmask_d": dins["c_cmask"], "ones1": sb["c_ones1"]}
    a0_w = {"l0_wq": bfw["l0_wq"], "l0_wkd": bfw["l0_wkd"],
            "l0_wv": bfw["l0_wv"], "l0_wo": bfw["l0_wo"],
            "deps": {k: deps[k] for k in ("l0_wq", "l0_wkd", "l0_wv", "l0_wo")}}
    a1_cols = {"bq": sb["a_bq"], "bk": sb["a_bk"], "bvrow_d": dins["a_bvrow"],
               "bo": sb["a_bo"], "b1": sb["a_b1"], "b2": sb["a_b2"],
               "g1": sb["a_g1"], "b1n": sb["a_b1n"], "g2": sb["a_g2"],
               "b2n": sb["a_b2n"], "ones1": sb["c_ones1"]}
    a1_w = {"a_wqkv": bfw["a_wqkv"], "a_wo": bfw["a_wo"],
            "a_w1": bfw["a_w1"], "a_w2": bfw["a_w2"],
            "deps": {k: deps[k] for k in ("a_wqkv", "a_wo", "a_w1", "a_w2")}}

    m.layernorm()
    m.moe(moe_w, sb["m_b1c"], sb["m_b3c"], moe_cols)
    m.layernorm()
    m.attn0(a0_w, a0_cols)
    m.layernorm()
    m.moe(moe_w, sb["m_b1c"], sb["m_b3c"], moe_cols)
    m.layernorm()
    f1w = {"w1": bfw["f1_w1"], "w2": bfw["f1_w2"],
           "deps": {"w1": deps["f1_w1"], "w2": deps["f1_w2"]}}
    m.ffn1(f1w, sb["f1_b1c"], sb["f1_b2c"])
    m.layernorm()
    m.attn1(a1_w, a1_cols)
    m.layernorm()
    m.ffn1(f1w, sb["f1_b1c"], sb["f1_b2c"])
    yTr = yT.rearrange("(c p) t -> p c t", p=128)
    for c in range(DC):
        kb.dma("sync", yTr[:, c, :], m.Xc[c][:, 0:T], reads=[m.Xc[c]])
    kb.finalize()
    return kb


# version 15 (final)
# speedup vs baseline: 1.0497x; 1.0165x over previous
import math
import itertools
import contextlib
from collections import defaultdict

import numpy as np
import concourse.bass as bass
import concourse.mybir as mybir
from concourse.bass_utils import run_bass_kernel_spmd

F32 = mybir.dt.float32
BF16 = mybir.dt.bfloat16
U8 = mybir.dt.uint8
AF = mybir.ActivationFunctionType
ALU = mybir.AluOpType
AX = mybir.AxisListType

MAGIC = float(np.float32(12582912.0))


ENGS = ("tensor", "vector", "scalar", "gpsimd", "sync")


class Op:
    __slots__ = ("eng", "emit", "deps", "inc", "count", "is_dma", "dma_buf",
                 "skey", "sem_group")

    def __init__(self, eng, emit, deps, is_dma):
        self.eng = eng
        self.emit = emit
        self.deps = deps
        self.is_dma = is_dma
        self.inc = is_dma
        self.count = None
        self.sem_group = None


class Buf:

    __slots__ = ("h", "writer", "readers", "name", "win", "after_bufs")

    def __init__(self, h, name, win=None):
        self.h = h
        self.name = name
        self.writer = None
        self.readers = []
        self.win = win
        self.after_bufs = None

    def __getitem__(self, sl):
        if self.win is None:
            return self.h[sl]
        if len(self.win) == 4:
            ro, rows, off, n = self.win
        else:
            rows, off, n = self.win
            ro = 0
        rs, cs = sl
        start = (cs.start or 0) + off
        stop = (cs.stop if cs.stop is not None else n) + off
        assert stop - off <= n
        r0 = 0 if rs == slice(None) else (rs.start or 0)
        r1 = rows if rs == slice(None) else rs.stop
        return self.h[ro + r0:ro + r1, start:stop]

    @property
    def ap_all(self):
        if self.win is None:
            return self.h[:, :]
        if len(self.win) == 4:
            ro, rows, off, n = self.win
            return self.h[ro:ro + rows, off:off + n]
        rows, off, n = self.win
        return self.h[0:rows, off:off + n]


class KB:

    def __init__(self, name="mk"):
        self.nc = bass.Bass("TRN2", target_bir_lowering=False)
        self.ctx = contextlib.ExitStack()
        self.ops = []
        self.bufs = []
        self._uid = 0

    def uid(self):
        self._uid += 1
        return self._uid

    def dram_in(self, name, shape, dtype=F32):
        if not hasattr(self, "input_names"):
            self.input_names = []
        self.input_names.append(name)
        return self.nc.dram_tensor(name, shape, dtype, kind="ExternalInput").ap()

    def dram_out(self, name, shape, dtype=F32):
        return self.nc.dram_tensor(name, shape, dtype, kind="ExternalOutput").ap()

    def dram_internal(self, name, shape, dtype=BF16):
        return self.nc.dram_tensor(name, shape, dtype).ap()

    def sbuf(self, name, shape, dtype=F32):
        h = self.ctx.enter_context(self.nc.sbuf_tensor(name, shape, dtype))
        b = Buf(h, name)
        self.bufs.append(b)
        return b

    def psum(self, name, shape, dtype=F32):
        per = {F32: 512, BF16: 1024}[dtype]
        assert shape[1] <= per
        h = self.ctx.enter_context(self.nc.psum_tensor(name, [128, per], dtype))
        b = Buf(h, name)
        self.bufs.append(b)
        return b

    def op(self, eng, emit, reads=(), writes=(), is_dma=False):
        deps = []
        for b in reads:
            if b.writer is not None:
                deps.append(b.writer)
        o = Op(eng, emit, deps, is_dma)
        for b in reads:
            b.readers.append(o)
        for b in writes:
            if b.after_bufs:
                for ob in b.after_bufs:
                    deps.extend(ob.readers)
                    if ob.writer is not None:
                        deps.append(ob.writer)
                b.after_bufs = None
            if b.writer is not None:
                deps.append(b.writer)
            deps.extend(b.readers)
            b.writer = o
            b.readers = []
        o.deps = [d for d in dict.fromkeys(deps) if d is not o]
        self.ops.append(o)
        return o

    def dma(self, queue, out, in_, reads=(), writes=(), sem_group=None, **kw):
        o = self.op(
            queue,
            lambda e: e.dma_start(out=out, in_=in_, **kw),
            reads=reads, writes=writes, is_dma=True)
        o.dma_buf = writes[0] if writes else None
        o.sem_group = sem_group
        return o

    def emit_all(self):
        def sem_key(o):
            if not o.is_dma:
                return o.eng
            if getattr(o, "sem_group", None) is not None:
                return ("buf", o.sem_group)
            if getattr(o, "dma_buf", None) is not None:
                return ("buf", id(o.dma_buf))
            return ("store", o.eng)

        for o in self.ops:
            for d in o.deps:
                if d.eng == "tensor" and o.eng == "tensor":
                    continue
                d.inc = True
        cnt = defaultdict(int)
        store_totals = defaultdict(int)
        for o in self.ops:
            if o.inc:
                k = sem_key(o)
                cnt[k] += 16 if o.is_dma else 1
                o.count = cnt[k]
                o.skey = k
                if not isinstance(k, str) and k[0] == "store":
                    store_totals[k] = cnt[k]
        sems = {}
        for k in cnt:
            nm = k if isinstance(k, str) else f"{k[0]}_{k[1]}"
            sems[k] = self.ctx.enter_context(self.nc.semaphore(f"s_{nm}"))
        running_store = defaultdict(int)
        waitsfor = []
        ptr = {}
        for o in self.ops:
            if o.inc and not isinstance(o.skey, str) and o.skey[0] == "store":
                running_store[o.skey] = o.count
            wmap = {}
            for d in o.deps:
                if d.eng == "tensor" and o.eng == "tensor":
                    continue
                k = d.skey if hasattr(d, "skey") and d.inc else None
                if k is None:
                    continue
                c = d.count
                if not isinstance(k, str) and k[0] == "store":
                    c = running_store[k]
                if wmap.get(k, 0) < c:
                    wmap[k] = c
            waitsfor.append(wmap)
        waited = defaultdict(int)
        plans = {e: [] for e in ENGS}
        for o, wmap in zip(self.ops, waitsfor):
            w2 = {}
            for k, c in wmap.items():
                if waited[(o.eng, k)] < c:
                    waited[(o.eng, k)] = c
                    w2[k] = c
            plans[o.eng].append((w2, o))
        block = self.ctx.enter_context(self.nc.Block())

        def run_section(engine, eng_name):
            for wmap, o in plans[eng_name]:
                for k, c in wmap.items():
                    engine.wait_ge(sems[k], c)
                ins = o.emit(engine)
                if o.inc:
                    ins.then_inc(sems[o.skey], 16 if o.is_dma else 1)

        for e in ENGS:
            sec = getattr(block, e)

            def make(e=e):
                def _(engine):
                    run_section(engine, e)
                return _
            sec(make())
        self.ctx.close()
        self.n_sems = len(sems)

    def finalize(self):
        last_dma = {}
        for o in self.ops:
            if o.is_dma:
                last_dma[o.eng] = o
        for e, o in last_dma.items():
            w = Op(e, lambda engine: engine.nop(), [o], False)
            self.ops.append(w)
        self.emit_all()



def ceil_div(a, b):
    return (a + b - 1) // b


T = 2048
D = 1024
DC = D // 128
NTT = T // 512
LN_EPS = 1e-5


def host_constants():
    import ml_dtypes
    bf = ml_dtypes.bfloat16
    lnsel = np.zeros((128, 16 * NTT), dtype=bf)
    for i in range(NTT):
        lnsel[:, 16 * i + i] = 1.0
    bcsel = np.zeros((16, 128 * NTT), dtype=np.float32)
    for i in range(NTT):
        bcsel[i, 128 * i:128 * (i + 1)] = 1.0
    return {"c_lnsel": lnsel, "c_bcsel": bcsel}


class Model:

    def __init__(self, kb: KB):
        self.kb = kb
        nc = kb.nc
        self.X = kb.sbuf("X", [128, DC * T], F32)
        self.Xc = [Buf(self.X.h, f"Xc{c}", win=(128, c * T, T)) for c in range(DC)]
        kb.bufs.extend(self.Xc)
        self.xn = kb.sbuf("xn", [128, DC * T], BF16)
        self.xnc = [Buf(self.xn.h, f"xnc{c}", win=(128, c * T, T)) for c in range(DC)]
        kb.bufs.extend(self.xnc)
        self.psd = []
        for j in range(4):
            h = kb.ctx.enter_context(kb.nc.psum_tensor(f"psd{j}", [128, 1024], F32))
            self.psd.append(h)
        self.ps = []
        for i in range(8):
            b = Buf(self.psd[i // 2], f"ps{i}", win=(128, 512 * (i % 2), 512))
            kb.bufs.append(b)
            self.ps.append(b)
        self.xsq = [kb.sbuf("xsq0", [128, 512], BF16)]
        self.xbb = [kb.sbuf("xbb0", [128, 512], BF16)]
        strow = kb.sbuf("strow", [16, 2 * 512], F32)
        self.strow = [Buf(strow.h, f"strow{i}", win=(16, 512 * (i % 2), 512))
                      for i in range(2)]
        self._tscr = []
        self._mscr = None
        self.ccol = kb.sbuf("ccol", [128, 8], F32)
        self._ccol_vals = {}
        self.lnsel = kb.sbuf("lnsel", [128, 16 * NTT], BF16)
        self.bcsel = kb.sbuf("bcsel", [16, 128 * NTT], F32)

    def arena_win(self, name, off, n, after=()):
        if getattr(self, "_arena", None) is None:
            self._arena = self.kb.sbuf("arena", [128, 40960], BF16)
            self._arena_owners = []
        assert off + n <= 40960, (name, off, n)
        b = Buf(self._arena.h, name, win=(128, off, n))
        aft = list(after)
        for (o2, n2, ob) in self._arena_owners:
            if o2 < off + n and off < o2 + n2:
                aft.append(ob)
        b.after_bufs = aft
        self._arena_owners.append((off, n, b))
        self.kb.bufs.append(b)
        return b

    def arena_win_f32(self, name, off_bf16, n_f32):
        if getattr(self, "_arena", None) is None:
            self.arena_win("_dummy", 0, 1)
        if getattr(self, "_arena_f32h", None) is None:
            self._arena_f32h = self._arena.h.bitcast(F32)
        assert off_bf16 % 2 == 0 and off_bf16 + 2 * n_f32 <= 40960
        b = Buf(self._arena_f32h, name, win=(128, off_bf16 // 2, n_f32))
        aft = []
        for (o2, n2, ob) in self._arena_owners:
            if o2 < off_bf16 + 2 * n_f32 and off_bf16 < o2 + n2:
                aft.append(ob)
        b.after_bufs = aft
        self._arena_owners.append((off_bf16, 2 * n_f32, b))
        self.kb.bufs.append(b)
        return b

    def arena_win_u8(self, name, off_bf16, n_u8):
        if getattr(self, "_arena", None) is None:
            self.arena_win("_dummy", 0, 1)
        if getattr(self, "_arena_u8h", None) is None:
            self._arena_u8h = self._arena.h.bitcast(U8)
        assert n_u8 % 2 == 0 and off_bf16 * 2 + n_u8 <= 81920
        b = Buf(self._arena_u8h, name, win=(128, off_bf16 * 2, n_u8))
        aft = []
        for (o2, n2, ob) in self._arena_owners:
            if o2 < off_bf16 + n_u8 // 2 and off_bf16 < o2 + n2:
                aft.append(ob)
        b.after_bufs = aft
        self._arena_owners.append((off_bf16, n_u8 // 2, b))
        self.kb.bufs.append(b)
        return b

    def buf(self, name, shape, dtype=F32):
        if not hasattr(self, "_bufcache"):
            self._bufcache = {}
        if name not in self._bufcache:
            self._bufcache[name] = self.kb.sbuf(name, shape, dtype)
        return self._bufcache[name]

    def scr(self, i):
        while len(self._tscr) <= i:
            self._tscr.append(self.kb.sbuf(f"tscr{len(self._tscr)}", [128, 512], F32))
        return self._tscr[i]

    @property
    def mscr(self):
        if self._mscr is None:
            self._mscr = self.kb.sbuf("mscr", [128, 512], U8)
        return self._mscr

    def const_col(self, val):
        kb = self.kb
        if val not in self._ccol_vals:
            i = len(self._ccol_vals)
            assert i < 8
            self._ccol_vals[val] = i
            kb.op("vector", lambda e, i=i, val=val: e.memset(
                self.ccol[:, i:i + 1], float(val)),
                writes=[self.ccol])
        i = self._ccol_vals[val]
        return self.ccol[:, i:i + 1]

    def load_consts(self, kb, cmap):
        kb.dma("sync", self.lnsel.ap_all, cmap["c_lnsel"], writes=[self.lnsel])
        kb.dma("sync", self.bcsel.ap_all, cmap["c_bcsel"], writes=[self.bcsel])

    def layernorm(self, dst=None, src=None, eps=LN_EPS, gain_bias=None):
        kb = self.kb
        src = src if src is not None else self.Xc
        dst = dst or self.xn
        if dst is self.xn:
            dsts = self.xnc
            dstap = lambda c, a, b: self.xnc[c][:, a:b]
        else:
            dsts = [dst] * DC
            dstap = lambda c, a, b: dst[:, c * T + a:c * T + b]
        if isinstance(src, list):
            srcs = src
            srcap = lambda c, a, b: srcs[c][:, a:b]
        else:
            srcs = [src] * DC
            srcap = lambda c, a, b: src[:, c * T + a:c * T + b]
        ps_sx, ps_sq = self.ps[6], self.ps[7]
        mrow = self.strow[0]
        qrow = self.strow[1]
        trow = self.strow[1]
        srow = self.strow[1]
        arow = self.strow[1]
        brow = self.strow[0]
        for c in range(DC):
            for i in range(NTT):
                xs = self.xsq[0]
                xb = self.xbb[0]
                off = 512 * i
                kb.op("scalar",
                      lambda e, off=off, xs=xs, c=c: e.activation(
                          xs[:, :], srcap(c, off, off + 512), AF.Square),
                      reads=[srcs[c]], writes=[xs])
                kb.op("vector",
                      lambda e, off=off, xb=xb, c=c: e.tensor_copy(
                          xb[:, :], srcap(c, off, off + 512)),
                      reads=[srcs[c]], writes=[xb])
                st = (c == 0 and i == 0)
                fin = (c == DC - 1 and i == NTT - 1)
                kb.op("tensor",
                      lambda e, i=i, xs=xs, st=st, fin=fin: e.matmul(
                          ps_sq[0:16, :], self.lnsel[:, 16 * i:16 * (i + 1)],
                          xs[:, :], start=st, stop=fin),
                      reads=[xs, self.lnsel], writes=[ps_sq])
                kb.op("tensor",
                      lambda e, i=i, xb=xb, st=st, fin=fin: e.matmul(
                          ps_sx[0:16, :], self.lnsel[:, 16 * i:16 * (i + 1)],
                          xb[:, :], start=st, stop=fin),
                      reads=[xb, self.lnsel], writes=[ps_sx])
        kb.op("vector", lambda e: e.tensor_scalar(
            mrow[:, :], ps_sx[0:16, :], 1.0 / D, None, ALU.mult),
            reads=[ps_sx], writes=[mrow])
        kb.op("vector", lambda e: e.tensor_tensor(
            qrow[:, :], mrow[:, :], mrow[:, :], ALU.mult),
            reads=[mrow], writes=[qrow])
        kb.op("vector", lambda e: e.scalar_tensor_tensor(
            qrow[:, :], ps_sq[0:16, :], 1.0 / D, qrow[:, :],
            ALU.mult, ALU.subtract),
            reads=[ps_sq, qrow], writes=[qrow])
        epsc = self.const_col(float(eps))
        kb.op("scalar", lambda e, epsc=epsc: e.activation(
            srow[:, :], qrow[:, :], AF.Sqrt, bias=epsc[:16, :]),
            reads=[qrow, self.ccol], writes=[srow])
        kb.op("vector", lambda e: e.reciprocal(arow[:, :], srow[:, :]),
              reads=[srow], writes=[arow])
        kb.op("vector", lambda e: e.scalar_tensor_tensor(
            brow[:, :], mrow[:, :], -1.0, arow[:, :], ALU.mult, ALU.mult),
            reads=[mrow, arow], writes=[brow])
        psA, psB = self.ps[4], self.ps[5]
        for i in range(NTT):
            kb.op("tensor", lambda e, i=i: e.matmul(
                psA[:, :], self.bcsel[:, 128 * i:128 * (i + 1)], arow[:, :],
                start=True, stop=True),
                reads=[arow, self.bcsel], writes=[psA])
            kb.op("tensor", lambda e, i=i: e.matmul(
                psB[:, :], self.bcsel[:, 128 * i:128 * (i + 1)], brow[:, :],
                start=True, stop=True),
                reads=[brow, self.bcsel], writes=[psB])
            for c in range(DC):
                tsc = self.scr(c % 2)
                sl = slice(c * T + 512 * i, c * T + 512 * (i + 1))
                kb.op("vector", lambda e, i=i, tsc=tsc, c=c: e.tensor_tensor(
                    tsc[:, 0:512], srcap(c, 512 * i, 512 * (i + 1)), psA[:, :],
                    ALU.mult),
                    reads=[srcs[c], psA], writes=[tsc])
                if gain_bias is None:
                    kb.op("vector", lambda e, i=i, tsc=tsc, c=c: e.tensor_tensor(
                        dstap(c, 512 * i, 512 * (i + 1)), tsc[:, 0:512],
                        psB[:, :], ALU.add),
                        reads=[tsc, psB], writes=[dsts[c]])
                else:
                    gcol, bcol = gain_bias
                    kb.op("vector", lambda e, sl=sl, tsc=tsc: e.tensor_tensor(
                        tsc[:, 0:512], tsc[:, 0:512], psB[:, :], ALU.add),
                        reads=[tsc, psB], writes=[tsc])
                    kb.op("vector", lambda e, i=i, tsc=tsc, c=c: e.tensor_scalar(
                        dstap(c, 512 * i, 512 * (i + 1)), tsc[:, 0:512], gcol[:, c:c + 1],
                        bcol[:, c:c + 1], ALU.mult, ALU.add),
                        reads=[tsc, gcol, bcol], writes=[dsts[c]])

    def ffn1(self, wmap, b1c, b2c):
        kb = self.kb
        HF = 4096
        NHC = HF // 128
        wdep = wmap.get("deps", {})
        hbuf = self.arena_win("f1_hbuf", 0, NHC * 512)
        w1sl = [self.arena_win(f"f1_w1sl{i}", 16384 + 4096 * i, DC * 512)
                for i in range(2)]
        w2ch = [self.arena_win(f"f1_w2ch{i}", 24576 + 2048 * i, 2048)
                for i in range(2)]
        scrsets = [[self.scr(i) for i in range(5)]]
        for k in range(2):
            scrsets.append([self.arena_win_f32(f"f1_scrB{k}_{i}",
                                               28672 + 5120 * k + 1024 * i, 512)
                            for i in range(5)])
        w1r = wmap["w1"].rearrange("(c p) o -> p c o", p=128)
        w2r = wmap["w2"]
        b256 = self.buf("f1_b256", [128, 32], F32)
        kb.op("vector", lambda e: e.tensor_scalar(
            b256[:, 0:32], b1c[:, 0:32], 256.0, None, ALU.mult),
            reads=[b1c], writes=[b256])
        for ti in range(NTT):
            toff = 512 * ti
            for sl_i in range(DC):
                wb = w1sl[sl_i % 2]
                kb.dma("sync", wb.ap_all, w1r[:, :, 512 * sl_i:512 * (sl_i + 1)],
                       reads=wdep.get("w1", ()), writes=[wb])
                for hh in range(4):
                    hc = 4 * sl_i + hh
                    ps = self.ps[hc % 4]
                    for kc in range(DC):
                        kb.op("tensor", lambda e, wb=wb, hh=hh, kc=kc, ps=ps, toff=toff: e.matmul(
                            ps[:, :],
                            wb[:, 512 * kc + 128 * hh: 512 * kc + 128 * hh + 128],
                            self.xnc[kc][:, toff:toff + 512],
                            start=(kc == 0), stop=(kc == DC - 1)),
                            reads=[wb, self.xnc[kc]], writes=[ps])
                    self._pwl(ps, b256, hc, hbuf, 512 * hc, 512,
                              scrset=scrsets[hc % 3])
            for kcp in range(16):
                wc = w2ch[kcp % 2]
                kb.dma("sync", wc.ap_all,
                       w2r[256 * kcp:256 * (kcp + 1), :]
                       .rearrange("(c p) o -> p c o", p=128),
                       reads=wdep.get("w2", ()), writes=[wc])
                for kk in range(2):
                    kc = 2 * kcp + kk
                    for dc in range(DC):
                        ps = self.ps[dc]
                        kb.op("tensor", lambda e, wc=wc, kc=kc, kk=kk, dc=dc, ps=ps: e.matmul(
                            ps[:, :],
                            wc[:, 1024 * kk + 128 * dc: 1024 * kk + 128 * dc + 128],
                            hbuf[:, kc * 512: (kc + 1) * 512],
                            start=(kc == 0), stop=(kc == NHC - 1)),
                            reads=[wc, hbuf], writes=[ps])
            for dc in range(DC):
                ps = self.ps[dc]
                sl = slice(dc * T + toff, dc * T + toff + 512)
                t0 = self.scr(dc % 4)
                Xc = self.Xc[dc]
                kb.op("vector", lambda e, dc=dc, ps=ps, t0=t0: e.tensor_scalar(
                    t0[:, 0:512], ps[:, :], b2c[:, dc:dc + 1], 256.0,
                    ALU.add, ALU.mult),
                    reads=[ps, b2c], writes=[t0])
                kb.op("vector", lambda e, t0=t0: e.tensor_scalar(
                    t0[:, 0:512], t0[:, 0:512], MAGIC, MAGIC,
                    ALU.add, ALU.subtract),
                    reads=[t0], writes=[t0])
                kb.op("vector", lambda e, t0=t0, toff=toff, Xc=Xc: e.scalar_tensor_tensor(
                    Xc[:, toff:toff + 512], t0[:, 0:512], 1.0 / 512.0,
                    Xc[:, toff:toff + 512], ALU.mult, ALU.add),
                    reads=[t0, Xc], writes=[Xc])
        return

    def _pwl(self, ps, b1c, hc, hbuf, hoff, W, scrset=None):
        kb = self.kb
        GELU4_256 = 256.0 * (2.0 + 2.0 * math.erf(2.0 * math.sqrt(2.0)))
        if scrset is None:
            scrset = [self.scr(i) for i in range(5)]
        nt, ncl, jj, tt_, g0 = scrset
        g1 = jj
        dd = ncl
        yy = tt_
        g4c = self.const_col(-GELU4_256)
        mgc = self.const_col(MAGIC)
        kb.op("scalar", lambda e: e.activation(
            nt[:, 0:W], ps[:, 0:W], AF.Identity,
            bias=b1c[:, hc:hc + 1], scale=256.0),
            reads=[ps, b1c, self.ccol], writes=[nt])
        kb.op("vector", lambda e: e.tensor_scalar(
            nt[:, 0:W], nt[:, 0:W], MAGIC, MAGIC, ALU.add, ALU.subtract),
            reads=[nt], writes=[nt])
        kb.op("vector", lambda e: e.tensor_scalar(
            ncl[:, 0:W], nt[:, 0:W], -1024.0, 1024.0, ALU.max, ALU.min),
            reads=[nt], writes=[ncl])
        MAGIC64 = float(np.float32(64.0 * 12582912.0))
        kb.op("vector", lambda e: e.tensor_scalar(
            g0[:, 0:W], ncl[:, 0:W], 31.5, None, ALU.subtract),
            reads=[ncl], writes=[g0])
        kb.op("vector", lambda e, MAGIC64=MAGIC64: e.tensor_scalar(
            jj[:, 0:W], g0[:, 0:W], MAGIC64, MAGIC64, ALU.add, ALU.subtract),
            reads=[g0], writes=[jj])
        kb.op("gpsimd", lambda e: e.tensor_tensor(
            tt_[:, 0:W], ncl[:, 0:W], jj[:, 0:W], ALU.subtract),
            reads=[ncl, jj], writes=[tt_])
        kb.op("scalar", lambda e: e.activation(
            g0[:, 0:W], jj[:, 0:W], AF.Gelu, bias=0.0, scale=1.0 / 256.0),
            reads=[jj], writes=[g0])
        qc = self.const_col(0.25)
        kb.op("scalar", lambda e, qc=qc: e.activation(
            g1[:, 0:W], jj[:, 0:W], AF.Gelu, bias=qc, scale=1.0 / 256.0),
            reads=[jj, self.ccol], writes=[g1])
        kb.op("gpsimd", lambda e: e.tensor_tensor(
            dd[:, 0:W], g1[:, 0:W], g0[:, 0:W], ALU.subtract),
            reads=[g1, g0], writes=[dd])
        kb.op("gpsimd", lambda e: e.tensor_tensor(
            dd[:, 0:W], tt_[:, 0:W], dd[:, 0:W], ALU.mult),
            reads=[tt_, dd], writes=[dd])
        kb.op("vector", lambda e: e.scalar_tensor_tensor(
            yy[:, 0:W], dd[:, 0:W], 1.0 / 64.0, g0[:, 0:W], ALU.mult, ALU.add),
            reads=[dd, g0], writes=[yy])
        kb.op("scalar", lambda e, g4c=g4c: e.activation(
            nt[:, 0:W], nt[:, 0:W], AF.Relu, bias=g4c),
            reads=[nt, self.ccol], writes=[nt])
        kb.op("vector", lambda e: e.scalar_tensor_tensor(
            yy[:, 0:W], nt[:, 0:W], 1.0 / 256.0, yy[:, 0:W], ALU.mult, ALU.add),
            reads=[nt, yy], writes=[yy])
        kb.op("scalar", lambda e, mgc=mgc: e.activation(
            yy[:, 0:W], yy[:, 0:W], AF.Identity, bias=mgc,
            scale=256.0),
            reads=[yy, self.ccol], writes=[yy])
        kb.op("vector", lambda e: e.tensor_scalar(
            hbuf[:, hoff:hoff + W], yy[:, 0:W], MAGIC, 1.0 / 256.0,
            ALU.subtract, ALU.mult),
            reads=[yy], writes=[hbuf])

    def moe(self, wmap, b1c, b3c, cols):
        kb = self.kb
        HE = 1408
        NHC = 11
        gwt, gbt, b2s, id128, esel = (cols[k] for k in
                                      ("gwt", "gbt", "b2s", "id128", "esel"))
        gscr = self.buf("moe_gscr", [128, 48], F32)
        rw9f = self.buf("moe_rw9f", [128, 16], F32)
        rwT9b = self.buf("moe_rwT9", [9, T], BF16)
        rwT9 = rwT9b
        w1b = self.arena_win("moe_w1b", 0, DC * HE)
        w3b = self.arena_win("moe_w3b", DC * HE, DC * HE)
        w2b = self.arena_win("moe_w2b", 2 * DC * HE, NHC * 1024)
        hst = [self.arena_win("moe_hst", 33792, NHC * 512)]
        cbuf, ubuf = self.scr(0), self.scr(1)
        cbuf2, ubuf2 = self.scr(2), self.scr(3)
        ssb = self.arena_win("moe_ssb", 39424, 512)
        ssb2 = self.arena_win("moe_ssb2", 40448, 512)
        bcsb = self.arena_win("moe_bcsb", 39936, 512)
        psG, psT, psB = self.ps[0], self.ps[1], self.ps[7]

        kb.op("vector", lambda e: e.memset(rw9f[:, 8:9], 1.0), writes=[rw9f])
        for ti in range(16):
            for kc in range(DC):
                kb.op("tensor", lambda e, ti=ti, kc=kc: e.matmul(
                    psG[:, 0:8], self.xnc[kc][:, 128 * ti:128 * (ti + 1)],
                    gwt[:, 8 * kc:8 * (kc + 1)],
                    start=(kc == 0), stop=(kc == DC - 1)),
                    reads=[self.xnc[kc], gwt], writes=[psG])
            kb.op("vector", lambda e: e.reduce_max(
                gscr[:, 0:1], psG[:, 0:8], axis=AX.X, negate=True),
                reads=[psG], writes=[gscr])
            kb.op("scalar", lambda e: e.activation(
                gscr[:, 8:16], psG[:, 0:8], AF.Exp, bias=gscr[:, 0:1],
                accum_out=gscr[:, 1:2]),
                reads=[psG, gscr], writes=[gscr])
            kb.op("vector", lambda e: e.reciprocal(
                gscr[:, 2:3], gscr[:, 1:2]), reads=[gscr], writes=[gscr])
            kb.op("vector", lambda e: e.tensor_scalar(
                gscr[:, 16:24], gscr[:, 8:16], gscr[:, 2:3], None, ALU.mult),
                reads=[gscr], writes=[gscr])
            kb.op("vector", lambda e: e.tensor_tensor(
                gscr[:, 24:32], gscr[:, 16:24], gbt[:, 0:8], ALU.add),
                reads=[gscr, gbt], writes=[gscr])
            kb.op("vector", lambda e: e.tensor_scalar(
                gscr[:, 32:40], gscr[:, 24:32], gscr[:, 24:25], None, ALU.is_lt),
                reads=[gscr], writes=[gscr])
            for e2 in range(1, 8):
                kb.op("vector", lambda e, e2=e2: e.scalar_tensor_tensor(
                    gscr[:, 32:40], gscr[:, 24:32], gscr[:, 24 + e2:25 + e2],
                    gscr[:, 32:40], ALU.is_lt, ALU.add),
                    reads=[gscr], writes=[gscr])
            kb.op("vector", lambda e: e.scalar_tensor_tensor(
                rw9f[:, 0:8], gscr[:, 32:40], 1.5, gscr[:, 16:24],
                ALU.is_lt, ALU.mult),
                reads=[gscr], writes=[rw9f])
            kb.op("tensor", lambda e: e.transpose(
                psT[0:9, 0:128], rw9f[:, 0:9], id128[:, :]),
                reads=[rw9f, id128], writes=[psT])
            kb.op("vector", lambda e, ti=ti: e.tensor_copy(
                rwT9[0:9, 128 * ti:128 * (ti + 1)], psT[0:9, 0:128]),
                reads=[psT], writes=[rwT9])

        w1r = wmap["m_w1"]
        w3r = wmap["m_w3"]
        w2r = wmap["m_w2"]
        wdep = wmap.get("deps", {})
        for ex in range(9):
            kb.dma("sync", w1b.ap_all,
                   w1r[1024 * ex:1024 * (ex + 1), :].rearrange("(c p) h -> p c h", p=128),
                   reads=wdep.get(("m_w1", ex), ()), writes=[w1b])
            kb.dma("sync", w3b.ap_all,
                   w3r[1024 * ex:1024 * (ex + 1), :].rearrange("(c p) h -> p c h", p=128),
                   reads=wdep.get(("m_w3", ex), ()), writes=[w3b])
            kb.dma("sync", w2b.ap_all,
                   w2r[HE * ex:HE * (ex + 1), :].rearrange("(c p) d -> p c d", p=128),
                   reads=wdep.get(("m_w2", ex), ()), writes=[w2b])
            for qt in range(4):
                toff = 512 * qt
                hs = hst[0]
                kb.op("tensor", lambda e, ex=ex, toff=toff: e.matmul(
                    psB[:, :], esel[0:9, 128 * ex:128 * (ex + 1)],
                    rwT9[0:9, toff:toff + 512], start=True, stop=True),
                    reads=[esel, rwT9], writes=[psB])
                kb.op("vector", lambda e: e.tensor_copy(
                    bcsb[:, :], psB[:, :]), reads=[psB], writes=[bcsb])
                for hc in range(NHC):
                    pa = self.ps[hc % 2]
                    pc = self.ps[2 + hc % 2]
                    cb = [cbuf, cbuf2][hc % 2]
                    ub = [ubuf, ubuf2][hc % 2]
                    sbb = [ssb, ssb2][hc % 2]
                    for kc in range(DC):
                        kb.op("tensor", lambda e, hc=hc, kc=kc, toff=toff, pa=pa: e.matmul(
                            pa[:, :], w1b[:, HE * kc + 128 * hc: HE * kc + 128 * hc + 128],
                            self.xnc[kc][:, toff:toff + 512],
                            start=(kc == 0), stop=(kc == DC - 1)),
                            reads=[w1b, self.xnc[kc]], writes=[pa])
                        kb.op("tensor", lambda e, hc=hc, kc=kc, toff=toff, pc=pc: e.matmul(
                            pc[:, :], w3b[:, HE * kc + 128 * hc: HE * kc + 128 * hc + 128],
                            self.xnc[kc][:, toff:toff + 512],
                            start=(kc == 0), stop=(kc == DC - 1)),
                            reads=[w3b, self.xnc[kc]], writes=[pc])
                    bcol1 = 11 * ex + hc
                    kb.op("scalar", lambda e, bcol1=bcol1, pc=pc, cb=cb: e.activation(
                        cb[:, 0:512], pc[:, :], AF.Identity,
                        bias=b3c[:, bcol1:bcol1 + 1]),
                        reads=[pc, b3c], writes=[cb])
                    kb.op("vector", lambda e, bcol1=bcol1, pa=pa, cb=cb, ub=ub: e.scalar_tensor_tensor(
                        ub[:, 0:512], pa[:, :], b1c[:, bcol1:bcol1 + 1],
                        cb[:, 0:512], ALU.add, ALU.mult),
                        reads=[pa, b1c, cb], writes=[ub])
                    kb.op("scalar", lambda e, ub=ub, sbb=sbb: e.activation(
                        sbb[:, 0:512], ub[:, 0:512], AF.Silu),
                        reads=[ub], writes=[sbb])
                    kb.op("gpsimd", lambda e, hc=hc, sbb=sbb: e.tensor_tensor(
                        hs[:, 512 * hc:512 * (hc + 1)], sbb[:, 0:512], bcsb[:, :],
                        ALU.mult),
                        reads=[sbb, bcsb], writes=[hs])
                for dc in range(DC):
                    ps = self.ps[4 + dc % 3]
                    for kc in range(NHC):
                        kb.op("tensor", lambda e, kc=kc, dc=dc, ps=ps: e.matmul(
                            ps[:, :],
                            w2b[:, 1024 * kc + 128 * dc: 1024 * kc + 128 * dc + 128],
                            hs[:, 512 * kc:512 * (kc + 1)],
                            start=(kc == 0), stop=(kc == NHC - 1)),
                            reads=[w2b, hs], writes=[ps])
                    Xc = self.Xc[dc]
                    kb.op("vector", lambda e, ps=ps, toff=toff, Xc=Xc: e.scalar_tensor_tensor(
                        Xc[:, toff:toff + 512], ps[:, :], 0.5,
                        Xc[:, toff:toff + 512], ALU.mult, ALU.add),
                        reads=[ps, Xc], writes=[Xc])
        for qt in range(4):
            toff = 512 * qt
            for dc in range(DC):
                ps = self.ps[4 + dc % 3]
                kb.op("tensor", lambda e, dc=dc, toff=toff, ps=ps: e.matmul(
                    ps[:, :], b2s[0:9, 128 * dc:128 * (dc + 1)],
                    rwT9[0:9, toff:toff + 512], start=True, stop=True),
                    reads=[b2s, rwT9], writes=[ps])
                Xc = self.Xc[dc]
                kb.op("vector", lambda e, ps=ps, toff=toff, Xc=Xc: e.scalar_tensor_tensor(
                    Xc[:, toff:toff + 512], ps[:, :], 0.5,
                    Xc[:, toff:toff + 512], ALU.mult, ALU.add),
                    reads=[ps, Xc], writes=[Xc])
        return

    def attn1(self, wmap, cols):
        kb = self.kb
        bq, bk, bo, b1, b2, g1c, b1nc, g2c, b2nc, ones1 = (
            cols[k] for k in ("bq", "bk", "bo", "b1", "b2",
                              "g1", "b1n", "g2", "b2n", "ones1"))
        SC16 = 1.0 / 8.0
        wdep = wmap.get("deps", {})
        INV_SQRT_S = 1.0 / math.sqrt(T)
        wqkv = wmap["a_wqkv"]
        Vb = self.arena_win("a1_V", 0, 16384)
        wvb = self.arena_win("a1_wvb", 32768, DC * 256)
        Qb = self.arena_win("a1_Q", 32768, T)
        Kb = self.arena_win("a1_K", 32768 + T, T)
        bvrow = Buf(self._arena.h, "a1_bvrow", win=(1, 36864, 1024))
        bvrow.after_bufs = [o for (o2, n2, o) in self._arena_owners
                            if o2 < 36864 + 1024 and 36864 < o2 + n2]
        self._arena_owners.append((36864, 1024, bvrow))
        kb.bufs.append(bvrow)
        kb.dma("gpsimd", bvrow.ap_all, cols["bvrow_d"], writes=[bvrow])
        Pst0 = self.arena_win("a1_P0", 36864, T)
        vt = [kb.sbuf(f"a1_vt{i}", [128, 64], BF16) for i in range(2)]
        dscr = [kb.sbuf("a1_dscr0", [128, 16], F32),
                kb.sbuf("a1_dscr1", [128, 16], F32)]
        nrm = kb.sbuf("a1_nrm", [1, 16], F32)
        ebias = kb.sbuf("a1_eb", [128, 1], F32)
        attnb = self.arena_win("a1_attn", 16384, DC * T)
        sqq = self.xsq[0]
        wqkvr = wqkv.rearrange("(c p) o -> p c o", p=128)

        for vh in range(4):
            kb.dma("sync", wvb.ap_all,
                   wqkvr[:, :, 2048 + 256 * vh:2048 + 256 * (vh + 1)],
                   reads=wdep.get("a_wqkv", ()), writes=[wvb])
            for tc in range(16):
                ps = self.ps[tc % 2]
                for kc in range(DC):
                    kb.op("tensor", lambda e, tc=tc, kc=kc, ps=ps: e.matmul(
                        ps[:, 0:256], self.xnc[kc][:, 128 * tc:128 * (tc + 1)],
                        wvb[:, 256 * kc:256 * (kc + 1)],
                        start=(kc == 0), stop=False),
                        reads=[self.xnc[kc], wvb], writes=[ps])
                kb.op("tensor", lambda e, vh=vh, ps=ps: e.matmul(
                    ps[:, 0:256], ones1[0:1, 0:128], bvrow[0:1, 256 * vh:256 * (vh + 1)],
                    start=False, stop=True),
                    reads=[ones1, bvrow], writes=[ps])
                kb.op("vector", lambda e, tc=tc, vh=vh, ps=ps: e.tensor_copy(
                    Vb[:, 1024 * tc + 256 * vh: 1024 * tc + 256 * (vh + 1)], ps[:, 0:256]),
                    reads=[ps], writes=[Vb])

        for hp in range(DC):
            for which, dstb, bcol in ((0, Qb, bq), (1, Kb, bk)):
                wb = self.arena_win(f"a1_wqk_{hp}_{which}", 38912, DC * 128)
                kb.dma("sync", wb.ap_all,
                       wqkvr[:, :, 1024 * which + 128 * hp:1024 * which + 128 * (hp + 1)],
                       reads=wdep.get("a_wqkv", ()), writes=[wb],
                       sem_group="a1wqk")
                for tt in range(4):
                    ps = self.ps[2 + which]
                    for kc in range(DC):
                        kb.op("tensor", lambda e, kc=kc, tt=tt, ps=ps, wb=wb: e.matmul(
                            ps[:, :], wb[:, 128 * kc:128 * (kc + 1)],
                            self.xnc[kc][:, 512 * tt:512 * (tt + 1)],
                            start=(kc == 0), stop=(kc == DC - 1)),
                            reads=[wb, self.xnc[kc]], writes=[ps])
                    kb.op("scalar", lambda e, tt=tt, ps=ps, dstb=dstb, bcol=bcol, hp=hp: e.activation(
                        dstb[:, 512 * tt:512 * (tt + 1)], ps[:, :], AF.Identity,
                        bias=bcol[:, hp:hp + 1]),
                        reads=[ps, bcol], writes=[dstb])
            ps6 = self.ps[6]
            for which, srcb in ((0, Qb), (1, Kb)):
                for tt in range(4):
                    kb.op("scalar", lambda e, tt=tt, srcb=srcb: e.activation(
                        sqq[:, :], srcb[:, 512 * tt:512 * (tt + 1)], AF.Square),
                        reads=[srcb], writes=[sqq])
                    kb.op("tensor", lambda e: e.matmul(
                        ps6[0:1, 0:512], self.lnsel[:, 0:1], sqq[:, :],
                        start=True, stop=True),
                        reads=[self.lnsel, sqq], writes=[ps6])
                    kb.op("vector", lambda e, tt=tt, which=which: e.reduce_max(
                        nrm[0:1, 4 * which + tt:4 * which + tt + 1],
                        ps6[0:1, 0:512], axis=AX.X),
                        reads=[ps6], writes=[nrm])
            kb.op("vector", lambda e: e.reduce_max(
                nrm[0:1, 8:9], nrm[0:1, 0:4], axis=AX.X), reads=[nrm], writes=[nrm])
            kb.op("vector", lambda e: e.reduce_max(
                nrm[0:1, 9:10], nrm[0:1, 4:8], axis=AX.X), reads=[nrm], writes=[nrm])
            kb.op("vector", lambda e: e.tensor_tensor(
                nrm[0:1, 10:11], nrm[0:1, 8:9], nrm[0:1, 9:10], ALU.add),
                reads=[nrm], writes=[nrm])
            kb.op("vector", lambda e: e.tensor_scalar(
                nrm[0:1, 11:12], nrm[0:1, 10:11], -SC16 / 2.0, None, ALU.mult),
                reads=[nrm], writes=[nrm])
            kb.op("tensor", lambda e: e.matmul(
                ps6[:, 0:1], self.bcsel[0:1, 0:128], nrm[0:1, 11:12],
                start=True, stop=True),
                reads=[self.bcsel, nrm], writes=[ps6])
            kb.op("vector", lambda e: e.tensor_copy(
                ebias[:, 0:1], ps6[:, 0:1]), reads=[ps6], writes=[ebias])
            Pst1 = self.arena_win(f"a1_P1_{hp}", 38912, T)
            Pst = [Pst0, Pst1]
            psPV = [self.ps[2 + i] for i in range(4)]
            for sc in range(16):
                for h2 in range(2):
                    dsc = dscr[sc % 2]
                    pp = Pst[h2]
                    for tt in range(4):
                        pS = self.ps[[0, 1, 6, 7][2 * h2 + tt % 2]]
                        kb.op("tensor", lambda e, h2=h2, sc=sc, tt=tt, pS=pS: e.matmul(
                            pS[:, :],
                            Kb[64 * h2:64 * (h2 + 1), 128 * sc:128 * (sc + 1)],
                            Qb[64 * h2:64 * (h2 + 1), 512 * tt:512 * (tt + 1)],
                            start=True, stop=True, tile_position=(64 * h2, 0)),
                            reads=[Kb, Qb], writes=[pS])
                        kb.op("scalar", lambda e, h2=h2, tt=tt, pS=pS, pp=pp, dsc=dsc: e.activation(
                            pp[:, 512 * tt:512 * (tt + 1)], pS[:, :], AF.Exp,
                            bias=ebias[:, 0:1], scale=SC16,
                            accum_out=dsc[:, 4 * h2 + tt:4 * h2 + tt + 1]),
                            reads=[pS, ebias], writes=[pp, dsc])
                    kb.op("vector", lambda e, h2=h2, dsc=dsc: e.reduce_sum(
                        dsc[:, 8 + h2:9 + h2], dsc[:, 4 * h2:4 * h2 + 4], axis=AX.X),
                        reads=[dsc], writes=[dsc])
                    kb.op("vector", lambda e, h2=h2, dsc=dsc: e.reciprocal(
                        dsc[:, 10 + h2:11 + h2], dsc[:, 8 + h2:9 + h2]),
                        reads=[dsc], writes=[dsc])
                    kb.op("vector", lambda e, h2=h2, sc=sc, hp=hp, dsc=dsc: e.tensor_scalar(
                        vt[h2][:, :], Vb[:, 1024 * sc + 128 * hp + 64 * h2:
                                         1024 * sc + 128 * hp + 64 * (h2 + 1)],
                        dsc[:, 10 + h2:11 + h2], None, ALU.mult),
                        reads=[Vb, dsc], writes=[vt[h2]])
                    for tt in range(4):
                        kb.op("tensor", lambda e, h2=h2, sc=sc, tt=tt, pp=pp: e.matmul(
                            psPV[tt][64 * h2:64 * (h2 + 1), :], vt[h2][:, :],
                            pp[:, 512 * tt:512 * (tt + 1)],
                            start=(sc == 0 and h2 == 0), stop=(sc == 15 and h2 == 1),
                            tile_position=(0, 64 * h2), skip_group_check=True),
                            reads=[vt[h2], pp], writes=[psPV[tt]])
            for tt in range(4):
                g = self.scr(tt % 2)
                kb.op("scalar", lambda e, tt=tt, g=g: e.activation(
                    g[:, 0:512], psPV[tt][:, :], AF.Gelu),
                    reads=[psPV[tt]], writes=[g])
                kb.op("vector", lambda e, tt=tt, hp=hp, g=g: e.tensor_scalar(
                    attnb[:, hp * T + 512 * tt: hp * T + 512 * (tt + 1)],
                    g[:, 0:512], INV_SQRT_S, None, ALU.mult),
                    reads=[g], writes=[attnb])
        s1b = self.arena_win("a1_s1", 0, DC * T, after=[Vb])
        wsl = [self.arena_win("a1_wsl0", 32768, DC * 256),
               self.arena_win("a1_wsl1", 34816, DC * 256),
               self.arena_win("a1_wsl2", 36864, DC * 256)]
        w2ch = [self.arena_win(f"a1_w2ch{i}", 38912 + 1024 * i, 1024)
                for i in range(2)]
        wo = wmap["a_wo"]
        wor = wo.rearrange("(c p) o -> p c o", p=128)
        for oc in range(DC):
            wb = wsl[oc % 3]
            kb.dma("sync", wb[:, 0:DC * 128],
                   wor[:, :, 128 * oc:128 * (oc + 1)],
                   reads=wdep.get("a_wo", ()), writes=[wb])
            for tt in range(4):
                ps = self.ps[4 + (tt % 2)]
                for ac in range(DC):
                    kb.op("tensor", lambda e, oc=oc, tt=tt, ac=ac, ps=ps, wb=wb: e.matmul(
                        ps[:, :], wb[:, 128 * ac:128 * (ac + 1)],
                        attnb[:, ac * T + 512 * tt: ac * T + 512 * (tt + 1)],
                        start=(ac == 0), stop=(ac == DC - 1)),
                        reads=[wb, attnb], writes=[ps])
                sl = slice(oc * T + 512 * tt, oc * T + 512 * (tt + 1))
                kb.op("vector", lambda e, ps=ps, sl=sl, oc=oc, tt=tt: e.scalar_tensor_tensor(
                    s1b[:, sl], ps[:, :], bo[:, oc:oc + 1],
                    self.xnc[oc][:, 512 * tt:512 * (tt + 1)],
                    ALU.add, ALU.add),
                    reads=[ps, bo, self.xnc[oc]], writes=[s1b])
        self.layernorm(dst=self.xn, src=s1b, gain_bias=(g1c, b1nc))
        h1b = self.arena_win("a1_h1", 16384, 32 * 512, after=[attnb])
        w1 = wmap["a_w1"]
        w2 = wmap["a_w2"]
        w1r = w1.rearrange("(c p) o -> p c o", p=128)
        for ti in range(NTT):
            toff = 512 * ti
            for sl_i in range(16):
                wb = wsl[sl_i % 3]
                kb.dma("sync", wb.ap_all,
                       w1r[:, :, 256 * sl_i:256 * (sl_i + 1)],
                       reads=wdep.get("a_w1", ()), writes=[wb])
                for hh in range(2):
                    hc = 2 * sl_i + hh
                    ps = self.ps[hc % 4]
                    for kc in range(DC):
                        kb.op("tensor", lambda e, hh=hh, kc=kc, ps=ps, wb=wb, toff=toff: e.matmul(
                            ps[:, :],
                            wb[:, 256 * kc + 128 * hh: 256 * kc + 128 * hh + 128],
                            self.xnc[kc][:, toff:toff + 512],
                            start=(kc == 0), stop=(kc == DC - 1)),
                            reads=[wb, self.xnc[kc]], writes=[ps])
                    kb.op("scalar", lambda e, hc=hc, ps=ps: e.activation(
                        h1b[:, 512 * hc:512 * (hc + 1)], ps[:, :], AF.Gelu,
                        bias=b1[:, hc:hc + 1]),
                        reads=[ps, b1], writes=[h1b])
            for kc in range(32):
                wc = w2ch[kc % 2]
                kb.dma("sync", wc.ap_all,
                       w2[128 * kc:128 * (kc + 1), :],
                       reads=wdep.get("a_w2", ()), writes=[wc])
                for dc in range(DC):
                    ps = self.ps[dc]
                    kb.op("tensor", lambda e, wc=wc, kc=kc, dc=dc, ps=ps: e.matmul(
                        ps[:, :], wc[:, 128 * dc:128 * (dc + 1)],
                        h1b[:, 512 * kc:512 * (kc + 1)],
                        start=(kc == 0), stop=(kc == 31)),
                        reads=[wc, h1b], writes=[ps])
            for dc in range(DC):
                ps = self.ps[dc]
                sl = slice(dc * T + toff, dc * T + toff + 512)
                kb.op("vector", lambda e, ps=ps, sl=sl, dc=dc, toff=toff: e.scalar_tensor_tensor(
                    s1b[:, sl], ps[:, :], b2[:, dc:dc + 1],
                    self.xnc[dc][:, toff:toff + 512],
                    ALU.add, ALU.add),
                    reads=[ps, b2, self.xnc[dc]], writes=[s1b])
        ln2out = self.arena_win("a1_ln2o", 16384, DC * T, after=[h1b])
        self.layernorm(dst=ln2out, src=s1b, gain_bias=(g2c, b2nc))
        for c in range(DC):
            Xc = self.Xc[c]
            kb.op("vector", lambda e, c=c, Xc=Xc: e.tensor_tensor(
                Xc[:, 0:T], Xc[:, 0:T],
                ln2out[:, c * T:(c + 1) * T], ALU.add),
                reads=[Xc, ln2out], writes=[Xc])
        return


    def attn0(self, wmap, cols):
        kb = self.kb
        lq1, lk1, lq2, lk2, rmsw, ones1 = (
            cols[k] for k in ("lq1", "lk1", "lq2", "lk2", "rmsw", "ones1"))
        SCL = 1.0 / math.sqrt(32.0)
        wdep = wmap.get("deps", {})
        LAMBDA_INIT = 0.8 - 0.6 * math.exp(-0.3 * 12)
        wq, wkd, wv, wo = (wmap[k] for k in ("l0_wq", "l0_wkd", "l0_wv", "l0_wo"))
        Vb = self.arena_win("a0_V", 0, 16 * 512)
        attnb = self.arena_win("a0_attn", 8192, DC * T)
        Qc = self.arena_win("a0_Qc", 24576, T)
        Kc = self.arena_win("a0_Kc", 26624, T)
        raw = self.arena_win("a0_raw", 28672, 512)
        shf = self.arena_win("a0_shf", 29184, 512)
        v65 = self.arena_win("a0_v65", 29696, 16 * 65)
        wqb = self.arena_win("a0_wqb", 31776, DC * 128)
        wkb = self.arena_win("a0_wkb", 32800, DC * 128)
        wvb = self.arena_win("a0_wvb", 33824, DC * 256)
        Pbp = [self.arena_win("a0_Pb0", 30752, 1024),
               self.arena_win("a0_Pb1", 33824, 1024, after=[wvb]),
               self.arena_win("a0_Pb2", 34848, 1024, after=[wvb])]
        wob = [self.arena_win("a0_wob0", 31776, DC * 128)]
        ropec = self.arena_win("a0_ropec", 35872, T)
        ropes = self.arena_win("a0_ropes", 37920, T)
        cmask = self.arena_win("a0_cmask", 39968, 896)
        kb.dma("sync", ropec.ap_all, cols["ropec_d"], writes=[ropec])
        kb.dma("sync", ropes.ap_all, cols["ropes_d"], writes=[ropes])
        kb.dma("sync", cmask.ap_all, cols["cmask_d"], writes=[cmask])
        oc2 = [[self.scr(2), self.scr(3)], [self.scr(0), self.scr(1)]]
        _dbh = self.scr(4).h.bitcast(BF16)
        dbr2 = []
        for h2 in range(2):
            b = Buf(_dbh, f"a0_dbr{h2}", win=(0, 1, 512 * h2, 512))
            kb.bufs.append(b)
            dbr2.append([b, b])
        sqq2 = [self.xsq[0], self.xbb[0]]
        lsc = self.buf("a0_lsc", [1, 64], F32)
        lcol = self.buf("a0_lcol", [128, 1], F32)
        ebias = self.buf("a0_eb", [128, 1], F32)
        nrm = self.buf("a0_nrm", [1, 16], F32)
        sqq = sqq2[0]
        wqr = wq.rearrange("(c p) o -> p c o", p=128)
        wkr = wkd.rearrange("(c p) o -> p c o", p=128)
        wvr = wv.rearrange("(c p) o -> p c o", p=128)
        wor = wo.rearrange("(c p) o -> p c o", p=128)

        kb.op("vector", lambda e: e.tensor_tensor(
            lsc[0:1, 0:32], lq1[0:1, 0:32], lk1[0:1, 0:32], ALU.mult),
            reads=[lq1, lk1], writes=[lsc])
        kb.op("vector", lambda e: e.tensor_tensor(
            lsc[0:1, 32:64], lq2[0:1, 0:32], lk2[0:1, 0:32], ALU.mult),
            reads=[lq2, lk2], writes=[lsc])
        kb.op("vector", lambda e: e.reduce_sum(
            lsc[0:1, 0:1], lsc[0:1, 0:32], axis=AX.X), reads=[lsc], writes=[lsc])
        kb.op("vector", lambda e: e.reduce_sum(
            lsc[0:1, 1:2], lsc[0:1, 32:64], axis=AX.X), reads=[lsc], writes=[lsc])
        kb.op("scalar", lambda e: e.activation(
            lsc[0:1, 2:4], lsc[0:1, 0:2], AF.Exp), reads=[lsc], writes=[lsc])
        kb.op("vector", lambda e: e.tensor_tensor(
            lsc[0:1, 4:5], lsc[0:1, 2:3], lsc[0:1, 3:4], ALU.subtract),
            reads=[lsc], writes=[lsc])
        kb.op("vector", lambda e: e.tensor_scalar(
            lsc[0:1, 5:6], lsc[0:1, 4:5], LAMBDA_INIT, -1.0, ALU.add, ALU.mult),
            reads=[lsc], writes=[lsc])
        psL = self.ps[6]
        kb.op("tensor", lambda e: e.matmul(
            psL[:, 0:1], self.bcsel[0:1, 0:128], lsc[0:1, 5:6],
            start=True, stop=True), reads=[self.bcsel, lsc], writes=[psL])
        kb.op("vector", lambda e: e.tensor_copy(
            lcol[:, 0:1], psL[:, 0:1]), reads=[psL], writes=[lcol])

        for vh in range(2):
            kb.dma("sync", wvb.ap_all, wvr[:, :, 256 * vh:256 * (vh + 1)],
                   reads=wdep.get("l0_wv", ()), writes=[wvb])
            for tc in range(16):
                ps = self.ps[tc % 2]
                for kc in range(DC):
                    kb.op("tensor", lambda e, tc=tc, kc=kc, ps=ps: e.matmul(
                        ps[:, 0:256],
                        self.xnc[kc][:, 128 * tc:128 * (tc + 1)],
                        wvb[:, 256 * kc:256 * (kc + 1)],
                        start=(kc == 0), stop=(kc == DC - 1)),
                        reads=[self.xnc[kc], wvb], writes=[ps])
                kb.op("vector", lambda e, tc=tc, vh=vh, ps=ps: e.tensor_copy(
                    Vb[:, 512 * tc + 256 * vh:512 * tc + 256 * (vh + 1)],
                    ps[:, 0:256]),
                    reads=[ps], writes=[Vb])

        for hc in range(DC):
            for which, wsb, wsrc, dst in ((1, wkb, wkr, Kc), (0, wqb, wqr, Qc)):
                kb.dma("sync", wsb.ap_all,
                       wsrc[:, :, 128 * hc:128 * (hc + 1)],
                       reads=wdep.get("l0_wkd" if which else "l0_wq", ()),
                       writes=[wsb])
                for tt in range(4):
                    ps = self.ps[2 + which]
                    for kc in range(DC):
                        kb.op("tensor", lambda e, kc=kc, tt=tt, ps=ps, wsb=wsb: e.matmul(
                            ps[:, :], wsb[:, 128 * kc:128 * (kc + 1)],
                            self.xnc[kc][:, 512 * tt:512 * (tt + 1)],
                            start=(kc == 0), stop=(kc == DC - 1)),
                            reads=[wsb, self.xnc[kc]], writes=[ps])
                    tsl = slice(512 * tt, 512 * (tt + 1))
                    kb.op("scalar", lambda e, ps=ps: e.copy(
                        raw[:, 0:512], ps[:, :]), reads=[ps], writes=[raw])
                    mask = list(range(16, 32)) + list(range(0, 16))
                    kb.op("vector", lambda e, mask=mask: e.stream_shuffle(
                        shf[:, 0:512], raw[:, 0:512], mask), reads=[raw], writes=[shf])
                    kb.op("gpsimd", lambda e, tsl=tsl: e.tensor_tensor(
                        raw[:, 0:512], raw[:, 0:512], ropec[:, tsl], ALU.mult),
                        reads=[raw, ropec], writes=[raw])
                    kb.op("gpsimd", lambda e, tsl=tsl: e.tensor_tensor(
                        shf[:, 0:512], shf[:, 0:512], ropes[:, tsl], ALU.mult),
                        reads=[shf, ropes], writes=[shf])
                    kb.op("vector", lambda e, tsl=tsl, dst=dst: e.tensor_tensor(
                        dst[:, tsl], raw[:, 0:512], shf[:, 0:512], ALU.add),
                        reads=[raw, shf], writes=[dst])
            ps6 = self.ps[6]
            for which, srcb in ((0, Qc), (1, Kc)):
                for tt in range(4):
                    kb.op("scalar", lambda e, tt=tt, srcb=srcb: e.activation(
                        sqq[:, :], srcb[:, 512 * tt:512 * (tt + 1)], AF.Square),
                        reads=[srcb], writes=[sqq])
                    kb.op("tensor", lambda e: e.matmul(
                        ps6[0:1, 0:512], self.lnsel[:, 0:1], sqq[:, :],
                        start=True, stop=True),
                        reads=[self.lnsel, sqq], writes=[ps6])
                    kb.op("vector", lambda e, tt=tt, which=which: e.reduce_max(
                        nrm[0:1, 4 * which + tt:4 * which + tt + 1],
                        ps6[0:1, 0:512], axis=AX.X),
                        reads=[ps6], writes=[nrm])
            kb.op("vector", lambda e: e.reduce_max(
                nrm[0:1, 8:9], nrm[0:1, 0:4], axis=AX.X), reads=[nrm], writes=[nrm])
            kb.op("vector", lambda e: e.reduce_max(
                nrm[0:1, 9:10], nrm[0:1, 4:8], axis=AX.X), reads=[nrm], writes=[nrm])
            kb.op("vector", lambda e: e.tensor_tensor(
                nrm[0:1, 10:11], nrm[0:1, 8:9], nrm[0:1, 9:10], ALU.add),
                reads=[nrm], writes=[nrm])
            kb.op("vector", lambda e: e.tensor_scalar(
                nrm[0:1, 11:12], nrm[0:1, 10:11], -SCL / 2.0, None, ALU.mult),
                reads=[nrm], writes=[nrm])
            kb.op("tensor", lambda e: e.matmul(
                ps6[:, 0:1], self.bcsel[0:1, 0:128], nrm[0:1, 11:12],
                start=True, stop=True), reads=[self.bcsel, nrm], writes=[ps6])
            kb.op("vector", lambda e: e.tensor_copy(
                ebias[:, 0:1], ps6[:, 0:1]), reads=[ps6], writes=[ebias])
            for sc in range(16):
                kb.op("vector", lambda e, sc=sc, hc=hc: e.tensor_copy(
                    v65[:, 65 * sc:65 * sc + 64],
                    Vb[:, 512 * sc + 64 * hc: 512 * sc + 64 * (hc + 1)]),
                    reads=[Vb], writes=[v65])
                kb.op("vector", lambda e, sc=sc: e.memset(
                    v65[:, 65 * sc + 64:65 * (sc + 1)], 1.0), writes=[v65])
            psS = [self.ps[i] for i in range(4)]
            psPV = [self.ps[4 + i] for i in range(4)]
            for tt in range(4):
                nsc = 4 * (tt + 1)
                for sc in range(nsc):
                    for u in range(4):
                        pS = psS[u]
                        kb.op("tensor", lambda e, sc=sc, tt=tt, u=u, pS=pS: e.matmul(
                            pS[:, :],
                            Kc[32 * u:32 * (u + 1), 128 * sc:128 * (sc + 1)],
                            Qc[32 * u:32 * (u + 1), 512 * tt:512 * (tt + 1)],
                            start=True, stop=True, tile_position=(32 * u, 0)),
                            reads=[Kc, Qc], writes=[pS])
                    diag = 512 * tt - 128 * sc
                    for j in range(2):
                        pbp = Pbp[(2 * sc + j) % 3]
                        kb.op("scalar", lambda e, j=j, pbp=pbp: e.activation(
                            pbp[:, 0:1024], self.psd[j][:, 0:1024], AF.Exp,
                            bias=ebias[:, 0:1], scale=SCL),
                            reads=[psS[2 * j], psS[2 * j + 1], ebias], writes=[pbp])
                        for i, u in ((0, 2 * j), (1, 2 * j + 1)):
                            if diag < 127:
                                kb.op("gpsimd", lambda e, pbp=pbp, i=i, diag=diag:
                                    e.tensor_tensor(
                                        pbp[:, 512 * i:512 * (i + 1)],
                                        pbp[:, 512 * i:512 * (i + 1)],
                                        cmask[:, diag + 384:diag + 384 + 512],
                                        ALU.mult),
                                    reads=[pbp, cmask], writes=[pbp])
                            kb.op("tensor", lambda e, u=u, i=i, sc=sc, nsc=nsc, pbp=pbp:
                                e.matmul(
                                    psPV[u][0:65, :], v65[:, 65 * sc:65 * (sc + 1)],
                                    pbp[:, 512 * i:512 * (i + 1)],
                                    start=(sc == 0), stop=(sc == nsc - 1)),
                                reads=[v65, pbp], writes=[psPV[u]])
                chains = []
                for h2 in range(2):
                    u0, u1 = 2 * h2, 2 * h2 + 1
                    bc0, bc1 = psPV[u0], psPV[u1]
                    oc = oc2[h2]
                    sqh = sqq2[h2]
                    dbr = dbr2[h2]
                    ch = []
                    for i, u, bc in ((0, u0, bc0), (1, u1, bc1)):
                        def _rcp(e, i=i, u=u, dbr=dbr):
                            with e.bass.allow_low_precision(reason="1/d bf16 ok"):
                                return e.reciprocal(
                                    dbr[i][0:1, 0:512], psPV[u][64:65, :])
                        ch.append(("vector", _rcp, [psPV[u]], [dbr[i]]))
                        ch.append(("scalar", lambda e, u=u, oci=oc[i]: e.copy(
                            oci[0:64, 0:512], psPV[u][0:64, :]),
                            [psPV[u]], [oc[i]]))
                        ch.append(("tensor", lambda e, bc=bc, db=dbr[i]: e.matmul(
                            bc[:, :], ones1[0:1, 0:128], db[0:1, 0:512],
                            start=True, stop=True),
                            [ones1, dbr[i]], [bc]))
                    ch.append(("vector", lambda e, oc0=oc[0], bc0=bc0:
                        e.tensor_tensor(
                            oc0[0:64, 0:512], oc0[0:64, 0:512], bc0[0:64, :],
                            ALU.mult),
                        [oc[0], bc0], [oc[0]]))
                    ch.append(("vector", lambda e, oc1=oc[1], bc1=bc1:
                        e.tensor_tensor(
                            oc1[0:64, 0:512], oc1[0:64, 0:512], bc1[0:64, :],
                            ALU.mult),
                        [oc[1], bc1], [oc[1]]))
                    ch.append(("vector", lambda e, oc0=oc[0], oc1=oc[1]:
                        e.scalar_tensor_tensor(
                            oc0[0:64, 0:512], oc1[0:64, 0:512], lcol[0:64, 0:1],
                            oc0[0:64, 0:512], ALU.mult, ALU.add),
                        [oc[1], lcol, oc[0]], [oc[0]]))
                    ch.append(("scalar", lambda e, sqh=sqh, oc0=oc[0]: e.activation(
                        sqh[0:64, :], oc0[0:64, 0:512], AF.Square),
                        [oc[0]], [sqh]))
                    ch.append(("tensor", lambda e, bc0=bc0, sqh=sqh: e.matmul(
                        bc0[0:1, :], self.lnsel[0:64, 0:1], sqh[0:64, :],
                        start=True, stop=True),
                        [self.lnsel, sqh], [bc0]))
                    epsc = self.const_col(1e-6)
                    ch.append(("scalar", lambda e, bc0=bc0, epsc=epsc: e.activation(
                        bc0[0:1, :], bc0[0:1, :], AF.Ln,
                        bias=epsc[0:1, 0:1], scale=1.0 / 64.0),
                        [bc0, self.ccol], [bc0]))
                    ch.append(("scalar", lambda e, bc0=bc0, db=dbr[0]: e.activation(
                        db[0:1, 0:512], bc0[0:1, :], AF.Exp, scale=-0.5),
                        [bc0], [dbr[0]]))
                    ch.append(("tensor", lambda e, bc1=bc1, db=dbr[0]: e.matmul(
                        bc1[:, :], ones1[0:1, 0:128], db[0:1, 0:512],
                        start=True, stop=True),
                        [ones1, dbr[0]], [bc1]))
                    ch.append(("vector", lambda e, oc0=oc[0], bc1=bc1:
                        e.tensor_tensor(
                            oc0[0:64, 0:512], oc0[0:64, 0:512], bc1[0:64, :],
                            ALU.mult),
                        [oc[0], bc1], [oc[0]]))
                    osl = slice(hc * T + 512 * tt, hc * T + 512 * (tt + 1))
                    ch.append(("vector", lambda e, h2=h2, osl=osl, oc0=oc[0]:
                        e.tensor_scalar(
                            attnb[64 * h2:64 * (h2 + 1), osl], oc0[0:64, 0:512],
                            rmsw[0:64, 0:1], 1.0 - LAMBDA_INIT, ALU.mult, ALU.mult),
                        [oc[0], rmsw], [attnb]))
                    chains.append(ch)
                for pair in itertools.zip_longest(*chains):
                    for item in pair:
                        if item is not None:
                            eng, emit, rd, wr = item
                            kb.op(eng, emit, reads=rd, writes=wr)
        for oc2 in range(DC):
            wb = wob[0]
            kb.dma("sync", wb.ap_all,
                   wor[:, :, 128 * oc2:128 * (oc2 + 1)],
                   reads=wdep.get("l0_wo", ()), writes=[wb])
            for tt in range(4):
                ps = self.ps[4 + (tt % 2)]
                for ac in range(DC):
                    kb.op("tensor", lambda e, tt=tt, ac=ac, ps=ps, wb=wb: e.matmul(
                        ps[:, :], wb[:, 128 * ac:128 * (ac + 1)],
                        attnb[:, ac * T + 512 * tt: ac * T + 512 * (tt + 1)],
                        start=(ac == 0), stop=(ac == DC - 1)),
                        reads=[wb, attnb], writes=[ps])
                Xc = self.Xc[oc2]
                kb.op("vector", lambda e, ps=ps, tt=tt, Xc=Xc: e.tensor_tensor(
                    Xc[:, 512 * tt:512 * (tt + 1)], Xc[:, 512 * tt:512 * (tt + 1)],
                    ps[:, :], ALU.add),
                    reads=[ps, Xc], writes=[Xc])
        return



def build_ffn1_program():
    kb = KB()
    xT = kb.dram_in("xT", [D, T])
    w1 = kb.dram_in("f1_w1", [1024, 4096])
    w2 = kb.dram_in("f1_w2", [4096, 1024])
    b1cd = kb.dram_in("f1_b1c", [128, 32])
    b2cd = kb.dram_in("f1_b2c", [128, 8])
    clnsel = kb.dram_in("c_lnsel", [128, 16 * NTT], BF16)
    cbcsel = kb.dram_in("c_bcsel", [16, 128 * NTT])
    yT = kb.dram_out("yT", [D, T])

    m = Model(kb)
    b1c = kb.sbuf("b1c", [128, 32], F32)
    b2c = kb.sbuf("b2c", [128, 8], F32)
    _xTr = xT.rearrange("(c p) t -> p c t", p=128)
    for _c in range(DC):
        kb.dma("sync", m.Xc[_c][:, 0:T], _xTr[:, _c, :], writes=[m.Xc[_c]])
    kb.dma("sync", m.lnsel.ap_all, clnsel, writes=[m.lnsel])
    kb.dma("sync", m.bcsel.ap_all, cbcsel, writes=[m.bcsel])
    kb.dma("sync", b1c.ap_all, b1cd, writes=[b1c])
    kb.dma("sync", b2c.ap_all, b2cd, writes=[b2c])
    m.layernorm()
    m.ffn1({"w1": w1, "w2": w2}, b1c, b2c)
    yTr = yT.rearrange("(c p) t -> p c t", p=128)
    for c in range(DC):
        kb.dma("sync", yTr[:, c, :], m.Xc[c][:, 0:T], reads=[m.Xc[c]])
    kb.finalize()
    return kb


def prep_ffn1_inputs(w1, b1, w2, b2):
    return {
        "f1_w1": np.ascontiguousarray(w1, dtype=np.float32),
        "f1_w2": np.ascontiguousarray(w2, dtype=np.float32),
        "f1_b1c": np.ascontiguousarray(np.asarray(b1, np.float32).reshape(32, 128).T),
        "f1_b2c": np.ascontiguousarray(np.asarray(b2, np.float32).reshape(8, 128).T),
    }

    def _pwl(self, psm, b1c, hc, hbuf, hoff):
        kb = self.kb
        nt, ncl, jj, tt_ = self.scr(0), self.scr(1), self.scr(2), self.scr(3)
        g0, g1 = self.scr(4), self.scr(2)
        dd = self.scr(1)
        yy = self.scr(3)
        msk = self.mscr
        p0, p1 = psm
        kb.op("vector", lambda e: e.tensor_scalar(
            nt[:, 0:512], p0.h[:, :], b1c[:, hc:hc + 1], 256.0, ALU.add, ALU.mult),
            reads=[p0, b1c], writes=[nt])
        kb.op("vector", lambda e: e.tensor_scalar(
            nt[:, 512:1024], p1.h[:, :], b1c[:, hc:hc + 1], 256.0, ALU.add, ALU.mult),
            reads=[p1, b1c], writes=[nt])
        kb.op("vector", lambda e: e.tensor_scalar(
            nt[:, :], nt[:, :], MAGIC, MAGIC, ALU.add, ALU.subtract),
            reads=[nt], writes=[nt])
        kb.op("vector", lambda e: e.tensor_scalar(
            ncl[:, :], nt[:, :], -1024.0, 1024.0, ALU.max, ALU.min),
            reads=[nt], writes=[ncl])
        kb.op("vector", lambda e: e.tensor_scalar(
            ncl[:, :], ncl[:, :], 1.0 / 64.0, 63.0 / 128.0, ALU.mult, ALU.subtract),
            reads=[ncl], writes=[ncl])
        kb.op("vector", lambda e: e.tensor_scalar(
            jj[:, :], ncl[:, :], MAGIC, MAGIC, ALU.add, ALU.subtract),
            reads=[ncl], writes=[jj])
        kb.op("vector", lambda e: e.tensor_scalar(
            jj[:, :], jj[:, :], 15.0, None, ALU.min),
            reads=[jj], writes=[jj])
        kb.op("vector", lambda e: e.tensor_scalar(
            ncl[:, :], ncl[:, :], 63.0 / 128.0, None, ALU.add),
            reads=[ncl], writes=[ncl])
        kb.op("gpsimd", lambda e: e.tensor_tensor(
            tt_[:, :], ncl[:, :], jj[:, :], ALU.subtract),
            reads=[ncl, jj], writes=[tt_])
        kb.op("scalar", lambda e: e.activation(
            g0[:, :], jj[:, :], AF.Gelu, bias=0.0, scale=0.25),
            reads=[jj], writes=[g0])
        kb.op("scalar", lambda e: e.activation(
            g1[:, :], jj[:, :], AF.Gelu, bias=0.25, scale=0.25),
            reads=[jj], writes=[g1])
        kb.op("gpsimd", lambda e: e.tensor_tensor(
            dd[:, :], g1[:, :], g0[:, :], ALU.subtract),
            reads=[g1, g0], writes=[dd])
        kb.op("gpsimd", lambda e: e.tensor_tensor(
            dd[:, :], tt_[:, :], dd[:, :], ALU.mult),
            reads=[tt_, dd], writes=[dd])
        kb.op("gpsimd", lambda e: e.tensor_tensor(
            yy[:, :], dd[:, :], g0[:, :], ALU.add),
            reads=[dd, g0], writes=[yy])
        kb.op("vector", lambda e: e.tensor_scalar(
            yy[:, :], yy[:, :], 256.0, MAGIC, ALU.mult, ALU.add),
            reads=[yy], writes=[yy])
        kb.op("vector", lambda e: e.tensor_scalar(
            yy[:, :], yy[:, :], MAGIC, None, ALU.subtract),
            reads=[yy], writes=[yy])
        kb.op("vector", lambda e: e.tensor_scalar(
            msk[:, :], nt[:, :], 1024.0, None, ALU.is_gt),
            reads=[nt], writes=[msk])
        kb.op("vector", lambda e: e.select(
            yy[:, :], msk[:, :], nt[:, :], yy[:, :]),
            reads=[msk, nt, yy], writes=[yy])
        kb.op("vector", lambda e: e.tensor_scalar(
            hbuf[:, hoff:hoff + 1024], yy[:, :], 1.0 / 256.0, None, ALU.mult),
            reads=[yy], writes=[hbuf])


def build_moe_program():
    kb = KB()
    xT = kb.dram_in("xT", [D, T])
    w1 = kb.dram_in("m_w1", [9 * 1024, 1408])
    w3 = kb.dram_in("m_w3", [9 * 1024, 1408])
    w2 = kb.dram_in("m_w2", [9 * 1408, 1024])
    b1cd = kb.dram_in("m_b1c", [128, 99])
    b3cd = kb.dram_in("m_b3c", [128, 99])
    b2sd = kb.dram_in("m_b2s", [9, 1024])
    gwtd = kb.dram_in("g_gwt", [128, 64])
    gbtd = kb.dram_in("g_gbt", [128, 8])
    id128d = kb.dram_in("c_id128", [128, 128])
    eseld = kb.dram_in("c_esel", [9, 9 * 128], BF16)
    clnsel = kb.dram_in("c_lnsel", [128, 16 * NTT], BF16)
    cbcsel = kb.dram_in("c_bcsel", [16, 128 * NTT])
    yT = kb.dram_out("yT", [D, T])

    m = Model(kb)
    b1c = kb.sbuf("b1c", [128, 99], F32)
    b3c = kb.sbuf("b3c", [128, 99], F32)
    b2s = kb.sbuf("b2s", [9, 1024], BF16)
    gwt = kb.sbuf("gwt", [128, 64], BF16)
    gbt = kb.sbuf("gbt", [128, 8], F32)
    id128 = kb.sbuf("id128", [128, 128], F32)
    esel = kb.sbuf("esel", [9, 9 * 128], BF16)
    _xTr = xT.rearrange("(c p) t -> p c t", p=128)
    for _c in range(DC):
        kb.dma("sync", m.Xc[_c][:, 0:T], _xTr[:, _c, :], writes=[m.Xc[_c]])
    kb.dma("sync", m.lnsel.ap_all, clnsel, writes=[m.lnsel])
    kb.dma("sync", m.bcsel.ap_all, cbcsel, writes=[m.bcsel])
    kb.dma("sync", b1c.ap_all, b1cd, writes=[b1c])
    kb.dma("sync", b3c.ap_all, b3cd, writes=[b3c])
    kb.dma("gpsimd", b2s.ap_all, b2sd, writes=[b2s])
    kb.dma("gpsimd", gwt.ap_all, gwtd, writes=[gwt])
    kb.dma("sync", gbt.ap_all, gbtd, writes=[gbt])
    kb.dma("sync", id128.ap_all, id128d, writes=[id128])
    kb.dma("sync", esel.ap_all, eseld, writes=[esel])
    m.layernorm()
    m.moe({"m_w1": w1, "m_w3": w3, "m_w2": w2}, b1c, b3c,
          {"gwt": gwt, "gbt": gbt, "b2s": b2s, "id128": id128, "esel": esel})
    yTr = yT.rearrange("(c p) t -> p c t", p=128)
    for c in range(DC):
        kb.dma("sync", yTr[:, c, :], m.Xc[c][:, 0:T], reads=[m.Xc[c]])
    kb.finalize()
    return kb


def _esel_const():
    import ml_dtypes
    esel = np.zeros((9, 9 * 128), dtype=ml_dtypes.bfloat16)
    for e in range(9):
        esel[e, 128 * e:128 * (e + 1)] = 1.0
    return esel


def prep_moe_inputs(gate_w, gate_b, w1, b1, w3, b3, w2, b2, sw1, sb1, sw3, sb3, sw2, sb2):
    f = np.float32
    w1a = np.concatenate([np.asarray(w1, f).reshape(8 * 1024, 1408),
                          np.asarray(sw1, f)], axis=0)
    w3a = np.concatenate([np.asarray(w3, f).reshape(8 * 1024, 1408),
                          np.asarray(sw3, f)], axis=0)
    w2a = np.concatenate([np.asarray(w2, f).reshape(8 * 1408, 1024),
                          np.asarray(sw2, f)], axis=0)
    b1a = np.concatenate([np.asarray(b1, f), np.asarray(sb1, f)[None]], axis=0)
    b3a = np.concatenate([np.asarray(b3, f), np.asarray(sb3, f)[None]], axis=0)
    b2a = np.concatenate([np.asarray(b2, f), np.asarray(sb2, f)[None]], axis=0)
    b1c = np.concatenate([b1a[e].reshape(11, 128).T for e in range(9)], axis=1)
    b3c = np.concatenate([b3a[e].reshape(11, 128).T for e in range(9)], axis=1)
    gw = np.asarray(gate_w, f)
    gwt = np.zeros((128, 64), f)
    for kc in range(8):
        gwt[:, 8 * kc:8 * (kc + 1)] = gw[:, 128 * kc:128 * (kc + 1)].T
    gbt = np.broadcast_to(np.asarray(gate_b, f)[None, :], (128, 8)).copy()
    return {
        "m_w1": np.ascontiguousarray(w1a), "m_w3": np.ascontiguousarray(w3a),
        "m_w2": np.ascontiguousarray(w2a),
        "m_b1c": np.ascontiguousarray(b1c), "m_b3c": np.ascontiguousarray(b3c),
        "m_b2s": np.ascontiguousarray(b2a),
        "g_gwt": gwt, "g_gbt": gbt,
        "c_id128": np.eye(128, dtype=f),
        "c_esel": _esel_const(),
    }


def build_attn1_program():
    kb = KB()
    xT = kb.dram_in("xT", [D, T])
    wqkv = kb.dram_in("a_wqkv", [1024, 3072])
    wo = kb.dram_in("a_wo", [1024, 1024])
    w1 = kb.dram_in("a_w1", [1024, 4096])
    w2 = kb.dram_in("a_w2", [4096, 1024])
    colsd = {}
    for nm, sh in (("a_bq", [128, 8]), ("a_bk", [128, 8]), ("a_bo", [128, 8]),
                   ("a_b1", [128, 32]), ("a_b2", [128, 8]),
                   ("a_g1", [128, 8]), ("a_b1n", [128, 8]),
                   ("a_g2", [128, 8]), ("a_b2n", [128, 8])):
        colsd[nm] = kb.dram_in(nm, sh)
    bvrowd = kb.dram_in("a_bvrow", [1, 1024])
    ones1d = kb.dram_in("c_ones1", [1, 128])
    clnsel = kb.dram_in("c_lnsel", [128, 16 * NTT], BF16)
    cbcsel = kb.dram_in("c_bcsel", [16, 128 * NTT])
    yT = kb.dram_out("yT", [D, T])

    m = Model(kb)
    cols = {}
    for nm, key in (("a_bq", "bq"), ("a_bk", "bk"), ("a_bo", "bo"),
                    ("a_b1", "b1"), ("a_b2", "b2"), ("a_g1", "g1"),
                    ("a_b1n", "b1n"), ("a_g2", "g2"), ("a_b2n", "b2n")):
        b = kb.sbuf("sb_" + nm, list(colsd[nm].shape), F32)
        kb.dma("sync", b.ap_all, colsd[nm], writes=[b])
        cols[key] = b
    cols["bvrow_d"] = bvrowd
    ones1 = kb.sbuf("sb_ones1", [1, 128], BF16)
    kb.dma("gpsimd", ones1.ap_all, ones1d, writes=[ones1])
    cols["ones1"] = ones1
    _xTr = xT.rearrange("(c p) t -> p c t", p=128)
    for _c in range(DC):
        kb.dma("sync", m.Xc[_c][:, 0:T], _xTr[:, _c, :], writes=[m.Xc[_c]])
    kb.dma("sync", m.lnsel.ap_all, clnsel, writes=[m.lnsel])
    kb.dma("sync", m.bcsel.ap_all, cbcsel, writes=[m.bcsel])
    m.layernorm()
    m.attn1({"a_wqkv": wqkv, "a_wo": wo, "a_w1": w1, "a_w2": w2}, cols)
    yTr = yT.rearrange("(c p) t -> p c t", p=128)
    for c in range(DC):
        kb.dma("sync", yTr[:, c, :], m.Xc[c][:, 0:T], reads=[m.Xc[c]])
    kb.finalize()
    return kb


def prep_attn1_inputs(in_proj_w, in_proj_b, out_w, out_b, lin1_w, lin1_b,
                      lin2_w, lin2_b, ln1_g, ln1_b, ln2_g, ln2_b):
    f = np.float32
    import ml_dtypes
    col = lambda v, n: np.ascontiguousarray(np.asarray(v, f).reshape(n, 128).T)
    return {
        "a_wqkv": np.ascontiguousarray(np.asarray(in_proj_w, f).T),
        "a_wo": np.ascontiguousarray(np.asarray(out_w, f).T),
        "a_w1": np.ascontiguousarray(np.asarray(lin1_w, f).T),
        "a_w2": np.ascontiguousarray(np.asarray(lin2_w, f).T),
        "a_bq": col(np.asarray(in_proj_b, f)[0:1024], 8),
        "a_bk": col(np.asarray(in_proj_b, f)[1024:2048], 8),
        "a_bvrow": np.asarray(in_proj_b, f)[None, 2048:3072].copy(),
        "a_bo": col(out_b, 8),
        "a_b1": col(lin1_b, 32),
        "a_b2": col(lin2_b, 8),
        "a_g1": col(ln1_g, 8), "a_b1n": col(ln1_b_ := ln1_b, 8),
        "a_g2": col(ln2_g, 8), "a_b2n": col(ln2_b, 8),
        "c_ones1": np.ones((1, 128), f),
    }


def build_attn0_program():
    kb = KB()
    xT = kb.dram_in("xT", [D, T])
    wq = kb.dram_in("l0_wq", [1024, 1024])
    wkd = kb.dram_in("l0_wkd", [1024, 1024])
    wv = kb.dram_in("l0_wv", [1024, 512])
    wo = kb.dram_in("l0_wo", [1024, 1024])
    lqd = {nm: kb.dram_in(nm, [1, 32]) for nm in ("l0_lq1", "l0_lk1", "l0_lq2", "l0_lk2")}
    rmswd = kb.dram_in("l0_rmsw", [128, 1])
    ropecd = kb.dram_in("c_ropec", [128, T], BF16)
    ropesd = kb.dram_in("c_ropes", [128, T], BF16)
    cmaskd = kb.dram_in("c_cmask", [128, 896], BF16)
    ones1d = kb.dram_in("c_ones1", [1, 128])
    clnsel = kb.dram_in("c_lnsel", [128, 16 * NTT], BF16)
    cbcsel = kb.dram_in("c_bcsel", [16, 128 * NTT])
    yT = kb.dram_out("yT", [D, T])

    m = Model(kb)
    cols = {}
    for nm, key in (("l0_lq1", "lq1"), ("l0_lk1", "lk1"),
                    ("l0_lq2", "lq2"), ("l0_lk2", "lk2")):
        b = kb.sbuf("sb_" + nm, [1, 32], F32)
        kb.dma("sync", b.ap_all, lqd[nm], writes=[b])
        cols[key] = b
    rmsw = kb.sbuf("sb_rmsw", [128, 1], F32)
    kb.dma("sync", rmsw.ap_all, rmswd, writes=[rmsw])
    cols["rmsw"] = rmsw
    cols["ropec_d"] = ropecd
    cols["ropes_d"] = ropesd
    cols["cmask_d"] = cmaskd
    ones1 = kb.sbuf("sb_ones1", [1, 128], BF16)
    kb.dma("gpsimd", ones1.ap_all, ones1d, writes=[ones1])
    cols["ones1"] = ones1
    _xTr = xT.rearrange("(c p) t -> p c t", p=128)
    for _c in range(DC):
        kb.dma("sync", m.Xc[_c][:, 0:T], _xTr[:, _c, :], writes=[m.Xc[_c]])
    kb.dma("sync", m.lnsel.ap_all, clnsel, writes=[m.lnsel])
    kb.dma("sync", m.bcsel.ap_all, cbcsel, writes=[m.bcsel])
    m.layernorm()
    m.attn0({"l0_wq": wq, "l0_wkd": wkd, "l0_wv": wv, "l0_wo": wo}, cols)
    yTr = yT.rearrange("(c p) t -> p c t", p=128)
    for c in range(DC):
        kb.dma("sync", yTr[:, c, :], m.Xc[c][:, 0:T], reads=[m.Xc[c]])
    kb.finalize()
    return kb


def rope_consts():
    import ml_dtypes
    bf = ml_dtypes.bfloat16
    HD = 32
    inv = 1.0 / 10000.0 ** (np.arange(0, HD, 2, dtype=np.float32) / HD)
    t = np.arange(T, dtype=np.float32)
    fr = np.outer(t, inv)
    cosv, sinv = np.cos(fr), np.sin(fr)
    C = np.zeros((128, T), np.float32)
    S = np.zeros((128, T), np.float32)
    for p in range(128):
        C[p] = cosv[:, p % 16]
        S[p] = sinv[:, p % 16] * (1.0 if (p % 32) < 16 else -1.0)
    cm = np.zeros((128, 896), np.float32)
    for p in range(128):
        cm[p] = (p <= np.arange(896) - 384)
    return {"c_ropec": C.astype(bf), "c_ropes": S.astype(bf),
            "c_cmask": cm.astype(bf)}


def prep_attn0_inputs(Wq, Wk, Wv, Wo, lq1, lk1, lq2, lk2, rms_w):
    f = np.float32
    Wk = np.asarray(Wk, f)
    wkd = np.zeros((1024, 1024), f)
    for hc in range(8):
        wkd[:, 128 * hc:128 * hc + 64] = Wk[:, 64 * hc:64 * (hc + 1)]
        wkd[:, 128 * hc + 64:128 * (hc + 1)] = Wk[:, 64 * hc:64 * (hc + 1)]
    return {
        "l0_wq": np.ascontiguousarray(np.asarray(Wq, f)),
        "l0_wkd": wkd,
        "l0_wv": np.ascontiguousarray(np.asarray(Wv, f)),
        "l0_wo": np.ascontiguousarray(np.asarray(Wo, f)),
        "l0_lq1": np.asarray(lq1, f)[None, :].copy(),
        "l0_lk1": np.asarray(lk1, f)[None, :].copy(),
        "l0_lq2": np.asarray(lq2, f)[None, :].copy(),
        "l0_lk2": np.asarray(lk2, f)[None, :].copy(),
        "l0_rmsw": np.ascontiguousarray(np.tile(np.asarray(rms_w, f), 2)[:, None]),
        "c_ones1": np.ones((1, 128), f),
    }



_PROG_CACHE = {}


def _prog(name, builder):
    if name not in _PROG_CACHE:
        _PROG_CACHE[name] = builder()
    return _PROG_CACHE[name]


def kernel(**inp):
    consts = {**host_constants(), **rope_consts(),
              "c_id128": np.eye(128, dtype=np.float32),
              "c_esel": _esel_const(),
              "c_ones1": np.ones((1, 128), np.float32)}
    moe_w = prep_moe_inputs(
        inp["L0_ffn_gate_w"], inp["L0_ffn_gate_b"], inp["L0_ffn_w1"],
        inp["L0_ffn_b1"], inp["L0_ffn_w3"], inp["L0_ffn_b3"], inp["L0_ffn_w2"],
        inp["L0_ffn_b2"], inp["L0_ffn_sw1"], inp["L0_ffn_sb1"],
        inp["L0_ffn_sw3"], inp["L0_ffn_sb3"], inp["L0_ffn_sw2"],
        inp["L0_ffn_sb2"])
    a0_w = prep_attn0_inputs(
        inp["L0_attn_Wq"], inp["L0_attn_Wk"], inp["L0_attn_Wv"],
        inp["L0_attn_Wo"], inp["L0_attn_lambda_q1"], inp["L0_attn_lambda_k1"],
        inp["L0_attn_lambda_q2"], inp["L0_attn_lambda_k2"], inp["L0_attn_rms_w"])
    f1_w = prep_ffn1_inputs(inp["L1_ffn_w1"], inp["L1_ffn_b1"],
                            inp["L1_ffn_w2"], inp["L1_ffn_b2"])
    a1_w = prep_attn1_inputs(
        inp["L1_attn_in_proj_w"], inp["L1_attn_in_proj_b"],
        inp["L1_attn_out_w"], inp["L1_attn_out_b"], inp["L1_attn_lin1_w"],
        inp["L1_attn_lin1_b"], inp["L1_attn_lin2_w"], inp["L1_attn_lin2_b"],
        inp["L1_attn_ln1_g"], inp["L1_attn_ln1_b"], inp["L1_attn_ln2_g"],
        inp["L1_attn_ln2_b"])
    allw = {**consts, **moe_w, **a0_w, **f1_w, **a1_w}

    x = np.asarray(inp["L0_attn_x"], np.float32)
    cur = [np.ascontiguousarray(x[c].T) for c in range(8)]
    kb = _prog("full", build_full_program)
    names = [n for n in kb.input_names if n != "xT"]
    in_maps = [dict(xT=cur[c], **{n: allw[n] for n in names}) for c in range(8)]
    res = run_bass_kernel_spmd(kb.nc, in_maps, core_ids=list(range(8)))
    return np.stack([np.asarray(res.results[c]["yT"], np.float32).T
                     for c in range(8)])


def full_in_map(inp_or_xT=None, **inp):
    consts = {**host_constants(), **rope_consts(),
              "c_id128": np.eye(128, dtype=np.float32),
              "c_esel": _esel_const(),
              "c_ones1": np.ones((1, 128), np.float32)}
    return consts


def build_full_program():
    kb = KB()
    xT = kb.dram_in("xT", [D, T])
    yT = kb.dram_out("yT", [D, T])
    m_w1 = kb.dram_in("m_w1", [9 * 1024, 1408])
    m_w3 = kb.dram_in("m_w3", [9 * 1024, 1408])
    m_w2 = kb.dram_in("m_w2", [9 * 1408, 1024])
    dins = {}
    for nm, sh, dt in (
            ("m_b1c", [128, 99], F32), ("m_b3c", [128, 99], F32),
            ("m_b2s", [9, 1024], F32), ("g_gwt", [128, 64], F32),
            ("g_gbt", [128, 8], F32), ("c_id128", [128, 128], F32),
            ("c_esel", [9, 9 * 128], BF16),
            ("l0_wq", [1024, 1024], F32), ("l0_wkd", [1024, 1024], F32),
            ("l0_wv", [1024, 512], F32), ("l0_wo", [1024, 1024], F32),
            ("l0_lq1", [1, 32], F32), ("l0_lk1", [1, 32], F32),
            ("l0_lq2", [1, 32], F32), ("l0_lk2", [1, 32], F32),
            ("l0_rmsw", [128, 1], F32),
            ("c_ropec", [128, T], BF16), ("c_ropes", [128, T], BF16),
            ("c_cmask", [128, 896], BF16), ("c_ones1", [1, 128], F32),
            ("f1_w1", [1024, 4096], F32), ("f1_w2", [4096, 1024], F32),
            ("f1_b1c", [128, 32], F32), ("f1_b2c", [128, 8], F32),
            ("a_wqkv", [1024, 3072], F32), ("a_wo", [1024, 1024], F32),
            ("a_w1", [1024, 4096], F32), ("a_w2", [4096, 1024], F32),
            ("a_bq", [128, 8], F32), ("a_bk", [128, 8], F32),
            ("a_bo", [128, 8], F32),
            ("a_b1", [128, 32], F32), ("a_b2", [128, 8], F32),
            ("a_g1", [128, 8], F32), ("a_b1n", [128, 8], F32),
            ("a_g2", [128, 8], F32), ("a_b2n", [128, 8], F32),
            ("c_lnsel", [128, 16 * NTT], BF16), ("c_bcsel", [16, 128 * NTT], F32)):
        dins[nm] = kb.dram_in(nm, sh, dt)
    dins["a_bvrow"] = kb.dram_in("a_bvrow", [1, 1024], F32)

    m = Model(kb)
    sb = {}
    bf_ins = {"c_esel"}
    cast_bf = {"m_b2s", "g_gwt", "c_ones1"}
    streamed = {"l0_wq", "l0_wkd", "l0_wv", "l0_wo", "f1_w1", "f1_w2",
                "a_wqkv", "a_wo", "a_w1", "a_w2", "c_ropec", "c_ropes",
                "c_cmask", "a_bvrow"}
    prevc = {"sync": None, "gpsimd": None}
    for nm in dins:
        if nm in ("c_lnsel", "c_bcsel") or nm in streamed:
            continue
        sh = list(dins[nm].shape)
        dt = BF16 if (nm in bf_ins or nm in cast_bf) else F32
        b = kb.sbuf("sb_" + nm, sh, dt)
        q = "gpsimd" if (nm in cast_bf) else "sync"
        rd = [prevc[q]] if prevc[q] is not None else []
        kb.dma(q, b.ap_all, dins[nm], reads=rd, writes=[b],
               sem_group="consts_" + q)
        prevc[q] = b
        sb[nm] = b
    xTr = xT.rearrange("(c p) t -> p c t", p=128)
    for c in range(DC):
        kb.dma("sync", m.Xc[c][:, 0:T], xTr[:, c, :], writes=[m.Xc[c]])
    kb.dma("sync", m.lnsel.ap_all, dins["c_lnsel"], writes=[m.lnsel])
    kb.dma("sync", m.bcsel.ap_all, dins["c_bcsel"], writes=[m.bcsel])

    bfw = {}
    deps = {}

    prev_cast = [None]

    def cast_w(name, src, chunks):
        shp = list(src.shape)
        dst = kb.dram_internal("bfw_" + name, shp)
        bufs = []
        for i, (r0, r1) in enumerate(chunks):
            b = Buf(None, f"cw_{name}_{i}")
            kb.bufs.append(b)
            rd = [prev_cast[0]] if prev_cast[0] is not None else []
            o = kb.op("gpsimd",
                      lambda e, src=src, dst=dst, r0=r0, r1=r1: e.dma_start(
                          out=dst[r0:r1, :], in_=src[r0:r1, :]),
                      reads=rd, writes=[b], is_dma=True)
            o.dma_buf = b
            o.sem_group = "castq"
            prev_cast[0] = b
            bufs.append(b)
        bfw[name] = dst
        return bufs

    def cast_chunk(name, src, r0, r1):
        if name not in bfw:
            bfw[name] = kb.dram_internal("bfw_" + name, list(src.shape))
        dst = bfw[name]
        b = Buf(None, f"cw_{name}_{r0}")
        kb.bufs.append(b)
        rd = [prev_cast[0]] if prev_cast[0] is not None else []
        o = kb.op("gpsimd",
                  lambda e, src=src, dst=dst, r0=r0, r1=r1: e.dma_start(
                      out=dst[r0:r1, :], in_=src[r0:r1, :]),
                  reads=rd, writes=[b], is_dma=True)
        o.dma_buf = b
        o.sem_group = "castq"
        prev_cast[0] = b
        return b

    for ex in range(9):
        for nm, src, rw in (("m_w1", m_w1, 1024), ("m_w3", m_w3, 1024),
                            ("m_w2", m_w2, 1408)):
            deps[(nm, ex)] = [cast_chunk(nm, src, rw * ex, rw * (ex + 1))]
    for nm in ("l0_wq", "l0_wkd", "l0_wv", "l0_wo", "f1_w1", "f1_w2",
               "a_wqkv", "a_wo", "a_w1", "a_w2"):
        deps[nm] = [cast_chunk(nm, dins[nm], 0, dins[nm].shape[0])]

    moe_cols = {"gwt": sb["g_gwt"], "gbt": sb["g_gbt"], "b2s": sb["m_b2s"],
                "id128": sb["c_id128"], "esel": sb["c_esel"]}
    moe_deps = {}
    for ex in range(9):
        for nm in ("m_w1", "m_w3", "m_w2"):
            moe_deps[(nm, ex)] = deps[(nm, ex)]
    moe_w = {"m_w1": bfw["m_w1"], "m_w3": bfw["m_w3"],
             "m_w2": bfw["m_w2"], "deps": moe_deps}
    a0_cols = {"lq1": sb["l0_lq1"], "lk1": sb["l0_lk1"], "lq2": sb["l0_lq2"],
               "lk2": sb["l0_lk2"], "rmsw": sb["l0_rmsw"],
               "ropec_d": dins["c_ropec"], "ropes_d": dins["c_ropes"],
               "cmask_d": dins["c_cmask"], "ones1": sb["c_ones1"]}
    a0_w = {"l0_wq": bfw["l0_wq"], "l0_wkd": bfw["l0_wkd"],
            "l0_wv": bfw["l0_wv"], "l0_wo": bfw["l0_wo"],
            "deps": {k: deps[k] for k in ("l0_wq", "l0_wkd", "l0_wv", "l0_wo")}}
    a1_cols = {"bq": sb["a_bq"], "bk": sb["a_bk"], "bvrow_d": dins["a_bvrow"],
               "bo": sb["a_bo"], "b1": sb["a_b1"], "b2": sb["a_b2"],
               "g1": sb["a_g1"], "b1n": sb["a_b1n"], "g2": sb["a_g2"],
               "b2n": sb["a_b2n"], "ones1": sb["c_ones1"]}
    a1_w = {"a_wqkv": bfw["a_wqkv"], "a_wo": bfw["a_wo"],
            "a_w1": bfw["a_w1"], "a_w2": bfw["a_w2"],
            "deps": {k: deps[k] for k in ("a_wqkv", "a_wo", "a_w1", "a_w2")}}

    m.layernorm()
    m.moe(moe_w, sb["m_b1c"], sb["m_b3c"], moe_cols)
    m.layernorm()
    m.attn0(a0_w, a0_cols)
    m.layernorm()
    m.moe(moe_w, sb["m_b1c"], sb["m_b3c"], moe_cols)
    m.layernorm()
    f1w = {"w1": bfw["f1_w1"], "w2": bfw["f1_w2"],
           "deps": {"w1": deps["f1_w1"], "w2": deps["f1_w2"]}}
    m.ffn1(f1w, sb["f1_b1c"], sb["f1_b2c"])
    m.layernorm()
    m.attn1(a1_w, a1_cols)
    m.layernorm()
    m.ffn1(f1w, sb["f1_b1c"], sb["f1_b2c"])
    yTr = yT.rearrange("(c p) t -> p c t", p=128)
    for c in range(DC):
        kb.dma("sync", yTr[:, c, :], m.Xc[c][:, 0:T], reads=[m.Xc[c]])
    kb.finalize()
    return kb


# version 16
# speedup vs baseline: 1.0565x; 1.0064x over previous
import math
import itertools
import contextlib
from collections import defaultdict

import numpy as np
import concourse.bass as bass
import concourse.mybir as mybir
from concourse.bass_utils import run_bass_kernel_spmd

F32 = mybir.dt.float32
BF16 = mybir.dt.bfloat16
U8 = mybir.dt.uint8
AF = mybir.ActivationFunctionType
ALU = mybir.AluOpType
AX = mybir.AxisListType

MAGIC = float(np.float32(12582912.0))


ENGS = ("tensor", "vector", "scalar", "gpsimd", "sync")


class Op:
    __slots__ = ("eng", "emit", "deps", "inc", "count", "is_dma", "dma_buf",
                 "skey", "sem_group")

    def __init__(self, eng, emit, deps, is_dma):
        self.eng = eng
        self.emit = emit
        self.deps = deps
        self.is_dma = is_dma
        self.inc = is_dma
        self.count = None
        self.sem_group = None


class Buf:

    __slots__ = ("h", "writer", "readers", "name", "win", "after_bufs")

    def __init__(self, h, name, win=None):
        self.h = h
        self.name = name
        self.writer = None
        self.readers = []
        self.win = win
        self.after_bufs = None

    def __getitem__(self, sl):
        if self.win is None:
            return self.h[sl]
        if len(self.win) == 4:
            ro, rows, off, n = self.win
        else:
            rows, off, n = self.win
            ro = 0
        rs, cs = sl
        start = (cs.start or 0) + off
        stop = (cs.stop if cs.stop is not None else n) + off
        assert stop - off <= n
        r0 = 0 if rs == slice(None) else (rs.start or 0)
        r1 = rows if rs == slice(None) else rs.stop
        return self.h[ro + r0:ro + r1, start:stop]

    @property
    def ap_all(self):
        if self.win is None:
            return self.h[:, :]
        if len(self.win) == 4:
            ro, rows, off, n = self.win
            return self.h[ro:ro + rows, off:off + n]
        rows, off, n = self.win
        return self.h[0:rows, off:off + n]


class KB:

    def __init__(self, name="mk"):
        self.nc = bass.Bass("TRN2", target_bir_lowering=False)
        self.ctx = contextlib.ExitStack()
        self.ops = []
        self.bufs = []
        self._uid = 0

    def uid(self):
        self._uid += 1
        return self._uid

    def dram_in(self, name, shape, dtype=F32):
        if not hasattr(self, "input_names"):
            self.input_names = []
        self.input_names.append(name)
        return self.nc.dram_tensor(name, shape, dtype, kind="ExternalInput").ap()

    def dram_out(self, name, shape, dtype=F32):
        return self.nc.dram_tensor(name, shape, dtype, kind="ExternalOutput").ap()

    def dram_internal(self, name, shape, dtype=BF16):
        return self.nc.dram_tensor(name, shape, dtype).ap()

    def sbuf(self, name, shape, dtype=F32):
        h = self.ctx.enter_context(self.nc.sbuf_tensor(name, shape, dtype))
        b = Buf(h, name)
        self.bufs.append(b)
        return b

    def psum(self, name, shape, dtype=F32):
        per = {F32: 512, BF16: 1024}[dtype]
        assert shape[1] <= per
        h = self.ctx.enter_context(self.nc.psum_tensor(name, [128, per], dtype))
        b = Buf(h, name)
        self.bufs.append(b)
        return b

    def op(self, eng, emit, reads=(), writes=(), is_dma=False):
        deps = []
        for b in reads:
            if b.writer is not None:
                deps.append(b.writer)
        o = Op(eng, emit, deps, is_dma)
        for b in reads:
            b.readers.append(o)
        for b in writes:
            if b.after_bufs:
                for ob in b.after_bufs:
                    deps.extend(ob.readers)
                    if ob.writer is not None:
                        deps.append(ob.writer)
                b.after_bufs = None
            if b.writer is not None:
                deps.append(b.writer)
            deps.extend(b.readers)
            b.writer = o
            b.readers = []
        o.deps = [d for d in dict.fromkeys(deps) if d is not o]
        self.ops.append(o)
        return o

    def dma(self, queue, out, in_, reads=(), writes=(), sem_group=None, **kw):
        o = self.op(
            queue,
            lambda e: e.dma_start(out=out, in_=in_, **kw),
            reads=reads, writes=writes, is_dma=True)
        o.dma_buf = writes[0] if writes else None
        o.sem_group = sem_group
        return o

    def emit_all(self):
        def sem_key(o):
            if not o.is_dma:
                return o.eng
            if getattr(o, "sem_group", None) is not None:
                return ("buf", o.sem_group)
            if getattr(o, "dma_buf", None) is not None:
                return ("buf", id(o.dma_buf))
            return ("store", o.eng)

        for o in self.ops:
            for d in o.deps:
                if d.eng == "tensor" and o.eng == "tensor":
                    continue
                d.inc = True
        cnt = defaultdict(int)
        store_totals = defaultdict(int)
        for o in self.ops:
            if o.inc:
                k = sem_key(o)
                cnt[k] += 16 if o.is_dma else 1
                o.count = cnt[k]
                o.skey = k
                if not isinstance(k, str) and k[0] == "store":
                    store_totals[k] = cnt[k]
        sems = {}
        for k in cnt:
            nm = k if isinstance(k, str) else f"{k[0]}_{k[1]}"
            sems[k] = self.ctx.enter_context(self.nc.semaphore(f"s_{nm}"))
        running_store = defaultdict(int)
        waitsfor = []
        ptr = {}
        for o in self.ops:
            if o.inc and not isinstance(o.skey, str) and o.skey[0] == "store":
                running_store[o.skey] = o.count
            wmap = {}
            for d in o.deps:
                if d.eng == "tensor" and o.eng == "tensor":
                    continue
                k = d.skey if hasattr(d, "skey") and d.inc else None
                if k is None:
                    continue
                c = d.count
                if not isinstance(k, str) and k[0] == "store":
                    c = running_store[k]
                if wmap.get(k, 0) < c:
                    wmap[k] = c
            waitsfor.append(wmap)
        waited = defaultdict(int)
        plans = {e: [] for e in ENGS}
        for o, wmap in zip(self.ops, waitsfor):
            w2 = {}
            for k, c in wmap.items():
                if waited[(o.eng, k)] < c:
                    waited[(o.eng, k)] = c
                    w2[k] = c
            plans[o.eng].append((w2, o))
        block = self.ctx.enter_context(self.nc.Block())

        def run_section(engine, eng_name):
            for wmap, o in plans[eng_name]:
                for k, c in wmap.items():
                    engine.wait_ge(sems[k], c)
                ins = o.emit(engine)
                if o.inc:
                    ins.then_inc(sems[o.skey], 16 if o.is_dma else 1)

        for e in ENGS:
            sec = getattr(block, e)

            def make(e=e):
                def _(engine):
                    run_section(engine, e)
                return _
            sec(make())
        self.ctx.close()
        self.n_sems = len(sems)

    def finalize(self):
        last_dma = {}
        for o in self.ops:
            if o.is_dma:
                last_dma[o.eng] = o
        for e, o in last_dma.items():
            w = Op(e, lambda engine: engine.nop(), [o], False)
            self.ops.append(w)
        self.emit_all()



def ceil_div(a, b):
    return (a + b - 1) // b


T = 2048
D = 1024
DC = D // 128
NTT = T // 512
LN_EPS = 1e-5


def host_constants():
    import ml_dtypes
    bf = ml_dtypes.bfloat16
    lnsel = np.zeros((128, 16 * NTT), dtype=bf)
    for i in range(NTT):
        lnsel[:, 16 * i + i] = 1.0
    bcsel = np.zeros((16, 128 * NTT), dtype=np.float32)
    for i in range(NTT):
        bcsel[i, 128 * i:128 * (i + 1)] = 1.0
    return {"c_lnsel": lnsel, "c_bcsel": bcsel}


class Model:

    def __init__(self, kb: KB):
        self.kb = kb
        nc = kb.nc
        self.X = kb.sbuf("X", [128, DC * T], F32)
        self.Xc = [Buf(self.X.h, f"Xc{c}", win=(128, c * T, T)) for c in range(DC)]
        kb.bufs.extend(self.Xc)
        self.xn = kb.sbuf("xn", [128, DC * T], BF16)
        self.xnc = [Buf(self.xn.h, f"xnc{c}", win=(128, c * T, T)) for c in range(DC)]
        kb.bufs.extend(self.xnc)
        self.psd = []
        for j in range(4):
            h = kb.ctx.enter_context(kb.nc.psum_tensor(f"psd{j}", [128, 1024], F32))
            self.psd.append(h)
        self.ps = []
        for i in range(8):
            b = Buf(self.psd[i // 2], f"ps{i}", win=(128, 512 * (i % 2), 512))
            kb.bufs.append(b)
            self.ps.append(b)
        self.xsq = [kb.sbuf("xsq0", [128, 512], BF16)]
        self.xbb = [kb.sbuf("xbb0", [128, 512], BF16)]
        strow = kb.sbuf("strow", [16, 2 * 512], F32)
        self.strow = [Buf(strow.h, f"strow{i}", win=(16, 512 * (i % 2), 512))
                      for i in range(2)]
        self._tscr = []
        self._mscr = None
        self.ccol = kb.sbuf("ccol", [128, 8], F32)
        self._ccol_vals = {}
        self.lnsel = kb.sbuf("lnsel", [128, 16 * NTT], BF16)
        self.bcsel = kb.sbuf("bcsel", [16, 128 * NTT], F32)

    def arena_win(self, name, off, n, after=()):
        if getattr(self, "_arena", None) is None:
            self._arena = self.kb.sbuf("arena", [128, 40960], BF16)
            self._arena_owners = []
        assert off + n <= 40960, (name, off, n)
        b = Buf(self._arena.h, name, win=(128, off, n))
        aft = list(after)
        for (o2, n2, ob) in self._arena_owners:
            if o2 < off + n and off < o2 + n2:
                aft.append(ob)
        b.after_bufs = aft
        self._arena_owners.append((off, n, b))
        self.kb.bufs.append(b)
        return b

    def arena_win_f32(self, name, off_bf16, n_f32):
        if getattr(self, "_arena", None) is None:
            self.arena_win("_dummy", 0, 1)
        if getattr(self, "_arena_f32h", None) is None:
            self._arena_f32h = self._arena.h.bitcast(F32)
        assert off_bf16 % 2 == 0 and off_bf16 + 2 * n_f32 <= 40960
        b = Buf(self._arena_f32h, name, win=(128, off_bf16 // 2, n_f32))
        aft = []
        for (o2, n2, ob) in self._arena_owners:
            if o2 < off_bf16 + 2 * n_f32 and off_bf16 < o2 + n2:
                aft.append(ob)
        b.after_bufs = aft
        self._arena_owners.append((off_bf16, 2 * n_f32, b))
        self.kb.bufs.append(b)
        return b

    def arena_win_u8(self, name, off_bf16, n_u8):
        if getattr(self, "_arena", None) is None:
            self.arena_win("_dummy", 0, 1)
        if getattr(self, "_arena_u8h", None) is None:
            self._arena_u8h = self._arena.h.bitcast(U8)
        assert n_u8 % 2 == 0 and off_bf16 * 2 + n_u8 <= 81920
        b = Buf(self._arena_u8h, name, win=(128, off_bf16 * 2, n_u8))
        aft = []
        for (o2, n2, ob) in self._arena_owners:
            if o2 < off_bf16 + n_u8 // 2 and off_bf16 < o2 + n2:
                aft.append(ob)
        b.after_bufs = aft
        self._arena_owners.append((off_bf16, n_u8 // 2, b))
        self.kb.bufs.append(b)
        return b

    def buf(self, name, shape, dtype=F32):
        if not hasattr(self, "_bufcache"):
            self._bufcache = {}
        if name not in self._bufcache:
            self._bufcache[name] = self.kb.sbuf(name, shape, dtype)
        return self._bufcache[name]

    def scr(self, i):
        while len(self._tscr) <= i:
            self._tscr.append(self.kb.sbuf(f"tscr{len(self._tscr)}", [128, 512], F32))
        return self._tscr[i]

    @property
    def mscr(self):
        if self._mscr is None:
            self._mscr = self.kb.sbuf("mscr", [128, 512], U8)
        return self._mscr

    def const_col(self, val):
        kb = self.kb
        if val not in self._ccol_vals:
            i = len(self._ccol_vals)
            assert i < 8
            self._ccol_vals[val] = i
            kb.op("vector", lambda e, i=i, val=val: e.memset(
                self.ccol[:, i:i + 1], float(val)),
                writes=[self.ccol])
        i = self._ccol_vals[val]
        return self.ccol[:, i:i + 1]

    def load_consts(self, kb, cmap):
        kb.dma("sync", self.lnsel.ap_all, cmap["c_lnsel"], writes=[self.lnsel])
        kb.dma("sync", self.bcsel.ap_all, cmap["c_bcsel"], writes=[self.bcsel])

    def layernorm(self, dst=None, src=None, eps=LN_EPS, gain_bias=None):
        kb = self.kb
        src = src if src is not None else self.Xc
        dst = dst or self.xn
        if dst is self.xn:
            dsts = self.xnc
            dstap = lambda c, a, b: self.xnc[c][:, a:b]
        else:
            dsts = [dst] * DC
            dstap = lambda c, a, b: dst[:, c * T + a:c * T + b]
        if isinstance(src, list):
            srcs = src
            srcap = lambda c, a, b: srcs[c][:, a:b]
        else:
            srcs = [src] * DC
            srcap = lambda c, a, b: src[:, c * T + a:c * T + b]
        ps_sx, ps_sq = self.ps[6], self.ps[7]
        mrow = self.strow[0]
        qrow = self.strow[1]
        trow = self.strow[1]
        srow = self.strow[1]
        arow = self.strow[1]
        brow = self.strow[0]
        for c in range(DC):
            for i in range(NTT):
                xs = self.xsq[0]
                xb = self.xbb[0]
                off = 512 * i
                kb.op("scalar",
                      lambda e, off=off, xs=xs, c=c: e.activation(
                          xs[:, :], srcap(c, off, off + 512), AF.Square),
                      reads=[srcs[c]], writes=[xs])
                kb.op("vector",
                      lambda e, off=off, xb=xb, c=c: e.tensor_copy(
                          xb[:, :], srcap(c, off, off + 512)),
                      reads=[srcs[c]], writes=[xb])
                st = (c == 0 and i == 0)
                fin = (c == DC - 1 and i == NTT - 1)
                kb.op("tensor",
                      lambda e, i=i, xs=xs, st=st, fin=fin: e.matmul(
                          ps_sq[0:16, :], self.lnsel[:, 16 * i:16 * (i + 1)],
                          xs[:, :], start=st, stop=fin),
                      reads=[xs, self.lnsel], writes=[ps_sq])
                kb.op("tensor",
                      lambda e, i=i, xb=xb, st=st, fin=fin: e.matmul(
                          ps_sx[0:16, :], self.lnsel[:, 16 * i:16 * (i + 1)],
                          xb[:, :], start=st, stop=fin),
                      reads=[xb, self.lnsel], writes=[ps_sx])
        kb.op("vector", lambda e: e.tensor_scalar(
            mrow[:, :], ps_sx[0:16, :], 1.0 / D, None, ALU.mult),
            reads=[ps_sx], writes=[mrow])
        kb.op("vector", lambda e: e.tensor_tensor(
            qrow[:, :], mrow[:, :], mrow[:, :], ALU.mult),
            reads=[mrow], writes=[qrow])
        kb.op("vector", lambda e: e.scalar_tensor_tensor(
            qrow[:, :], ps_sq[0:16, :], 1.0 / D, qrow[:, :],
            ALU.mult, ALU.subtract),
            reads=[ps_sq, qrow], writes=[qrow])
        epsc = self.const_col(float(eps))
        kb.op("scalar", lambda e, epsc=epsc: e.activation(
            srow[:, :], qrow[:, :], AF.Sqrt, bias=epsc[:16, :]),
            reads=[qrow, self.ccol], writes=[srow])
        kb.op("vector", lambda e: e.reciprocal(arow[:, :], srow[:, :]),
              reads=[srow], writes=[arow])
        kb.op("vector", lambda e: e.scalar_tensor_tensor(
            brow[:, :], mrow[:, :], -1.0, arow[:, :], ALU.mult, ALU.mult),
            reads=[mrow, arow], writes=[brow])
        psA, psB = self.ps[4], self.ps[5]
        for i in range(NTT):
            kb.op("tensor", lambda e, i=i: e.matmul(
                psA[:, :], self.bcsel[:, 128 * i:128 * (i + 1)], arow[:, :],
                start=True, stop=True),
                reads=[arow, self.bcsel], writes=[psA])
            kb.op("tensor", lambda e, i=i: e.matmul(
                psB[:, :], self.bcsel[:, 128 * i:128 * (i + 1)], brow[:, :],
                start=True, stop=True),
                reads=[brow, self.bcsel], writes=[psB])
            for c in range(DC):
                tsc = self.scr(c % 2)
                sl = slice(c * T + 512 * i, c * T + 512 * (i + 1))
                kb.op("vector", lambda e, i=i, tsc=tsc, c=c: e.tensor_tensor(
                    tsc[:, 0:512], srcap(c, 512 * i, 512 * (i + 1)), psA[:, :],
                    ALU.mult),
                    reads=[srcs[c], psA], writes=[tsc])
                if gain_bias is None:
                    kb.op("vector", lambda e, i=i, tsc=tsc, c=c: e.tensor_tensor(
                        dstap(c, 512 * i, 512 * (i + 1)), tsc[:, 0:512],
                        psB[:, :], ALU.add),
                        reads=[tsc, psB], writes=[dsts[c]])
                else:
                    gcol, bcol = gain_bias
                    kb.op("vector", lambda e, sl=sl, tsc=tsc: e.tensor_tensor(
                        tsc[:, 0:512], tsc[:, 0:512], psB[:, :], ALU.add),
                        reads=[tsc, psB], writes=[tsc])
                    kb.op("vector", lambda e, i=i, tsc=tsc, c=c: e.tensor_scalar(
                        dstap(c, 512 * i, 512 * (i + 1)), tsc[:, 0:512], gcol[:, c:c + 1],
                        bcol[:, c:c + 1], ALU.mult, ALU.add),
                        reads=[tsc, gcol, bcol], writes=[dsts[c]])

    def ffn1(self, wmap, b1c, b2c):
        kb = self.kb
        HF = 4096
        NHC = HF // 128
        wdep = wmap.get("deps", {})
        hbuf = self.arena_win("f1_hbuf", 0, NHC * 512)
        w1sl = [self.arena_win(f"f1_w1sl{i}", 16384 + 4096 * i, DC * 512)
                for i in range(2)]
        w2ch = [self.arena_win(f"f1_w2ch{i}", 24576 + 2048 * i, 2048)
                for i in range(2)]
        scrsets = [[self.scr(i) for i in range(5)]]
        for k in range(2):
            scrsets.append([self.arena_win_f32(f"f1_scrB{k}_{i}",
                                               28672 + 5120 * k + 1024 * i, 512)
                            for i in range(5)])
        w1r = wmap["w1"].rearrange("(c p) o -> p c o", p=128)
        w2r = wmap["w2"]
        b256 = self.buf("f1_b256", [128, 32], F32)
        kb.op("vector", lambda e: e.tensor_scalar(
            b256[:, 0:32], b1c[:, 0:32], 256.0, None, ALU.mult),
            reads=[b1c], writes=[b256])
        for ti in range(NTT):
            toff = 512 * ti
            for sl_i in range(DC):
                wb = w1sl[sl_i % 2]
                kb.dma("sync", wb.ap_all, w1r[:, :, 512 * sl_i:512 * (sl_i + 1)],
                       reads=wdep.get("w1", ()), writes=[wb])
                for hh in range(4):
                    hc = 4 * sl_i + hh
                    ps = self.ps[hc % 4]
                    for kc in range(DC):
                        kb.op("tensor", lambda e, wb=wb, hh=hh, kc=kc, ps=ps, toff=toff: e.matmul(
                            ps[:, :],
                            wb[:, 512 * kc + 128 * hh: 512 * kc + 128 * hh + 128],
                            self.xnc[kc][:, toff:toff + 512],
                            start=(kc == 0), stop=(kc == DC - 1)),
                            reads=[wb, self.xnc[kc]], writes=[ps])
                    self._pwl(ps, b256, hc, hbuf, 512 * hc, 512,
                              scrset=scrsets[hc % 3])
            for kcp in range(16):
                wc = w2ch[kcp % 2]
                kb.dma("sync", wc.ap_all,
                       w2r[256 * kcp:256 * (kcp + 1), :]
                       .rearrange("(c p) o -> p c o", p=128),
                       reads=wdep.get("w2", ()), writes=[wc])
                for kk in range(2):
                    kc = 2 * kcp + kk
                    for dc in range(DC):
                        ps = self.ps[dc]
                        kb.op("tensor", lambda e, wc=wc, kc=kc, kk=kk, dc=dc, ps=ps: e.matmul(
                            ps[:, :],
                            wc[:, 1024 * kk + 128 * dc: 1024 * kk + 128 * dc + 128],
                            hbuf[:, kc * 512: (kc + 1) * 512],
                            start=(kc == 0), stop=(kc == NHC - 1)),
                            reads=[wc, hbuf], writes=[ps])
            for dc in range(DC):
                ps = self.ps[dc]
                sl = slice(dc * T + toff, dc * T + toff + 512)
                t0 = self.scr(dc % 4)
                Xc = self.Xc[dc]
                kb.op("vector", lambda e, dc=dc, ps=ps, t0=t0: e.tensor_scalar(
                    t0[:, 0:512], ps[:, :], b2c[:, dc:dc + 1], 256.0,
                    ALU.add, ALU.mult),
                    reads=[ps, b2c], writes=[t0])
                kb.op("vector", lambda e, t0=t0: e.tensor_scalar(
                    t0[:, 0:512], t0[:, 0:512], MAGIC, MAGIC,
                    ALU.add, ALU.subtract),
                    reads=[t0], writes=[t0])
                kb.op("vector", lambda e, t0=t0, toff=toff, Xc=Xc: e.scalar_tensor_tensor(
                    Xc[:, toff:toff + 512], t0[:, 0:512], 1.0 / 512.0,
                    Xc[:, toff:toff + 512], ALU.mult, ALU.add),
                    reads=[t0, Xc], writes=[Xc])
        return

    def _pwl(self, ps, b1c, hc, hbuf, hoff, W, scrset=None):
        kb = self.kb
        GELU4_256 = 256.0 * (2.0 + 2.0 * math.erf(2.0 * math.sqrt(2.0)))
        if scrset is None:
            scrset = [self.scr(i) for i in range(5)]
        nt, ncl, jj, tt_, g0 = scrset
        g1 = jj
        dd = ncl
        yy = tt_
        g4c = self.const_col(-GELU4_256)
        mgc = self.const_col(MAGIC)
        kb.op("scalar", lambda e: e.activation(
            nt[:, 0:W], ps[:, 0:W], AF.Identity,
            bias=b1c[:, hc:hc + 1], scale=256.0),
            reads=[ps, b1c, self.ccol], writes=[nt])
        kb.op("vector", lambda e: e.tensor_scalar(
            nt[:, 0:W], nt[:, 0:W], MAGIC, MAGIC, ALU.add, ALU.subtract),
            reads=[nt], writes=[nt])
        kb.op("vector", lambda e: e.tensor_scalar(
            ncl[:, 0:W], nt[:, 0:W], -1024.0, 1024.0, ALU.max, ALU.min),
            reads=[nt], writes=[ncl])
        MAGIC64 = float(np.float32(64.0 * 12582912.0))
        kb.op("vector", lambda e: e.tensor_scalar(
            g0[:, 0:W], ncl[:, 0:W], 31.5, None, ALU.subtract),
            reads=[ncl], writes=[g0])
        kb.op("vector", lambda e, MAGIC64=MAGIC64: e.tensor_scalar(
            jj[:, 0:W], g0[:, 0:W], MAGIC64, MAGIC64, ALU.add, ALU.subtract),
            reads=[g0], writes=[jj])
        kb.op("gpsimd", lambda e: e.tensor_tensor(
            tt_[:, 0:W], ncl[:, 0:W], jj[:, 0:W], ALU.subtract),
            reads=[ncl, jj], writes=[tt_])
        kb.op("scalar", lambda e: e.activation(
            g0[:, 0:W], jj[:, 0:W], AF.Gelu, bias=0.0, scale=1.0 / 256.0),
            reads=[jj], writes=[g0])
        qc = self.const_col(0.25)
        kb.op("scalar", lambda e, qc=qc: e.activation(
            g1[:, 0:W], jj[:, 0:W], AF.Gelu, bias=qc, scale=1.0 / 256.0),
            reads=[jj, self.ccol], writes=[g1])
        kb.op("gpsimd", lambda e: e.tensor_tensor(
            dd[:, 0:W], g1[:, 0:W], g0[:, 0:W], ALU.subtract),
            reads=[g1, g0], writes=[dd])
        kb.op("gpsimd", lambda e: e.tensor_tensor(
            dd[:, 0:W], tt_[:, 0:W], dd[:, 0:W], ALU.mult),
            reads=[tt_, dd], writes=[dd])
        kb.op("vector", lambda e: e.scalar_tensor_tensor(
            yy[:, 0:W], dd[:, 0:W], 1.0 / 64.0, g0[:, 0:W], ALU.mult, ALU.add),
            reads=[dd, g0], writes=[yy])
        kb.op("scalar", lambda e, g4c=g4c: e.activation(
            nt[:, 0:W], nt[:, 0:W], AF.Relu, bias=g4c),
            reads=[nt, self.ccol], writes=[nt])
        kb.op("vector", lambda e: e.scalar_tensor_tensor(
            yy[:, 0:W], nt[:, 0:W], 1.0 / 256.0, yy[:, 0:W], ALU.mult, ALU.add),
            reads=[nt, yy], writes=[yy])
        kb.op("scalar", lambda e, mgc=mgc: e.activation(
            yy[:, 0:W], yy[:, 0:W], AF.Identity, bias=mgc,
            scale=256.0),
            reads=[yy, self.ccol], writes=[yy])
        kb.op("vector", lambda e: e.tensor_scalar(
            hbuf[:, hoff:hoff + W], yy[:, 0:W], MAGIC, 1.0 / 256.0,
            ALU.subtract, ALU.mult),
            reads=[yy], writes=[hbuf])

    def moe(self, wmap, b1c, b3c, cols):
        kb = self.kb
        HE = 1408
        NHC = 11
        gwt, gbt, b2s, id128, esel = (cols[k] for k in
                                      ("gwt", "gbt", "b2s", "id128", "esel"))
        gscr = self.buf("moe_gscr", [128, 48], F32)
        rw9f = self.buf("moe_rw9f", [128, 16], F32)
        rwT9b = self.buf("moe_rwT9", [9, T], BF16)
        rwT9 = rwT9b
        w1b = self.arena_win("moe_w1b", 0, DC * HE)
        w3b = self.arena_win("moe_w3b", DC * HE, DC * HE)
        w2b = self.arena_win("moe_w2b", 2 * DC * HE, NHC * 1024)
        hst = [self.arena_win("moe_hst", 33792, NHC * 512)]
        cbuf, ubuf = self.scr(0), self.scr(1)
        cbuf2, ubuf2 = self.scr(2), self.scr(3)
        ssb = self.arena_win("moe_ssb", 39424, 512)
        ssb2 = self.arena_win("moe_ssb2", 40448, 512)
        bcsb = self.arena_win("moe_bcsb", 39936, 512)
        psG, psT, psB = self.ps[0], self.ps[1], self.ps[7]

        kb.op("vector", lambda e: e.memset(rw9f[:, 8:9], 1.0), writes=[rw9f])
        for ti in range(16):
            for kc in range(DC):
                kb.op("tensor", lambda e, ti=ti, kc=kc: e.matmul(
                    psG[:, 0:8], self.xnc[kc][:, 128 * ti:128 * (ti + 1)],
                    gwt[:, 8 * kc:8 * (kc + 1)],
                    start=(kc == 0), stop=(kc == DC - 1)),
                    reads=[self.xnc[kc], gwt], writes=[psG])
            kb.op("vector", lambda e: e.reduce_max(
                gscr[:, 0:1], psG[:, 0:8], axis=AX.X, negate=True),
                reads=[psG], writes=[gscr])
            kb.op("scalar", lambda e: e.activation(
                gscr[:, 8:16], psG[:, 0:8], AF.Exp, bias=gscr[:, 0:1],
                accum_out=gscr[:, 1:2]),
                reads=[psG, gscr], writes=[gscr])
            kb.op("vector", lambda e: e.reciprocal(
                gscr[:, 2:3], gscr[:, 1:2]), reads=[gscr], writes=[gscr])
            kb.op("vector", lambda e: e.tensor_scalar(
                gscr[:, 16:24], gscr[:, 8:16], gscr[:, 2:3], None, ALU.mult),
                reads=[gscr], writes=[gscr])
            kb.op("vector", lambda e: e.tensor_tensor(
                gscr[:, 24:32], gscr[:, 16:24], gbt[:, 0:8], ALU.add),
                reads=[gscr, gbt], writes=[gscr])
            kb.op("vector", lambda e: e.tensor_scalar(
                gscr[:, 32:40], gscr[:, 24:32], gscr[:, 24:25], None, ALU.is_lt),
                reads=[gscr], writes=[gscr])
            for e2 in range(1, 8):
                kb.op("vector", lambda e, e2=e2: e.scalar_tensor_tensor(
                    gscr[:, 32:40], gscr[:, 24:32], gscr[:, 24 + e2:25 + e2],
                    gscr[:, 32:40], ALU.is_lt, ALU.add),
                    reads=[gscr], writes=[gscr])
            kb.op("vector", lambda e: e.scalar_tensor_tensor(
                rw9f[:, 0:8], gscr[:, 32:40], 1.5, gscr[:, 16:24],
                ALU.is_lt, ALU.mult),
                reads=[gscr], writes=[rw9f])
            kb.op("tensor", lambda e: e.transpose(
                psT[0:9, 0:128], rw9f[:, 0:9], id128[:, :]),
                reads=[rw9f, id128], writes=[psT])
            kb.op("vector", lambda e, ti=ti: e.tensor_copy(
                rwT9[0:9, 128 * ti:128 * (ti + 1)], psT[0:9, 0:128]),
                reads=[psT], writes=[rwT9])

        w1r = wmap["m_w1"]
        w3r = wmap["m_w3"]
        w2r = wmap["m_w2"]
        wdep = wmap.get("deps", {})
        for ex in range(9):
            kb.dma("sync", w1b.ap_all,
                   w1r[1024 * ex:1024 * (ex + 1), :].rearrange("(c p) h -> p c h", p=128),
                   reads=wdep.get(("m_w1", ex), ()), writes=[w1b])
            kb.dma("sync", w3b.ap_all,
                   w3r[1024 * ex:1024 * (ex + 1), :].rearrange("(c p) h -> p c h", p=128),
                   reads=wdep.get(("m_w3", ex), ()), writes=[w3b])
            kb.dma("sync", w2b.ap_all,
                   w2r[HE * ex:HE * (ex + 1), :].rearrange("(c p) d -> p c d", p=128),
                   reads=wdep.get(("m_w2", ex), ()), writes=[w2b])
            for qt in range(4):
                toff = 512 * qt
                hs = hst[0]
                kb.op("tensor", lambda e, ex=ex, toff=toff: e.matmul(
                    psB[:, :], esel[0:9, 128 * ex:128 * (ex + 1)],
                    rwT9[0:9, toff:toff + 512], start=True, stop=True),
                    reads=[esel, rwT9], writes=[psB])
                kb.op("vector", lambda e: e.tensor_copy(
                    bcsb[:, :], psB[:, :]), reads=[psB], writes=[bcsb])
                for hc in range(NHC):
                    pa = self.ps[hc % 2]
                    pc = self.ps[2 + hc % 2]
                    cb = [cbuf, cbuf2][hc % 2]
                    ub = [ubuf, ubuf2][hc % 2]
                    sbb = [ssb, ssb2][hc % 2]
                    for kc in range(DC):
                        kb.op("tensor", lambda e, hc=hc, kc=kc, toff=toff, pa=pa: e.matmul(
                            pa[:, :], w1b[:, HE * kc + 128 * hc: HE * kc + 128 * hc + 128],
                            self.xnc[kc][:, toff:toff + 512],
                            start=(kc == 0), stop=(kc == DC - 1)),
                            reads=[w1b, self.xnc[kc]], writes=[pa])
                        kb.op("tensor", lambda e, hc=hc, kc=kc, toff=toff, pc=pc: e.matmul(
                            pc[:, :], w3b[:, HE * kc + 128 * hc: HE * kc + 128 * hc + 128],
                            self.xnc[kc][:, toff:toff + 512],
                            start=(kc == 0), stop=(kc == DC - 1)),
                            reads=[w3b, self.xnc[kc]], writes=[pc])
                    bcol1 = 11 * ex + hc
                    kb.op("scalar", lambda e, bcol1=bcol1, pc=pc, cb=cb: e.activation(
                        cb[:, 0:512], pc[:, :], AF.Identity,
                        bias=b3c[:, bcol1:bcol1 + 1]),
                        reads=[pc, b3c], writes=[cb])
                    kb.op("vector", lambda e, bcol1=bcol1, pa=pa, cb=cb, ub=ub: e.scalar_tensor_tensor(
                        ub[:, 0:512], pa[:, :], b1c[:, bcol1:bcol1 + 1],
                        cb[:, 0:512], ALU.add, ALU.mult),
                        reads=[pa, b1c, cb], writes=[ub])
                    kb.op("scalar", lambda e, ub=ub, sbb=sbb: e.activation(
                        sbb[:, 0:512], ub[:, 0:512], AF.Silu),
                        reads=[ub], writes=[sbb])
                    kb.op("gpsimd", lambda e, hc=hc, sbb=sbb: e.tensor_tensor(
                        hs[:, 512 * hc:512 * (hc + 1)], sbb[:, 0:512], bcsb[:, :],
                        ALU.mult),
                        reads=[sbb, bcsb], writes=[hs])
                for dc in range(DC):
                    ps = self.ps[4 + dc % 3]
                    for kc in range(NHC):
                        kb.op("tensor", lambda e, kc=kc, dc=dc, ps=ps: e.matmul(
                            ps[:, :],
                            w2b[:, 1024 * kc + 128 * dc: 1024 * kc + 128 * dc + 128],
                            hs[:, 512 * kc:512 * (kc + 1)],
                            start=(kc == 0), stop=(kc == NHC - 1)),
                            reads=[w2b, hs], writes=[ps])
                    Xc = self.Xc[dc]
                    kb.op("vector", lambda e, ps=ps, toff=toff, Xc=Xc: e.scalar_tensor_tensor(
                        Xc[:, toff:toff + 512], ps[:, :], 0.5,
                        Xc[:, toff:toff + 512], ALU.mult, ALU.add),
                        reads=[ps, Xc], writes=[Xc])
        for qt in range(4):
            toff = 512 * qt
            for dc in range(DC):
                ps = self.ps[4 + dc % 3]
                kb.op("tensor", lambda e, dc=dc, toff=toff, ps=ps: e.matmul(
                    ps[:, :], b2s[0:9, 128 * dc:128 * (dc + 1)],
                    rwT9[0:9, toff:toff + 512], start=True, stop=True),
                    reads=[b2s, rwT9], writes=[ps])
                Xc = self.Xc[dc]
                kb.op("vector", lambda e, ps=ps, toff=toff, Xc=Xc: e.scalar_tensor_tensor(
                    Xc[:, toff:toff + 512], ps[:, :], 0.5,
                    Xc[:, toff:toff + 512], ALU.mult, ALU.add),
                    reads=[ps, Xc], writes=[Xc])
        return

    def attn1(self, wmap, cols):
        kb = self.kb
        bq, bk, bo, b1, b2, g1c, b1nc, g2c, b2nc, ones1 = (
            cols[k] for k in ("bq", "bk", "bo", "b1", "b2",
                              "g1", "b1n", "g2", "b2n", "ones1"))
        SC16 = 1.0 / 8.0
        wdep = wmap.get("deps", {})
        INV_SQRT_S = 1.0 / math.sqrt(T)
        wqkv = wmap["a_wqkv"]
        Vb = self.arena_win("a1_V", 0, 16384)
        wvb = self.arena_win("a1_wvb", 32768, DC * 256)
        Qb = self.arena_win("a1_Q", 32768, T)
        Kb = self.arena_win("a1_K", 32768 + T, T)
        bvrow = Buf(self._arena.h, "a1_bvrow", win=(1, 36864, 1024))
        bvrow.after_bufs = [o for (o2, n2, o) in self._arena_owners
                            if o2 < 36864 + 1024 and 36864 < o2 + n2]
        self._arena_owners.append((36864, 1024, bvrow))
        kb.bufs.append(bvrow)
        kb.dma("gpsimd", bvrow.ap_all, cols["bvrow_d"], writes=[bvrow])
        Pst0 = self.arena_win("a1_P0", 36864, T)
        vt = [kb.sbuf(f"a1_vt{i}", [128, 64], BF16) for i in range(2)]
        dscr = [kb.sbuf("a1_dscr0", [128, 16], F32),
                kb.sbuf("a1_dscr1", [128, 16], F32)]
        nrm = kb.sbuf("a1_nrm", [1, 16], F32)
        ebias = kb.sbuf("a1_eb", [128, 1], F32)
        attnb = self.arena_win("a1_attn", 16384, DC * T)
        sqq = self.xsq[0]
        wqkvr = wqkv.rearrange("(c p) o -> p c o", p=128)

        for vh in range(4):
            kb.dma("sync", wvb.ap_all,
                   wqkvr[:, :, 2048 + 256 * vh:2048 + 256 * (vh + 1)],
                   reads=wdep.get("a_wqkv", ()), writes=[wvb])
            for tc in range(16):
                ps = self.ps[tc % 2]
                for kc in range(DC):
                    kb.op("tensor", lambda e, tc=tc, kc=kc, ps=ps: e.matmul(
                        ps[:, 0:256], self.xnc[kc][:, 128 * tc:128 * (tc + 1)],
                        wvb[:, 256 * kc:256 * (kc + 1)],
                        start=(kc == 0), stop=False),
                        reads=[self.xnc[kc], wvb], writes=[ps])
                kb.op("tensor", lambda e, vh=vh, ps=ps: e.matmul(
                    ps[:, 0:256], ones1[0:1, 0:128], bvrow[0:1, 256 * vh:256 * (vh + 1)],
                    start=False, stop=True),
                    reads=[ones1, bvrow], writes=[ps])
                kb.op("vector", lambda e, tc=tc, vh=vh, ps=ps: e.tensor_copy(
                    Vb[:, 1024 * tc + 256 * vh: 1024 * tc + 256 * (vh + 1)], ps[:, 0:256]),
                    reads=[ps], writes=[Vb])

        for hp in range(DC):
            for which, dstb, bcol in ((0, Qb, bq), (1, Kb, bk)):
                wb = self.arena_win(f"a1_wqk_{hp}_{which}", 38912, DC * 128)
                kb.dma("sync", wb.ap_all,
                       wqkvr[:, :, 1024 * which + 128 * hp:1024 * which + 128 * (hp + 1)],
                       reads=wdep.get("a_wqkv", ()), writes=[wb],
                       sem_group="a1wqk")
                for tt in range(4):
                    ps = self.ps[2 + which]
                    for kc in range(DC):
                        kb.op("tensor", lambda e, kc=kc, tt=tt, ps=ps, wb=wb: e.matmul(
                            ps[:, :], wb[:, 128 * kc:128 * (kc + 1)],
                            self.xnc[kc][:, 512 * tt:512 * (tt + 1)],
                            start=(kc == 0), stop=(kc == DC - 1)),
                            reads=[wb, self.xnc[kc]], writes=[ps])
                    kb.op("scalar", lambda e, tt=tt, ps=ps, dstb=dstb, bcol=bcol, hp=hp: e.activation(
                        dstb[:, 512 * tt:512 * (tt + 1)], ps[:, :], AF.Identity,
                        bias=bcol[:, hp:hp + 1]),
                        reads=[ps, bcol], writes=[dstb])
            ps6 = self.ps[6]
            for which, srcb in ((0, Qb), (1, Kb)):
                for tt in range(4):
                    kb.op("scalar", lambda e, tt=tt, srcb=srcb: e.activation(
                        sqq[:, :], srcb[:, 512 * tt:512 * (tt + 1)], AF.Square),
                        reads=[srcb], writes=[sqq])
                    kb.op("tensor", lambda e: e.matmul(
                        ps6[0:1, 0:512], self.lnsel[:, 0:1], sqq[:, :],
                        start=True, stop=True),
                        reads=[self.lnsel, sqq], writes=[ps6])
                    kb.op("vector", lambda e, tt=tt, which=which: e.reduce_max(
                        nrm[0:1, 4 * which + tt:4 * which + tt + 1],
                        ps6[0:1, 0:512], axis=AX.X),
                        reads=[ps6], writes=[nrm])
            kb.op("vector", lambda e: e.reduce_max(
                nrm[0:1, 8:9], nrm[0:1, 0:4], axis=AX.X), reads=[nrm], writes=[nrm])
            kb.op("vector", lambda e: e.reduce_max(
                nrm[0:1, 9:10], nrm[0:1, 4:8], axis=AX.X), reads=[nrm], writes=[nrm])
            kb.op("vector", lambda e: e.tensor_tensor(
                nrm[0:1, 10:11], nrm[0:1, 8:9], nrm[0:1, 9:10], ALU.add),
                reads=[nrm], writes=[nrm])
            kb.op("vector", lambda e: e.tensor_scalar(
                nrm[0:1, 11:12], nrm[0:1, 10:11], -SC16 / 2.0, None, ALU.mult),
                reads=[nrm], writes=[nrm])
            kb.op("tensor", lambda e: e.matmul(
                ps6[:, 0:1], self.bcsel[0:1, 0:128], nrm[0:1, 11:12],
                start=True, stop=True),
                reads=[self.bcsel, nrm], writes=[ps6])
            kb.op("vector", lambda e: e.tensor_copy(
                ebias[:, 0:1], ps6[:, 0:1]), reads=[ps6], writes=[ebias])
            Pst1 = self.arena_win(f"a1_P1_{hp}", 38912, T)
            alts = []
            for k in range(2):
                nslice = hp + 1 + k
                if nslice <= 7:
                    alts.append(self.arena_win(f"a1_PA{k}_{hp}",
                                               16384 + nslice * T, T))
            Pstp = [[Pst0, Pst1],
                    [alts[0] if len(alts) > 0 else Pst0,
                     alts[1] if len(alts) > 1 else Pst1]]
            psPV = [self.ps[2 + i] for i in range(4)]
            for sc in range(16):
                for h2 in range(2):
                    dsc = dscr[sc % 2]
                    pp = Pstp[sc % 2][h2]
                    for tt in range(4):
                        pS = self.ps[[0, 1, 6, 7][2 * h2 + tt % 2]]
                        kb.op("tensor", lambda e, h2=h2, sc=sc, tt=tt, pS=pS: e.matmul(
                            pS[:, :],
                            Kb[64 * h2:64 * (h2 + 1), 128 * sc:128 * (sc + 1)],
                            Qb[64 * h2:64 * (h2 + 1), 512 * tt:512 * (tt + 1)],
                            start=True, stop=True, tile_position=(64 * h2, 0)),
                            reads=[Kb, Qb], writes=[pS])
                        kb.op("scalar", lambda e, h2=h2, tt=tt, pS=pS, pp=pp, dsc=dsc: e.activation(
                            pp[:, 512 * tt:512 * (tt + 1)], pS[:, :], AF.Exp,
                            bias=ebias[:, 0:1], scale=SC16,
                            accum_out=dsc[:, 4 * h2 + tt:4 * h2 + tt + 1]),
                            reads=[pS, ebias], writes=[pp, dsc])
                    kb.op("vector", lambda e, h2=h2, dsc=dsc: e.reduce_sum(
                        dsc[:, 8 + h2:9 + h2], dsc[:, 4 * h2:4 * h2 + 4], axis=AX.X),
                        reads=[dsc], writes=[dsc])
                    kb.op("vector", lambda e, h2=h2, dsc=dsc: e.reciprocal(
                        dsc[:, 10 + h2:11 + h2], dsc[:, 8 + h2:9 + h2]),
                        reads=[dsc], writes=[dsc])
                    kb.op("vector", lambda e, h2=h2, sc=sc, hp=hp, dsc=dsc: e.tensor_scalar(
                        vt[h2][:, :], Vb[:, 1024 * sc + 128 * hp + 64 * h2:
                                         1024 * sc + 128 * hp + 64 * (h2 + 1)],
                        dsc[:, 10 + h2:11 + h2], None, ALU.mult),
                        reads=[Vb, dsc], writes=[vt[h2]])
                    for tt in range(4):
                        kb.op("tensor", lambda e, h2=h2, sc=sc, tt=tt, pp=pp: e.matmul(
                            psPV[tt][64 * h2:64 * (h2 + 1), :], vt[h2][:, :],
                            pp[:, 512 * tt:512 * (tt + 1)],
                            start=(sc == 0 and h2 == 0), stop=(sc == 15 and h2 == 1),
                            tile_position=(0, 64 * h2), skip_group_check=True),
                            reads=[vt[h2], pp], writes=[psPV[tt]])
            for tt in range(4):
                g = self.scr(tt % 2)
                kb.op("scalar", lambda e, tt=tt, g=g: e.activation(
                    g[:, 0:512], psPV[tt][:, :], AF.Gelu),
                    reads=[psPV[tt]], writes=[g])
                kb.op("vector", lambda e, tt=tt, hp=hp, g=g: e.tensor_scalar(
                    attnb[:, hp * T + 512 * tt: hp * T + 512 * (tt + 1)],
                    g[:, 0:512], INV_SQRT_S, None, ALU.mult),
                    reads=[g], writes=[attnb])
        s1b = self.arena_win("a1_s1", 0, DC * T, after=[Vb])
        wsl = [self.arena_win("a1_wsl0", 32768, DC * 256),
               self.arena_win("a1_wsl1", 34816, DC * 256),
               self.arena_win("a1_wsl2", 36864, DC * 256)]
        w2ch = [self.arena_win(f"a1_w2ch{i}", 38912 + 1024 * i, 1024)
                for i in range(2)]
        wo = wmap["a_wo"]
        wor = wo.rearrange("(c p) o -> p c o", p=128)
        for oc in range(DC):
            wb = wsl[oc % 3]
            kb.dma("sync", wb[:, 0:DC * 128],
                   wor[:, :, 128 * oc:128 * (oc + 1)],
                   reads=wdep.get("a_wo", ()), writes=[wb])
            for tt in range(4):
                ps = self.ps[4 + (tt % 2)]
                for ac in range(DC):
                    kb.op("tensor", lambda e, oc=oc, tt=tt, ac=ac, ps=ps, wb=wb: e.matmul(
                        ps[:, :], wb[:, 128 * ac:128 * (ac + 1)],
                        attnb[:, ac * T + 512 * tt: ac * T + 512 * (tt + 1)],
                        start=(ac == 0), stop=(ac == DC - 1)),
                        reads=[wb, attnb], writes=[ps])
                sl = slice(oc * T + 512 * tt, oc * T + 512 * (tt + 1))
                kb.op("vector", lambda e, ps=ps, sl=sl, oc=oc, tt=tt: e.scalar_tensor_tensor(
                    s1b[:, sl], ps[:, :], bo[:, oc:oc + 1],
                    self.xnc[oc][:, 512 * tt:512 * (tt + 1)],
                    ALU.add, ALU.add),
                    reads=[ps, bo, self.xnc[oc]], writes=[s1b])
        self.layernorm(dst=self.xn, src=s1b, gain_bias=(g1c, b1nc))
        h1b = self.arena_win("a1_h1", 16384, 32 * 512, after=[attnb])
        w1 = wmap["a_w1"]
        w2 = wmap["a_w2"]
        w1r = w1.rearrange("(c p) o -> p c o", p=128)
        for ti in range(NTT):
            toff = 512 * ti
            for sl_i in range(16):
                wb = wsl[sl_i % 3]
                kb.dma("sync", wb.ap_all,
                       w1r[:, :, 256 * sl_i:256 * (sl_i + 1)],
                       reads=wdep.get("a_w1", ()), writes=[wb])
                for hh in range(2):
                    hc = 2 * sl_i + hh
                    ps = self.ps[hc % 4]
                    for kc in range(DC):
                        kb.op("tensor", lambda e, hh=hh, kc=kc, ps=ps, wb=wb, toff=toff: e.matmul(
                            ps[:, :],
                            wb[:, 256 * kc + 128 * hh: 256 * kc + 128 * hh + 128],
                            self.xnc[kc][:, toff:toff + 512],
                            start=(kc == 0), stop=(kc == DC - 1)),
                            reads=[wb, self.xnc[kc]], writes=[ps])
                    kb.op("scalar", lambda e, hc=hc, ps=ps: e.activation(
                        h1b[:, 512 * hc:512 * (hc + 1)], ps[:, :], AF.Gelu,
                        bias=b1[:, hc:hc + 1]),
                        reads=[ps, b1], writes=[h1b])
            for kc in range(32):
                wc = w2ch[kc % 2]
                kb.dma("sync", wc.ap_all,
                       w2[128 * kc:128 * (kc + 1), :],
                       reads=wdep.get("a_w2", ()), writes=[wc])
                for dc in range(DC):
                    ps = self.ps[dc]
                    kb.op("tensor", lambda e, wc=wc, kc=kc, dc=dc, ps=ps: e.matmul(
                        ps[:, :], wc[:, 128 * dc:128 * (dc + 1)],
                        h1b[:, 512 * kc:512 * (kc + 1)],
                        start=(kc == 0), stop=(kc == 31)),
                        reads=[wc, h1b], writes=[ps])
            for dc in range(DC):
                ps = self.ps[dc]
                sl = slice(dc * T + toff, dc * T + toff + 512)
                kb.op("vector", lambda e, ps=ps, sl=sl, dc=dc, toff=toff: e.scalar_tensor_tensor(
                    s1b[:, sl], ps[:, :], b2[:, dc:dc + 1],
                    self.xnc[dc][:, toff:toff + 512],
                    ALU.add, ALU.add),
                    reads=[ps, b2, self.xnc[dc]], writes=[s1b])
        ln2out = self.arena_win("a1_ln2o", 16384, DC * T, after=[h1b])
        self.layernorm(dst=ln2out, src=s1b, gain_bias=(g2c, b2nc))
        for c in range(DC):
            Xc = self.Xc[c]
            kb.op("vector", lambda e, c=c, Xc=Xc: e.tensor_tensor(
                Xc[:, 0:T], Xc[:, 0:T],
                ln2out[:, c * T:(c + 1) * T], ALU.add),
                reads=[Xc, ln2out], writes=[Xc])
        return


    def attn0(self, wmap, cols):
        kb = self.kb
        lq1, lk1, lq2, lk2, rmsw, ones1 = (
            cols[k] for k in ("lq1", "lk1", "lq2", "lk2", "rmsw", "ones1"))
        SCL = 1.0 / math.sqrt(32.0)
        wdep = wmap.get("deps", {})
        LAMBDA_INIT = 0.8 - 0.6 * math.exp(-0.3 * 12)
        wq, wkd, wv, wo = (wmap[k] for k in ("l0_wq", "l0_wkd", "l0_wv", "l0_wo"))
        Vb = self.arena_win("a0_V", 0, 16 * 512)
        attnb = self.arena_win("a0_attn", 8192, DC * T)
        Qc = self.arena_win("a0_Qc", 24576, T)
        Kc = self.arena_win("a0_Kc", 26624, T)
        raw = self.arena_win("a0_raw", 28672, 512)
        shf = self.arena_win("a0_shf", 29184, 512)
        v65 = self.arena_win("a0_v65", 29696, 16 * 65)
        wqb = self.arena_win("a0_wqb", 31776, DC * 128)
        wkb = self.arena_win("a0_wkb", 32800, DC * 128)
        wvb = self.arena_win("a0_wvb", 33824, DC * 256)
        Pbp = [self.arena_win("a0_Pb0", 30752, 1024),
               self.arena_win("a0_Pb1", 33824, 1024, after=[wvb]),
               self.arena_win("a0_Pb2", 34848, 1024, after=[wvb])]
        wob = [self.arena_win("a0_wob0", 31776, DC * 128)]
        ropec = self.arena_win("a0_ropec", 35872, T)
        ropes = self.arena_win("a0_ropes", 37920, T)
        cmask = self.arena_win("a0_cmask", 39968, 896)
        kb.dma("sync", ropec.ap_all, cols["ropec_d"], writes=[ropec])
        kb.dma("sync", ropes.ap_all, cols["ropes_d"], writes=[ropes])
        kb.dma("sync", cmask.ap_all, cols["cmask_d"], writes=[cmask])
        oc2 = [[self.scr(2), self.scr(3)], [self.scr(0), self.scr(1)]]
        _dbh = self.scr(4).h.bitcast(BF16)
        dbr2 = []
        for h2 in range(2):
            b = Buf(_dbh, f"a0_dbr{h2}", win=(0, 1, 512 * h2, 512))
            kb.bufs.append(b)
            dbr2.append([b, b])
        sqq2 = [self.xsq[0], self.xbb[0]]
        lsc = self.buf("a0_lsc", [1, 64], F32)
        lcol = self.buf("a0_lcol", [128, 1], F32)
        ebias = self.buf("a0_eb", [128, 1], F32)
        nrm = self.buf("a0_nrm", [1, 16], F32)
        sqq = sqq2[0]
        wqr = wq.rearrange("(c p) o -> p c o", p=128)
        wkr = wkd.rearrange("(c p) o -> p c o", p=128)
        wvr = wv.rearrange("(c p) o -> p c o", p=128)
        wor = wo.rearrange("(c p) o -> p c o", p=128)

        kb.op("vector", lambda e: e.tensor_tensor(
            lsc[0:1, 0:32], lq1[0:1, 0:32], lk1[0:1, 0:32], ALU.mult),
            reads=[lq1, lk1], writes=[lsc])
        kb.op("vector", lambda e: e.tensor_tensor(
            lsc[0:1, 32:64], lq2[0:1, 0:32], lk2[0:1, 0:32], ALU.mult),
            reads=[lq2, lk2], writes=[lsc])
        kb.op("vector", lambda e: e.reduce_sum(
            lsc[0:1, 0:1], lsc[0:1, 0:32], axis=AX.X), reads=[lsc], writes=[lsc])
        kb.op("vector", lambda e: e.reduce_sum(
            lsc[0:1, 1:2], lsc[0:1, 32:64], axis=AX.X), reads=[lsc], writes=[lsc])
        kb.op("scalar", lambda e: e.activation(
            lsc[0:1, 2:4], lsc[0:1, 0:2], AF.Exp), reads=[lsc], writes=[lsc])
        kb.op("vector", lambda e: e.tensor_tensor(
            lsc[0:1, 4:5], lsc[0:1, 2:3], lsc[0:1, 3:4], ALU.subtract),
            reads=[lsc], writes=[lsc])
        kb.op("vector", lambda e: e.tensor_scalar(
            lsc[0:1, 5:6], lsc[0:1, 4:5], LAMBDA_INIT, -1.0, ALU.add, ALU.mult),
            reads=[lsc], writes=[lsc])
        psL = self.ps[6]
        kb.op("tensor", lambda e: e.matmul(
            psL[:, 0:1], self.bcsel[0:1, 0:128], lsc[0:1, 5:6],
            start=True, stop=True), reads=[self.bcsel, lsc], writes=[psL])
        kb.op("vector", lambda e: e.tensor_copy(
            lcol[:, 0:1], psL[:, 0:1]), reads=[psL], writes=[lcol])

        for vh in range(2):
            kb.dma("sync", wvb.ap_all, wvr[:, :, 256 * vh:256 * (vh + 1)],
                   reads=wdep.get("l0_wv", ()), writes=[wvb])
            for tc in range(16):
                ps = self.ps[tc % 2]
                for kc in range(DC):
                    kb.op("tensor", lambda e, tc=tc, kc=kc, ps=ps: e.matmul(
                        ps[:, 0:256],
                        self.xnc[kc][:, 128 * tc:128 * (tc + 1)],
                        wvb[:, 256 * kc:256 * (kc + 1)],
                        start=(kc == 0), stop=(kc == DC - 1)),
                        reads=[self.xnc[kc], wvb], writes=[ps])
                kb.op("vector", lambda e, tc=tc, vh=vh, ps=ps: e.tensor_copy(
                    Vb[:, 512 * tc + 256 * vh:512 * tc + 256 * (vh + 1)],
                    ps[:, 0:256]),
                    reads=[ps], writes=[Vb])

        for hc in range(DC):
            for which, wsb, wsrc, dst in ((1, wkb, wkr, Kc), (0, wqb, wqr, Qc)):
                kb.dma("sync", wsb.ap_all,
                       wsrc[:, :, 128 * hc:128 * (hc + 1)],
                       reads=wdep.get("l0_wkd" if which else "l0_wq", ()),
                       writes=[wsb])
                for tt in range(4):
                    ps = self.ps[2 + which]
                    for kc in range(DC):
                        kb.op("tensor", lambda e, kc=kc, tt=tt, ps=ps, wsb=wsb: e.matmul(
                            ps[:, :], wsb[:, 128 * kc:128 * (kc + 1)],
                            self.xnc[kc][:, 512 * tt:512 * (tt + 1)],
                            start=(kc == 0), stop=(kc == DC - 1)),
                            reads=[wsb, self.xnc[kc]], writes=[ps])
                    tsl = slice(512 * tt, 512 * (tt + 1))
                    kb.op("scalar", lambda e, ps=ps: e.copy(
                        raw[:, 0:512], ps[:, :]), reads=[ps], writes=[raw])
                    mask = list(range(16, 32)) + list(range(0, 16))
                    kb.op("vector", lambda e, mask=mask: e.stream_shuffle(
                        shf[:, 0:512], raw[:, 0:512], mask), reads=[raw], writes=[shf])
                    kb.op("gpsimd", lambda e, tsl=tsl: e.tensor_tensor(
                        raw[:, 0:512], raw[:, 0:512], ropec[:, tsl], ALU.mult),
                        reads=[raw, ropec], writes=[raw])
                    kb.op("gpsimd", lambda e, tsl=tsl: e.tensor_tensor(
                        shf[:, 0:512], shf[:, 0:512], ropes[:, tsl], ALU.mult),
                        reads=[shf, ropes], writes=[shf])
                    kb.op("vector", lambda e, tsl=tsl, dst=dst: e.tensor_tensor(
                        dst[:, tsl], raw[:, 0:512], shf[:, 0:512], ALU.add),
                        reads=[raw, shf], writes=[dst])
            ps6 = self.ps[6]
            for which, srcb in ((0, Qc), (1, Kc)):
                for tt in range(4):
                    kb.op("scalar", lambda e, tt=tt, srcb=srcb: e.activation(
                        sqq[:, :], srcb[:, 512 * tt:512 * (tt + 1)], AF.Square),
                        reads=[srcb], writes=[sqq])
                    kb.op("tensor", lambda e: e.matmul(
                        ps6[0:1, 0:512], self.lnsel[:, 0:1], sqq[:, :],
                        start=True, stop=True),
                        reads=[self.lnsel, sqq], writes=[ps6])
                    kb.op("vector", lambda e, tt=tt, which=which: e.reduce_max(
                        nrm[0:1, 4 * which + tt:4 * which + tt + 1],
                        ps6[0:1, 0:512], axis=AX.X),
                        reads=[ps6], writes=[nrm])
            kb.op("vector", lambda e: e.reduce_max(
                nrm[0:1, 8:9], nrm[0:1, 0:4], axis=AX.X), reads=[nrm], writes=[nrm])
            kb.op("vector", lambda e: e.reduce_max(
                nrm[0:1, 9:10], nrm[0:1, 4:8], axis=AX.X), reads=[nrm], writes=[nrm])
            kb.op("vector", lambda e: e.tensor_tensor(
                nrm[0:1, 10:11], nrm[0:1, 8:9], nrm[0:1, 9:10], ALU.add),
                reads=[nrm], writes=[nrm])
            kb.op("vector", lambda e: e.tensor_scalar(
                nrm[0:1, 11:12], nrm[0:1, 10:11], -SCL / 2.0, None, ALU.mult),
                reads=[nrm], writes=[nrm])
            kb.op("tensor", lambda e: e.matmul(
                ps6[:, 0:1], self.bcsel[0:1, 0:128], nrm[0:1, 11:12],
                start=True, stop=True), reads=[self.bcsel, nrm], writes=[ps6])
            kb.op("vector", lambda e: e.tensor_copy(
                ebias[:, 0:1], ps6[:, 0:1]), reads=[ps6], writes=[ebias])
            for sc in range(16):
                kb.op("vector", lambda e, sc=sc, hc=hc: e.tensor_copy(
                    v65[:, 65 * sc:65 * sc + 64],
                    Vb[:, 512 * sc + 64 * hc: 512 * sc + 64 * (hc + 1)]),
                    reads=[Vb], writes=[v65])
                kb.op("vector", lambda e, sc=sc: e.memset(
                    v65[:, 65 * sc + 64:65 * (sc + 1)], 1.0), writes=[v65])
            psS = [self.ps[i] for i in range(4)]
            psPV = [self.ps[4 + i] for i in range(4)]
            for tt in range(4):
                nsc = 4 * (tt + 1)
                for sc in range(nsc):
                    for u in range(4):
                        pS = psS[u]
                        kb.op("tensor", lambda e, sc=sc, tt=tt, u=u, pS=pS: e.matmul(
                            pS[:, :],
                            Kc[32 * u:32 * (u + 1), 128 * sc:128 * (sc + 1)],
                            Qc[32 * u:32 * (u + 1), 512 * tt:512 * (tt + 1)],
                            start=True, stop=True, tile_position=(32 * u, 0)),
                            reads=[Kc, Qc], writes=[pS])
                    diag = 512 * tt - 128 * sc
                    for j in range(2):
                        pbp = Pbp[(2 * sc + j) % 3]
                        kb.op("scalar", lambda e, j=j, pbp=pbp: e.activation(
                            pbp[:, 0:1024], self.psd[j][:, 0:1024], AF.Exp,
                            bias=ebias[:, 0:1], scale=SCL),
                            reads=[psS[2 * j], psS[2 * j + 1], ebias], writes=[pbp])
                        for i, u in ((0, 2 * j), (1, 2 * j + 1)):
                            if diag < 127:
                                kb.op("gpsimd", lambda e, pbp=pbp, i=i, diag=diag:
                                    e.tensor_tensor(
                                        pbp[:, 512 * i:512 * (i + 1)],
                                        pbp[:, 512 * i:512 * (i + 1)],
                                        cmask[:, diag + 384:diag + 384 + 512],
                                        ALU.mult),
                                    reads=[pbp, cmask], writes=[pbp])
                            kb.op("tensor", lambda e, u=u, i=i, sc=sc, nsc=nsc, pbp=pbp:
                                e.matmul(
                                    psPV[u][0:65, :], v65[:, 65 * sc:65 * (sc + 1)],
                                    pbp[:, 512 * i:512 * (i + 1)],
                                    start=(sc == 0), stop=(sc == nsc - 1)),
                                reads=[v65, pbp], writes=[psPV[u]])
                chains = []
                for h2 in range(2):
                    u0, u1 = 2 * h2, 2 * h2 + 1
                    bc0, bc1 = psPV[u0], psPV[u1]
                    oc = oc2[h2]
                    sqh = sqq2[h2]
                    dbr = dbr2[h2]
                    ch = []
                    for i, u, bc in ((0, u0, bc0), (1, u1, bc1)):
                        def _rcp(e, i=i, u=u, dbr=dbr):
                            with e.bass.allow_low_precision(reason="1/d bf16 ok"):
                                return e.reciprocal(
                                    dbr[i][0:1, 0:512], psPV[u][64:65, :])
                        ch.append(("vector", _rcp, [psPV[u]], [dbr[i]]))
                        ch.append(("scalar", lambda e, u=u, oci=oc[i]: e.copy(
                            oci[0:64, 0:512], psPV[u][0:64, :]),
                            [psPV[u]], [oc[i]]))
                        ch.append(("tensor", lambda e, bc=bc, db=dbr[i]: e.matmul(
                            bc[:, :], ones1[0:1, 0:128], db[0:1, 0:512],
                            start=True, stop=True),
                            [ones1, dbr[i]], [bc]))
                    ch.append(("vector", lambda e, oc0=oc[0], bc0=bc0:
                        e.tensor_tensor(
                            oc0[0:64, 0:512], oc0[0:64, 0:512], bc0[0:64, :],
                            ALU.mult),
                        [oc[0], bc0], [oc[0]]))
                    ch.append(("vector", lambda e, oc1=oc[1], bc1=bc1:
                        e.tensor_tensor(
                            oc1[0:64, 0:512], oc1[0:64, 0:512], bc1[0:64, :],
                            ALU.mult),
                        [oc[1], bc1], [oc[1]]))
                    ch.append(("vector", lambda e, oc0=oc[0], oc1=oc[1]:
                        e.scalar_tensor_tensor(
                            oc0[0:64, 0:512], oc1[0:64, 0:512], lcol[0:64, 0:1],
                            oc0[0:64, 0:512], ALU.mult, ALU.add),
                        [oc[1], lcol, oc[0]], [oc[0]]))
                    ch.append(("scalar", lambda e, sqh=sqh, oc0=oc[0]: e.activation(
                        sqh[0:64, :], oc0[0:64, 0:512], AF.Square),
                        [oc[0]], [sqh]))
                    ch.append(("tensor", lambda e, bc0=bc0, sqh=sqh: e.matmul(
                        bc0[0:1, :], self.lnsel[0:64, 0:1], sqh[0:64, :],
                        start=True, stop=True),
                        [self.lnsel, sqh], [bc0]))
                    epsc = self.const_col(1e-6)
                    ch.append(("scalar", lambda e, bc0=bc0, epsc=epsc: e.activation(
                        bc0[0:1, :], bc0[0:1, :], AF.Ln,
                        bias=epsc[0:1, 0:1], scale=1.0 / 64.0),
                        [bc0, self.ccol], [bc0]))
                    ch.append(("scalar", lambda e, bc0=bc0, db=dbr[0]: e.activation(
                        db[0:1, 0:512], bc0[0:1, :], AF.Exp, scale=-0.5),
                        [bc0], [dbr[0]]))
                    ch.append(("tensor", lambda e, bc1=bc1, db=dbr[0]: e.matmul(
                        bc1[:, :], ones1[0:1, 0:128], db[0:1, 0:512],
                        start=True, stop=True),
                        [ones1, dbr[0]], [bc1]))
                    ch.append(("vector", lambda e, oc0=oc[0], bc1=bc1:
                        e.tensor_tensor(
                            oc0[0:64, 0:512], oc0[0:64, 0:512], bc1[0:64, :],
                            ALU.mult),
                        [oc[0], bc1], [oc[0]]))
                    osl = slice(hc * T + 512 * tt, hc * T + 512 * (tt + 1))
                    ch.append(("vector", lambda e, h2=h2, osl=osl, oc0=oc[0]:
                        e.tensor_scalar(
                            attnb[64 * h2:64 * (h2 + 1), osl], oc0[0:64, 0:512],
                            rmsw[0:64, 0:1], 1.0 - LAMBDA_INIT, ALU.mult, ALU.mult),
                        [oc[0], rmsw], [attnb]))
                    chains.append(ch)
                for pair in itertools.zip_longest(*chains):
                    for item in pair:
                        if item is not None:
                            eng, emit, rd, wr = item
                            kb.op(eng, emit, reads=rd, writes=wr)
        for oc2 in range(DC):
            wb = wob[0]
            kb.dma("sync", wb.ap_all,
                   wor[:, :, 128 * oc2:128 * (oc2 + 1)],
                   reads=wdep.get("l0_wo", ()), writes=[wb])
            for tt in range(4):
                ps = self.ps[4 + (tt % 2)]
                for ac in range(DC):
                    kb.op("tensor", lambda e, tt=tt, ac=ac, ps=ps, wb=wb: e.matmul(
                        ps[:, :], wb[:, 128 * ac:128 * (ac + 1)],
                        attnb[:, ac * T + 512 * tt: ac * T + 512 * (tt + 1)],
                        start=(ac == 0), stop=(ac == DC - 1)),
                        reads=[wb, attnb], writes=[ps])
                Xc = self.Xc[oc2]
                kb.op("vector", lambda e, ps=ps, tt=tt, Xc=Xc: e.tensor_tensor(
                    Xc[:, 512 * tt:512 * (tt + 1)], Xc[:, 512 * tt:512 * (tt + 1)],
                    ps[:, :], ALU.add),
                    reads=[ps, Xc], writes=[Xc])
        return



def build_ffn1_program():
    kb = KB()
    xT = kb.dram_in("xT", [D, T])
    w1 = kb.dram_in("f1_w1", [1024, 4096])
    w2 = kb.dram_in("f1_w2", [4096, 1024])
    b1cd = kb.dram_in("f1_b1c", [128, 32])
    b2cd = kb.dram_in("f1_b2c", [128, 8])
    clnsel = kb.dram_in("c_lnsel", [128, 16 * NTT], BF16)
    cbcsel = kb.dram_in("c_bcsel", [16, 128 * NTT])
    yT = kb.dram_out("yT", [D, T])

    m = Model(kb)
    b1c = kb.sbuf("b1c", [128, 32], F32)
    b2c = kb.sbuf("b2c", [128, 8], F32)
    _xTr = xT.rearrange("(c p) t -> p c t", p=128)
    for _c in range(DC):
        kb.dma("sync", m.Xc[_c][:, 0:T], _xTr[:, _c, :], writes=[m.Xc[_c]])
    kb.dma("sync", m.lnsel.ap_all, clnsel, writes=[m.lnsel])
    kb.dma("sync", m.bcsel.ap_all, cbcsel, writes=[m.bcsel])
    kb.dma("sync", b1c.ap_all, b1cd, writes=[b1c])
    kb.dma("sync", b2c.ap_all, b2cd, writes=[b2c])
    m.layernorm()
    m.ffn1({"w1": w1, "w2": w2}, b1c, b2c)
    yTr = yT.rearrange("(c p) t -> p c t", p=128)
    for c in range(DC):
        kb.dma("sync", yTr[:, c, :], m.Xc[c][:, 0:T], reads=[m.Xc[c]])
    kb.finalize()
    return kb


def prep_ffn1_inputs(w1, b1, w2, b2):
    return {
        "f1_w1": np.ascontiguousarray(w1, dtype=np.float32),
        "f1_w2": np.ascontiguousarray(w2, dtype=np.float32),
        "f1_b1c": np.ascontiguousarray(np.asarray(b1, np.float32).reshape(32, 128).T),
        "f1_b2c": np.ascontiguousarray(np.asarray(b2, np.float32).reshape(8, 128).T),
    }

    def _pwl(self, psm, b1c, hc, hbuf, hoff):
        kb = self.kb
        nt, ncl, jj, tt_ = self.scr(0), self.scr(1), self.scr(2), self.scr(3)
        g0, g1 = self.scr(4), self.scr(2)
        dd = self.scr(1)
        yy = self.scr(3)
        msk = self.mscr
        p0, p1 = psm
        kb.op("vector", lambda e: e.tensor_scalar(
            nt[:, 0:512], p0.h[:, :], b1c[:, hc:hc + 1], 256.0, ALU.add, ALU.mult),
            reads=[p0, b1c], writes=[nt])
        kb.op("vector", lambda e: e.tensor_scalar(
            nt[:, 512:1024], p1.h[:, :], b1c[:, hc:hc + 1], 256.0, ALU.add, ALU.mult),
            reads=[p1, b1c], writes=[nt])
        kb.op("vector", lambda e: e.tensor_scalar(
            nt[:, :], nt[:, :], MAGIC, MAGIC, ALU.add, ALU.subtract),
            reads=[nt], writes=[nt])
        kb.op("vector", lambda e: e.tensor_scalar(
            ncl[:, :], nt[:, :], -1024.0, 1024.0, ALU.max, ALU.min),
            reads=[nt], writes=[ncl])
        kb.op("vector", lambda e: e.tensor_scalar(
            ncl[:, :], ncl[:, :], 1.0 / 64.0, 63.0 / 128.0, ALU.mult, ALU.subtract),
            reads=[ncl], writes=[ncl])
        kb.op("vector", lambda e: e.tensor_scalar(
            jj[:, :], ncl[:, :], MAGIC, MAGIC, ALU.add, ALU.subtract),
            reads=[ncl], writes=[jj])
        kb.op("vector", lambda e: e.tensor_scalar(
            jj[:, :], jj[:, :], 15.0, None, ALU.min),
            reads=[jj], writes=[jj])
        kb.op("vector", lambda e: e.tensor_scalar(
            ncl[:, :], ncl[:, :], 63.0 / 128.0, None, ALU.add),
            reads=[ncl], writes=[ncl])
        kb.op("gpsimd", lambda e: e.tensor_tensor(
            tt_[:, :], ncl[:, :], jj[:, :], ALU.subtract),
            reads=[ncl, jj], writes=[tt_])
        kb.op("scalar", lambda e: e.activation(
            g0[:, :], jj[:, :], AF.Gelu, bias=0.0, scale=0.25),
            reads=[jj], writes=[g0])
        kb.op("scalar", lambda e: e.activation(
            g1[:, :], jj[:, :], AF.Gelu, bias=0.25, scale=0.25),
            reads=[jj], writes=[g1])
        kb.op("gpsimd", lambda e: e.tensor_tensor(
            dd[:, :], g1[:, :], g0[:, :], ALU.subtract),
            reads=[g1, g0], writes=[dd])
        kb.op("gpsimd", lambda e: e.tensor_tensor(
            dd[:, :], tt_[:, :], dd[:, :], ALU.mult),
            reads=[tt_, dd], writes=[dd])
        kb.op("gpsimd", lambda e: e.tensor_tensor(
            yy[:, :], dd[:, :], g0[:, :], ALU.add),
            reads=[dd, g0], writes=[yy])
        kb.op("vector", lambda e: e.tensor_scalar(
            yy[:, :], yy[:, :], 256.0, MAGIC, ALU.mult, ALU.add),
            reads=[yy], writes=[yy])
        kb.op("vector", lambda e: e.tensor_scalar(
            yy[:, :], yy[:, :], MAGIC, None, ALU.subtract),
            reads=[yy], writes=[yy])
        kb.op("vector", lambda e: e.tensor_scalar(
            msk[:, :], nt[:, :], 1024.0, None, ALU.is_gt),
            reads=[nt], writes=[msk])
        kb.op("vector", lambda e: e.select(
            yy[:, :], msk[:, :], nt[:, :], yy[:, :]),
            reads=[msk, nt, yy], writes=[yy])
        kb.op("vector", lambda e: e.tensor_scalar(
            hbuf[:, hoff:hoff + 1024], yy[:, :], 1.0 / 256.0, None, ALU.mult),
            reads=[yy], writes=[hbuf])


def build_moe_program():
    kb = KB()
    xT = kb.dram_in("xT", [D, T])
    w1 = kb.dram_in("m_w1", [9 * 1024, 1408])
    w3 = kb.dram_in("m_w3", [9 * 1024, 1408])
    w2 = kb.dram_in("m_w2", [9 * 1408, 1024])
    b1cd = kb.dram_in("m_b1c", [128, 99])
    b3cd = kb.dram_in("m_b3c", [128, 99])
    b2sd = kb.dram_in("m_b2s", [9, 1024])
    gwtd = kb.dram_in("g_gwt", [128, 64])
    gbtd = kb.dram_in("g_gbt", [128, 8])
    id128d = kb.dram_in("c_id128", [128, 128])
    eseld = kb.dram_in("c_esel", [9, 9 * 128], BF16)
    clnsel = kb.dram_in("c_lnsel", [128, 16 * NTT], BF16)
    cbcsel = kb.dram_in("c_bcsel", [16, 128 * NTT])
    yT = kb.dram_out("yT", [D, T])

    m = Model(kb)
    b1c = kb.sbuf("b1c", [128, 99], F32)
    b3c = kb.sbuf("b3c", [128, 99], F32)
    b2s = kb.sbuf("b2s", [9, 1024], BF16)
    gwt = kb.sbuf("gwt", [128, 64], BF16)
    gbt = kb.sbuf("gbt", [128, 8], F32)
    id128 = kb.sbuf("id128", [128, 128], F32)
    esel = kb.sbuf("esel", [9, 9 * 128], BF16)
    _xTr = xT.rearrange("(c p) t -> p c t", p=128)
    for _c in range(DC):
        kb.dma("sync", m.Xc[_c][:, 0:T], _xTr[:, _c, :], writes=[m.Xc[_c]])
    kb.dma("sync", m.lnsel.ap_all, clnsel, writes=[m.lnsel])
    kb.dma("sync", m.bcsel.ap_all, cbcsel, writes=[m.bcsel])
    kb.dma("sync", b1c.ap_all, b1cd, writes=[b1c])
    kb.dma("sync", b3c.ap_all, b3cd, writes=[b3c])
    kb.dma("gpsimd", b2s.ap_all, b2sd, writes=[b2s])
    kb.dma("gpsimd", gwt.ap_all, gwtd, writes=[gwt])
    kb.dma("sync", gbt.ap_all, gbtd, writes=[gbt])
    kb.dma("sync", id128.ap_all, id128d, writes=[id128])
    kb.dma("sync", esel.ap_all, eseld, writes=[esel])
    m.layernorm()
    m.moe({"m_w1": w1, "m_w3": w3, "m_w2": w2}, b1c, b3c,
          {"gwt": gwt, "gbt": gbt, "b2s": b2s, "id128": id128, "esel": esel})
    yTr = yT.rearrange("(c p) t -> p c t", p=128)
    for c in range(DC):
        kb.dma("sync", yTr[:, c, :], m.Xc[c][:, 0:T], reads=[m.Xc[c]])
    kb.finalize()
    return kb


def _esel_const():
    import ml_dtypes
    esel = np.zeros((9, 9 * 128), dtype=ml_dtypes.bfloat16)
    for e in range(9):
        esel[e, 128 * e:128 * (e + 1)] = 1.0
    return esel


def prep_moe_inputs(gate_w, gate_b, w1, b1, w3, b3, w2, b2, sw1, sb1, sw3, sb3, sw2, sb2):
    f = np.float32
    w1a = np.concatenate([np.asarray(w1, f).reshape(8 * 1024, 1408),
                          np.asarray(sw1, f)], axis=0)
    w3a = np.concatenate([np.asarray(w3, f).reshape(8 * 1024, 1408),
                          np.asarray(sw3, f)], axis=0)
    w2a = np.concatenate([np.asarray(w2, f).reshape(8 * 1408, 1024),
                          np.asarray(sw2, f)], axis=0)
    b1a = np.concatenate([np.asarray(b1, f), np.asarray(sb1, f)[None]], axis=0)
    b3a = np.concatenate([np.asarray(b3, f), np.asarray(sb3, f)[None]], axis=0)
    b2a = np.concatenate([np.asarray(b2, f), np.asarray(sb2, f)[None]], axis=0)
    b1c = np.concatenate([b1a[e].reshape(11, 128).T for e in range(9)], axis=1)
    b3c = np.concatenate([b3a[e].reshape(11, 128).T for e in range(9)], axis=1)
    gw = np.asarray(gate_w, f)
    gwt = np.zeros((128, 64), f)
    for kc in range(8):
        gwt[:, 8 * kc:8 * (kc + 1)] = gw[:, 128 * kc:128 * (kc + 1)].T
    gbt = np.broadcast_to(np.asarray(gate_b, f)[None, :], (128, 8)).copy()
    return {
        "m_w1": np.ascontiguousarray(w1a), "m_w3": np.ascontiguousarray(w3a),
        "m_w2": np.ascontiguousarray(w2a),
        "m_b1c": np.ascontiguousarray(b1c), "m_b3c": np.ascontiguousarray(b3c),
        "m_b2s": np.ascontiguousarray(b2a),
        "g_gwt": gwt, "g_gbt": gbt,
        "c_id128": np.eye(128, dtype=f),
        "c_esel": _esel_const(),
    }


def build_attn1_program():
    kb = KB()
    xT = kb.dram_in("xT", [D, T])
    wqkv = kb.dram_in("a_wqkv", [1024, 3072])
    wo = kb.dram_in("a_wo", [1024, 1024])
    w1 = kb.dram_in("a_w1", [1024, 4096])
    w2 = kb.dram_in("a_w2", [4096, 1024])
    colsd = {}
    for nm, sh in (("a_bq", [128, 8]), ("a_bk", [128, 8]), ("a_bo", [128, 8]),
                   ("a_b1", [128, 32]), ("a_b2", [128, 8]),
                   ("a_g1", [128, 8]), ("a_b1n", [128, 8]),
                   ("a_g2", [128, 8]), ("a_b2n", [128, 8])):
        colsd[nm] = kb.dram_in(nm, sh)
    bvrowd = kb.dram_in("a_bvrow", [1, 1024])
    ones1d = kb.dram_in("c_ones1", [1, 128])
    clnsel = kb.dram_in("c_lnsel", [128, 16 * NTT], BF16)
    cbcsel = kb.dram_in("c_bcsel", [16, 128 * NTT])
    yT = kb.dram_out("yT", [D, T])

    m = Model(kb)
    cols = {}
    for nm, key in (("a_bq", "bq"), ("a_bk", "bk"), ("a_bo", "bo"),
                    ("a_b1", "b1"), ("a_b2", "b2"), ("a_g1", "g1"),
                    ("a_b1n", "b1n"), ("a_g2", "g2"), ("a_b2n", "b2n")):
        b = kb.sbuf("sb_" + nm, list(colsd[nm].shape), F32)
        kb.dma("sync", b.ap_all, colsd[nm], writes=[b])
        cols[key] = b
    cols["bvrow_d"] = bvrowd
    ones1 = kb.sbuf("sb_ones1", [1, 128], BF16)
    kb.dma("gpsimd", ones1.ap_all, ones1d, writes=[ones1])
    cols["ones1"] = ones1
    _xTr = xT.rearrange("(c p) t -> p c t", p=128)
    for _c in range(DC):
        kb.dma("sync", m.Xc[_c][:, 0:T], _xTr[:, _c, :], writes=[m.Xc[_c]])
    kb.dma("sync", m.lnsel.ap_all, clnsel, writes=[m.lnsel])
    kb.dma("sync", m.bcsel.ap_all, cbcsel, writes=[m.bcsel])
    m.layernorm()
    m.attn1({"a_wqkv": wqkv, "a_wo": wo, "a_w1": w1, "a_w2": w2}, cols)
    yTr = yT.rearrange("(c p) t -> p c t", p=128)
    for c in range(DC):
        kb.dma("sync", yTr[:, c, :], m.Xc[c][:, 0:T], reads=[m.Xc[c]])
    kb.finalize()
    return kb


def prep_attn1_inputs(in_proj_w, in_proj_b, out_w, out_b, lin1_w, lin1_b,
                      lin2_w, lin2_b, ln1_g, ln1_b, ln2_g, ln2_b):
    f = np.float32
    import ml_dtypes
    col = lambda v, n: np.ascontiguousarray(np.asarray(v, f).reshape(n, 128).T)
    return {
        "a_wqkv": np.ascontiguousarray(np.asarray(in_proj_w, f).T),
        "a_wo": np.ascontiguousarray(np.asarray(out_w, f).T),
        "a_w1": np.ascontiguousarray(np.asarray(lin1_w, f).T),
        "a_w2": np.ascontiguousarray(np.asarray(lin2_w, f).T),
        "a_bq": col(np.asarray(in_proj_b, f)[0:1024], 8),
        "a_bk": col(np.asarray(in_proj_b, f)[1024:2048], 8),
        "a_bvrow": np.asarray(in_proj_b, f)[None, 2048:3072].copy(),
        "a_bo": col(out_b, 8),
        "a_b1": col(lin1_b, 32),
        "a_b2": col(lin2_b, 8),
        "a_g1": col(ln1_g, 8), "a_b1n": col(ln1_b_ := ln1_b, 8),
        "a_g2": col(ln2_g, 8), "a_b2n": col(ln2_b, 8),
        "c_ones1": np.ones((1, 128), f),
    }


def build_attn0_program():
    kb = KB()
    xT = kb.dram_in("xT", [D, T])
    wq = kb.dram_in("l0_wq", [1024, 1024])
    wkd = kb.dram_in("l0_wkd", [1024, 1024])
    wv = kb.dram_in("l0_wv", [1024, 512])
    wo = kb.dram_in("l0_wo", [1024, 1024])
    lqd = {nm: kb.dram_in(nm, [1, 32]) for nm in ("l0_lq1", "l0_lk1", "l0_lq2", "l0_lk2")}
    rmswd = kb.dram_in("l0_rmsw", [128, 1])
    ropecd = kb.dram_in("c_ropec", [128, T], BF16)
    ropesd = kb.dram_in("c_ropes", [128, T], BF16)
    cmaskd = kb.dram_in("c_cmask", [128, 896], BF16)
    ones1d = kb.dram_in("c_ones1", [1, 128])
    clnsel = kb.dram_in("c_lnsel", [128, 16 * NTT], BF16)
    cbcsel = kb.dram_in("c_bcsel", [16, 128 * NTT])
    yT = kb.dram_out("yT", [D, T])

    m = Model(kb)
    cols = {}
    for nm, key in (("l0_lq1", "lq1"), ("l0_lk1", "lk1"),
                    ("l0_lq2", "lq2"), ("l0_lk2", "lk2")):
        b = kb.sbuf("sb_" + nm, [1, 32], F32)
        kb.dma("sync", b.ap_all, lqd[nm], writes=[b])
        cols[key] = b
    rmsw = kb.sbuf("sb_rmsw", [128, 1], F32)
    kb.dma("sync", rmsw.ap_all, rmswd, writes=[rmsw])
    cols["rmsw"] = rmsw
    cols["ropec_d"] = ropecd
    cols["ropes_d"] = ropesd
    cols["cmask_d"] = cmaskd
    ones1 = kb.sbuf("sb_ones1", [1, 128], BF16)
    kb.dma("gpsimd", ones1.ap_all, ones1d, writes=[ones1])
    cols["ones1"] = ones1
    _xTr = xT.rearrange("(c p) t -> p c t", p=128)
    for _c in range(DC):
        kb.dma("sync", m.Xc[_c][:, 0:T], _xTr[:, _c, :], writes=[m.Xc[_c]])
    kb.dma("sync", m.lnsel.ap_all, clnsel, writes=[m.lnsel])
    kb.dma("sync", m.bcsel.ap_all, cbcsel, writes=[m.bcsel])
    m.layernorm()
    m.attn0({"l0_wq": wq, "l0_wkd": wkd, "l0_wv": wv, "l0_wo": wo}, cols)
    yTr = yT.rearrange("(c p) t -> p c t", p=128)
    for c in range(DC):
        kb.dma("sync", yTr[:, c, :], m.Xc[c][:, 0:T], reads=[m.Xc[c]])
    kb.finalize()
    return kb


def rope_consts():
    import ml_dtypes
    bf = ml_dtypes.bfloat16
    HD = 32
    inv = 1.0 / 10000.0 ** (np.arange(0, HD, 2, dtype=np.float32) / HD)
    t = np.arange(T, dtype=np.float32)
    fr = np.outer(t, inv)
    cosv, sinv = np.cos(fr), np.sin(fr)
    C = np.zeros((128, T), np.float32)
    S = np.zeros((128, T), np.float32)
    for p in range(128):
        C[p] = cosv[:, p % 16]
        S[p] = sinv[:, p % 16] * (1.0 if (p % 32) < 16 else -1.0)
    cm = np.zeros((128, 896), np.float32)
    for p in range(128):
        cm[p] = (p <= np.arange(896) - 384)
    return {"c_ropec": C.astype(bf), "c_ropes": S.astype(bf),
            "c_cmask": cm.astype(bf)}


def prep_attn0_inputs(Wq, Wk, Wv, Wo, lq1, lk1, lq2, lk2, rms_w):
    f = np.float32
    Wk = np.asarray(Wk, f)
    wkd = np.zeros((1024, 1024), f)
    for hc in range(8):
        wkd[:, 128 * hc:128 * hc + 64] = Wk[:, 64 * hc:64 * (hc + 1)]
        wkd[:, 128 * hc + 64:128 * (hc + 1)] = Wk[:, 64 * hc:64 * (hc + 1)]
    return {
        "l0_wq": np.ascontiguousarray(np.asarray(Wq, f)),
        "l0_wkd": wkd,
        "l0_wv": np.ascontiguousarray(np.asarray(Wv, f)),
        "l0_wo": np.ascontiguousarray(np.asarray(Wo, f)),
        "l0_lq1": np.asarray(lq1, f)[None, :].copy(),
        "l0_lk1": np.asarray(lk1, f)[None, :].copy(),
        "l0_lq2": np.asarray(lq2, f)[None, :].copy(),
        "l0_lk2": np.asarray(lk2, f)[None, :].copy(),
        "l0_rmsw": np.ascontiguousarray(np.tile(np.asarray(rms_w, f), 2)[:, None]),
        "c_ones1": np.ones((1, 128), f),
    }



_PROG_CACHE = {}


def _prog(name, builder):
    if name not in _PROG_CACHE:
        _PROG_CACHE[name] = builder()
    return _PROG_CACHE[name]


def kernel(**inp):
    consts = {**host_constants(), **rope_consts(),
              "c_id128": np.eye(128, dtype=np.float32),
              "c_esel": _esel_const(),
              "c_ones1": np.ones((1, 128), np.float32)}
    moe_w = prep_moe_inputs(
        inp["L0_ffn_gate_w"], inp["L0_ffn_gate_b"], inp["L0_ffn_w1"],
        inp["L0_ffn_b1"], inp["L0_ffn_w3"], inp["L0_ffn_b3"], inp["L0_ffn_w2"],
        inp["L0_ffn_b2"], inp["L0_ffn_sw1"], inp["L0_ffn_sb1"],
        inp["L0_ffn_sw3"], inp["L0_ffn_sb3"], inp["L0_ffn_sw2"],
        inp["L0_ffn_sb2"])
    a0_w = prep_attn0_inputs(
        inp["L0_attn_Wq"], inp["L0_attn_Wk"], inp["L0_attn_Wv"],
        inp["L0_attn_Wo"], inp["L0_attn_lambda_q1"], inp["L0_attn_lambda_k1"],
        inp["L0_attn_lambda_q2"], inp["L0_attn_lambda_k2"], inp["L0_attn_rms_w"])
    f1_w = prep_ffn1_inputs(inp["L1_ffn_w1"], inp["L1_ffn_b1"],
                            inp["L1_ffn_w2"], inp["L1_ffn_b2"])
    a1_w = prep_attn1_inputs(
        inp["L1_attn_in_proj_w"], inp["L1_attn_in_proj_b"],
        inp["L1_attn_out_w"], inp["L1_attn_out_b"], inp["L1_attn_lin1_w"],
        inp["L1_attn_lin1_b"], inp["L1_attn_lin2_w"], inp["L1_attn_lin2_b"],
        inp["L1_attn_ln1_g"], inp["L1_attn_ln1_b"], inp["L1_attn_ln2_g"],
        inp["L1_attn_ln2_b"])
    allw = {**consts, **moe_w, **a0_w, **f1_w, **a1_w}

    x = np.asarray(inp["L0_attn_x"], np.float32)
    cur = [np.ascontiguousarray(x[c].T) for c in range(8)]
    kb = _prog("full", build_full_program)
    names = [n for n in kb.input_names if n != "xT"]
    in_maps = [dict(xT=cur[c], **{n: allw[n] for n in names}) for c in range(8)]
    res = run_bass_kernel_spmd(kb.nc, in_maps, core_ids=list(range(8)))
    return np.stack([np.asarray(res.results[c]["yT"], np.float32).T
                     for c in range(8)])


def full_in_map(inp_or_xT=None, **inp):
    consts = {**host_constants(), **rope_consts(),
              "c_id128": np.eye(128, dtype=np.float32),
              "c_esel": _esel_const(),
              "c_ones1": np.ones((1, 128), np.float32)}
    return consts


def build_full_program():
    kb = KB()
    xT = kb.dram_in("xT", [D, T])
    yT = kb.dram_out("yT", [D, T])
    m_w1 = kb.dram_in("m_w1", [9 * 1024, 1408])
    m_w3 = kb.dram_in("m_w3", [9 * 1024, 1408])
    m_w2 = kb.dram_in("m_w2", [9 * 1408, 1024])
    dins = {}
    for nm, sh, dt in (
            ("m_b1c", [128, 99], F32), ("m_b3c", [128, 99], F32),
            ("m_b2s", [9, 1024], F32), ("g_gwt", [128, 64], F32),
            ("g_gbt", [128, 8], F32), ("c_id128", [128, 128], F32),
            ("c_esel", [9, 9 * 128], BF16),
            ("l0_wq", [1024, 1024], F32), ("l0_wkd", [1024, 1024], F32),
            ("l0_wv", [1024, 512], F32), ("l0_wo", [1024, 1024], F32),
            ("l0_lq1", [1, 32], F32), ("l0_lk1", [1, 32], F32),
            ("l0_lq2", [1, 32], F32), ("l0_lk2", [1, 32], F32),
            ("l0_rmsw", [128, 1], F32),
            ("c_ropec", [128, T], BF16), ("c_ropes", [128, T], BF16),
            ("c_cmask", [128, 896], BF16), ("c_ones1", [1, 128], F32),
            ("f1_w1", [1024, 4096], F32), ("f1_w2", [4096, 1024], F32),
            ("f1_b1c", [128, 32], F32), ("f1_b2c", [128, 8], F32),
            ("a_wqkv", [1024, 3072], F32), ("a_wo", [1024, 1024], F32),
            ("a_w1", [1024, 4096], F32), ("a_w2", [4096, 1024], F32),
            ("a_bq", [128, 8], F32), ("a_bk", [128, 8], F32),
            ("a_bo", [128, 8], F32),
            ("a_b1", [128, 32], F32), ("a_b2", [128, 8], F32),
            ("a_g1", [128, 8], F32), ("a_b1n", [128, 8], F32),
            ("a_g2", [128, 8], F32), ("a_b2n", [128, 8], F32),
            ("c_lnsel", [128, 16 * NTT], BF16), ("c_bcsel", [16, 128 * NTT], F32)):
        dins[nm] = kb.dram_in(nm, sh, dt)
    dins["a_bvrow"] = kb.dram_in("a_bvrow", [1, 1024], F32)

    m = Model(kb)
    sb = {}
    bf_ins = {"c_esel"}
    cast_bf = {"m_b2s", "g_gwt", "c_ones1"}
    streamed = {"l0_wq", "l0_wkd", "l0_wv", "l0_wo", "f1_w1", "f1_w2",
                "a_wqkv", "a_wo", "a_w1", "a_w2", "c_ropec", "c_ropes",
                "c_cmask", "a_bvrow"}
    prevc = {"sync": None, "gpsimd": None}
    for nm in dins:
        if nm in ("c_lnsel", "c_bcsel") or nm in streamed:
            continue
        sh = list(dins[nm].shape)
        dt = BF16 if (nm in bf_ins or nm in cast_bf) else F32
        b = kb.sbuf("sb_" + nm, sh, dt)
        q = "gpsimd" if (nm in cast_bf) else "sync"
        rd = [prevc[q]] if prevc[q] is not None else []
        kb.dma(q, b.ap_all, dins[nm], reads=rd, writes=[b],
               sem_group="consts_" + q)
        prevc[q] = b
        sb[nm] = b
    xTr = xT.rearrange("(c p) t -> p c t", p=128)
    for c in range(DC):
        kb.dma("sync", m.Xc[c][:, 0:T], xTr[:, c, :], writes=[m.Xc[c]])
    kb.dma("sync", m.lnsel.ap_all, dins["c_lnsel"], writes=[m.lnsel])
    kb.dma("sync", m.bcsel.ap_all, dins["c_bcsel"], writes=[m.bcsel])

    bfw = {}
    deps = {}

    prev_cast = [None]

    def cast_w(name, src, chunks):
        shp = list(src.shape)
        dst = kb.dram_internal("bfw_" + name, shp)
        bufs = []
        for i, (r0, r1) in enumerate(chunks):
            b = Buf(None, f"cw_{name}_{i}")
            kb.bufs.append(b)
            rd = [prev_cast[0]] if prev_cast[0] is not None else []
            o = kb.op("gpsimd",
                      lambda e, src=src, dst=dst, r0=r0, r1=r1: e.dma_start(
                          out=dst[r0:r1, :], in_=src[r0:r1, :]),
                      reads=rd, writes=[b], is_dma=True)
            o.dma_buf = b
            o.sem_group = "castq"
            prev_cast[0] = b
            bufs.append(b)
        bfw[name] = dst
        return bufs

    def cast_chunk(name, src, r0, r1):
        if name not in bfw:
            bfw[name] = kb.dram_internal("bfw_" + name, list(src.shape))
        dst = bfw[name]
        b = Buf(None, f"cw_{name}_{r0}")
        kb.bufs.append(b)
        rd = [prev_cast[0]] if prev_cast[0] is not None else []
        o = kb.op("gpsimd",
                  lambda e, src=src, dst=dst, r0=r0, r1=r1: e.dma_start(
                      out=dst[r0:r1, :], in_=src[r0:r1, :]),
                  reads=rd, writes=[b], is_dma=True)
        o.dma_buf = b
        o.sem_group = "castq"
        prev_cast[0] = b
        return b

    for ex in range(9):
        for nm, src, rw in (("m_w1", m_w1, 1024), ("m_w3", m_w3, 1024),
                            ("m_w2", m_w2, 1408)):
            deps[(nm, ex)] = [cast_chunk(nm, src, rw * ex, rw * (ex + 1))]
    for nm in ("l0_wq", "l0_wkd", "l0_wv", "l0_wo", "f1_w1", "f1_w2",
               "a_wqkv", "a_wo", "a_w1", "a_w2"):
        deps[nm] = [cast_chunk(nm, dins[nm], 0, dins[nm].shape[0])]

    moe_cols = {"gwt": sb["g_gwt"], "gbt": sb["g_gbt"], "b2s": sb["m_b2s"],
                "id128": sb["c_id128"], "esel": sb["c_esel"]}
    moe_deps = {}
    for ex in range(9):
        for nm in ("m_w1", "m_w3", "m_w2"):
            moe_deps[(nm, ex)] = deps[(nm, ex)]
    moe_w = {"m_w1": bfw["m_w1"], "m_w3": bfw["m_w3"],
             "m_w2": bfw["m_w2"], "deps": moe_deps}
    a0_cols = {"lq1": sb["l0_lq1"], "lk1": sb["l0_lk1"], "lq2": sb["l0_lq2"],
               "lk2": sb["l0_lk2"], "rmsw": sb["l0_rmsw"],
               "ropec_d": dins["c_ropec"], "ropes_d": dins["c_ropes"],
               "cmask_d": dins["c_cmask"], "ones1": sb["c_ones1"]}
    a0_w = {"l0_wq": bfw["l0_wq"], "l0_wkd": bfw["l0_wkd"],
            "l0_wv": bfw["l0_wv"], "l0_wo": bfw["l0_wo"],
            "deps": {k: deps[k] for k in ("l0_wq", "l0_wkd", "l0_wv", "l0_wo")}}
    a1_cols = {"bq": sb["a_bq"], "bk": sb["a_bk"], "bvrow_d": dins["a_bvrow"],
               "bo": sb["a_bo"], "b1": sb["a_b1"], "b2": sb["a_b2"],
               "g1": sb["a_g1"], "b1n": sb["a_b1n"], "g2": sb["a_g2"],
               "b2n": sb["a_b2n"], "ones1": sb["c_ones1"]}
    a1_w = {"a_wqkv": bfw["a_wqkv"], "a_wo": bfw["a_wo"],
            "a_w1": bfw["a_w1"], "a_w2": bfw["a_w2"],
            "deps": {k: deps[k] for k in ("a_wqkv", "a_wo", "a_w1", "a_w2")}}

    m.layernorm()
    m.moe(moe_w, sb["m_b1c"], sb["m_b3c"], moe_cols)
    m.layernorm()
    m.attn0(a0_w, a0_cols)
    m.layernorm()
    m.moe(moe_w, sb["m_b1c"], sb["m_b3c"], moe_cols)
    m.layernorm()
    f1w = {"w1": bfw["f1_w1"], "w2": bfw["f1_w2"],
           "deps": {"w1": deps["f1_w1"], "w2": deps["f1_w2"]}}
    m.ffn1(f1w, sb["f1_b1c"], sb["f1_b2c"])
    m.layernorm()
    m.attn1(a1_w, a1_cols)
    m.layernorm()
    m.ffn1(f1w, sb["f1_b1c"], sb["f1_b2c"])
    yTr = yT.rearrange("(c p) t -> p c t", p=128)
    for c in range(DC):
        kb.dma("sync", yTr[:, c, :], m.Xc[c][:, 0:T], reads=[m.Xc[c]])
    kb.finalize()
    return kb
